# version 06
# baseline (speedup 1.0000x reference)
import numpy as np
import concourse.bass as bass
import concourse.mybir as mybir
from concourse.bass_utils import run_bass_kernel_spmd

F32 = mybir.dt.float32
BF16 = mybir.dt.bfloat16
AF = mybir.ActivationFunctionType
ALU = mybir.AluOpType
AXX = mybir.AxisListType.X

S = 2048
D = 1024
H = 16
HD = 64
DFF = 4096
NE = 8
NC = 8
TB = 512
NTB = S // TB
TT = 128
NTT = S // TT
NEG = -30000.0
EPS = 1e-5
VW65 = 65

ENGINES = ("pe", "dve", "act", "gp", "sync")
DQ = {"dq_s": "sync", "dq_g": "gp", "dq_a": "act"}


class P:

    def __init__(self, nc):
        self.nc = nc
        self.q = {e: [] for e in ENGINES}
        self.sem = {}
        self.cnt = {}
        for s in ENGINES + tuple(DQ):
            self.sem[s] = nc.alloc_semaphore(f"S_{s}")
            self.cnt[s] = 0
        self.waited = {e: {s: 0 for s in self.sem} for e in ENGINES}

    def _wait(self, eng, toks):
        for t in toks:
            if t is None:
                continue
            sname, v = t
            if v > self.waited[eng][sname]:
                self.waited[eng][sname] = v
                sem = self.sem[sname]
                self.q[eng].append(lambda E, sem=sem, v=v: E.wait_ge(sem, v))

    def do(self, eng, fn, wait=(), sig=False):
        self._wait(eng, wait)
        if sig:
            self.cnt[eng] += 1
            v = self.cnt[eng]
            sem = self.sem[eng]
            self.q[eng].append(lambda E, fn=fn, sem=sem: fn(E).then_inc(sem, 1))
            return (eng, v)
        self.q[eng].append(fn)
        return None

    def dma(self, qn, out, in_, wait=(), slow_ok=False):
        eng = DQ[qn]
        self._wait(eng, wait)
        self.cnt[qn] += 16
        v = self.cnt[qn]
        sem = self.sem[qn]
        self.q[eng].append(
            lambda E, out=out, in_=in_, sem=sem, so=slow_ok:
            E.dma_start(out=out, in_=in_, allow_slow_non_contiguous=so)
            .then_inc(sem, 16))
        return (qn, v)

    def drain(self, eng):
        self.q[eng].append(lambda E: E.drain())

    def finalize(self):
        for qn, eng in DQ.items():
            if self.cnt[qn]:
                self._wait(eng, [(qn, self.cnt[qn])])
        with self.nc.Block() as blk:
            @blk.tensor
            def _(E):
                for fn in self.q["pe"]:
                    fn(E)

            @blk.vector
            def _(E):
                for fn in self.q["dve"]:
                    fn(E)

            @blk.scalar
            def _(E):
                for fn in self.q["act"]:
                    fn(E)

            @blk.gpsimd
            def _(E):
                for fn in self.q["gp"]:
                    fn(E)

            @blk.sync
            def _(E):
                for fn in self.q["sync"]:
                    fn(E)


class Builder:
    def __init__(self, stop_after=None, debug_outputs=()):
        self.stop_after = stop_after
        self.debug_outputs = set(debug_outputs)
        nc = bass.Bass("TRN2", target_bir_lowering=False,
                       detect_race_conditions=False)
        self.nc = nc
        self.p = P(nc)
        self.pbank = [nc.alloc_psum_tensor(f"pb{i}", [128, 512], F32)
                      for i in range(7)]
        self.pbank_bf = nc.alloc_psum_tensor("pbt", [128, 1024], BF16)
        self.bank_tok = [None] * 8
        self._rr = 0
        self.rr_n = 7
        self.din = {}
        self.ds = {}
        self._declare_dram()
        self._alloc_sbuf()

    def _declare_dram(self):
        nc = self.nc
        self.din["x"] = nc.dram_tensor("x", [S, D], F32, kind="ExternalInput").ap()
        wspecs = {
            "Wq": [D, D], "bq": [1, D], "Wk": [D, D], "bk": [1, D],
            "Wv": [D, D], "bv": [1, D], "Wo": [D, D], "bo": [1, D],
            "ln1_g": [1, D], "ln1_b": [1, D],
            "w_qkv": [D, 3 * D], "b_qkv": [1, 3 * D],
            "w_o": [D, D], "b_o": [1, D],
            "ln2_g": [1, D], "ln2_b": [1, D],
            "gate_w": [D, NE], "gate_b": [1, NE],
            "W1": [NE, D, DFF], "b1": [NE, DFF],
            "W2": [NE, DFF, D], "b2": [NE, D],
        }
        for k, shp in wspecs.items():
            self.din[k] = nc.dram_tensor(k, shp, F32, kind="ExternalInput").ap()
        self.dout = nc.dram_tensor("out", [S, D], F32, kind="ExternalOutput").ap()
        scr = {
            "dbg_ex": ([H, NTT, 128, 64], F32),
            "dbg_w": ([2, S, TB], BF16),
            "dbg_gate": ([3, 128, NTT * NE], F32),
            "dbg_g2": ([NTT, 4, 128, NE], F32),
            "dbg_acc": ([S, D], F32),
            "x0T": ([D, S], F32), "x0Thi": ([D, S], BF16), "x0Tlo": ([D, S], BF16),
            "qhi": ([D, S], BF16), "qlo": ([D, S], BF16),
            "khi": ([D, S], BF16), "klo": ([D, S], BF16),
            "vhi": ([H, S, HD], BF16), "vlo": ([H, S, HD], BF16),
            "woThi": ([D, D], BF16), "woTlo": ([D, D], BF16),
            "oThi": ([D, S], BF16), "oTlo": ([D, S], BF16),
            "x1T": ([D, S], F32),
            "h1hi": ([D, S], BF16), "h1lo": ([D, S], BF16),
            "s_tok": ([S, D], F32),
            "h2hi": ([D, S], BF16), "h2lo": ([D, S], BF16),
            "W1hi": ([NE, D, DFF], BF16), "W1lo": ([NE, D, DFF], BF16),
            "W2hi": ([NE, DFF, D], BF16), "W2lo": ([NE, DFF, D], BF16),
        }
        for k, (shp, dt) in scr.items():
            kind = "ExternalOutput" if k in self.debug_outputs else "Internal"
            self.ds[k] = nc.dram_tensor("z_" + k, shp, dt, kind=kind).ap()

    def _alloc_sbuf(self):
        nc = self.nc

        def real(name, shape, dt):
            return nc.alloc_sbuf_tensor(name, shape, dt)

        self.arF = real("arF", [128, 20736], F32)
        self.arB = real("arB", [128, 40960], BF16)
        self.prjf = real("prjf", [128, 1024], F32)
        self.prjh = real("prjh", [128, 1024], BF16)
        self.prjl = real("prjl", [128, 1024], BF16)
        self.ones1 = real("ones1", [1, 128], F32)
        self.ones_sq = real("ones_sq", [128, 128], F32)
        self.identf = real("identf", [128, 128], F32)
        self.identb = real("identb", [128, 128], BF16)
        self.ex = real("ex", [128, 64], F32)
        self.mt = real("mt", [128, 8], F32)
        self.mrow = real("mrow", [1, 512], F32)
        self.trow = real("trow", [1, 512], F32)
        self.srow = [real(f"srow{i}", [1, 512], F32) for i in range(4)]
        self.bias = {}
        for nm in ("bq", "bk", "bv_fm", "bo", "b_o", "ln1_g", "ln1_b",
                   "ln2_g", "ln2_b"):
            self.bias[nm] = real(f"bt_{nm}", [128, NC], F32)
        self.bias["b_qkv"] = real("bt_b_qkv", [128, 3 * NC], F32)
        self.bias["b1"] = real("bt_b1", [128, NE * (DFF // 128)], F32)
        self.bq8 = real("bq8", [128, NC], F32)
        self.bqkv8 = real("bqkv8", [128, NC], F32)
        self.gate_b_row = real("gate_b_row", [1, NE], F32)
        self.glog = real("glog", [128, NTT * NE], F32)
        self.ew = real("ew", [128, NTT * NE], F32)
        self.gsc = real("gsc", [128, 24], F32)

    def fF(self, off, w, p=128):
        return self.arF[0:p, off:off + w]

    def fB(self, off, w, p=128):
        return self.arB[0:p, off:off + w]

    def bank(self, wait_extra=()):
        i = self._rr
        self._rr = (self._rr + 1) % self.rr_n
        if self.bank_tok[i] is not None:
            self.p._wait("pe", [self.bank_tok[i]])
        self.p._wait("pe", wait_extra)
        return i, self.pbank[i]

    def bank_fixed(self, i):
        if self.bank_tok[i] is not None:
            self.p._wait("pe", [self.bank_tok[i]])
        return i, self.pbank[i]

    def bank_done(self, i, tok):
        self.bank_tok[i] = tok

    def mm3(self, psum_ap, pairs, extra_f32=(), wait=()):
        p = self.p
        ops = []
        for (ah, al, bh, bl) in pairs:
            ops += [(ah, bh), (ah, bl), (al, bh)]
        ops += list(extra_f32)
        tok = None
        for k, (a, b) in enumerate(ops):
            st = (k == 0)
            sp = (k == len(ops) - 1)
            tok = p.do("pe",
                       lambda E, o=psum_ap, a=a, b=b, st=st, sp=sp:
                       E.matmul(o, a, b, start=st, stop=sp),
                       wait=(wait if k == 0 else ()), sig=sp)
        return tok

    def evac(self, psum_ap, out_ap, func=AF.Identity, bias=0.0, scale=1.0, wait=()):
        return self.p.do("act",
                         lambda E, o=out_ap, i=psum_ap, f=func, b=bias, s=scale:
                         E.activation(out=o, in_=i, func=f, bias=b, scale=s),
                         wait=wait, sig=True)

    def split(self, src, hi, lo, wait=()):
        p = self.p
        p.do("dve", lambda E, o=hi, i=src: E.tensor_copy(out=o, in_=i), wait=wait)
        return p.do("dve", lambda E, o=lo, a=src, b=hi: E.tensor_sub(o, a, b),
                    sig=True)

    def transpose(self, psum_ap, in_ap, bf16, wait=()):
        ident = self.identb if bf16 else self.identf
        return self.p.do("pe",
                         lambda E, o=psum_ap, i=in_ap, id_=ident:
                         E.transpose(o, i, id_[:]),
                         wait=wait, sig=True)

    def setup(self):
        p = self.p
        t0 = p.do("dve", lambda E: E.memset(self.ones1[:], 1.0), sig=True)
        t1 = p.do("dve", lambda E: E.memset(self.ones_sq[:], 1.0), sig=True)
        t2 = p.do("dve", lambda E: E.memset(self.identf[:], 1.0), sig=True)
        t3 = p.do("gp", lambda E: E.affine_select(
            out=self.identf[:], in_=self.identf[:], pattern=[[1, 128]],
            compare_op=ALU.is_equal, fill=0.0, base=0, channel_multiplier=-1),
            wait=[t2], sig=True)
        t4 = p.do("dve", lambda E: E.tensor_copy(out=self.identb[:], in_=self.identf[:]),
                  wait=[t3], sig=True)
        self.tok_setup = [t0, t1, t3, t4]
        self.tok_const = list(self.tok_setup)
        for nm, srcn in (("bq", "bq"), ("bk", "bk"), ("bv_fm", "bv"), ("bo", "bo"),
                         ("b_o", "b_o"), ("ln1_g", "ln1_g"), ("ln1_b", "ln1_b"),
                         ("ln2_g", "ln2_g"), ("ln2_b", "ln2_b")):
            self.tok_const.append(p.dma(
                "dq_s", self.bias[nm][:],
                self.din[srcn].rearrange("o (c p) -> p (o c)", p=128), slow_ok=True))
        self.tok_const.append(p.dma(
            "dq_s", self.bias["b_qkv"][:],
            self.din["b_qkv"].rearrange("o (c p) -> p (o c)", p=128), slow_ok=True))
        self.tok_const.append(p.dma(
            "dq_s", self.bias["b1"][:],
            self.din["b1"].rearrange("e (c p) -> p (e c)", p=128), slow_ok=True))
        self.tok_const.append(p.dma("dq_s", self.gate_b_row[:], self.din["gate_b"]))
        tbq8 = p.do("dve",
                    lambda E: E.tensor_scalar_mul(self.bq8[:], self.bias["bq"][:], 0.125),
                    wait=list(self.tok_const), sig=True)
        tbq82 = p.do("dve",
                     lambda E: E.tensor_scalar_mul(
                         self.bqkv8[:], self.bias["b_qkv"][:, 0:NC], 0.125),
                     sig=True)
        self.tok_const += [tbq8, tbq82]

    def p0_input(self):
        p = self.p
        ds = self.ds
        xstage = self.fF(0, D)
        xT_col = self.fF(1024, NC * TT)
        xT_hi = self.fB(0, NC * TT)
        xT_lo = self.fB(1024, NC * TT)
        tok_stage = None
        tok_col = None
        sp_toks = []
        for t in range(NTT):
            tok_ld = p.dma("dq_s", xstage, self.din["x"][t * TT:(t + 1) * TT, :],
                           wait=[tok_stage])
            evacs = []
            last_tt = None
            for c in range(NC):
                bi, pb = self.bank()
                last_tt = self.transpose(pb[0:128, 0:128],
                                         xstage[:, c * 128:(c + 1) * 128], False,
                                         wait=[tok_ld] + self.tok_setup)
                et = self.evac(pb[0:128, 0:128], xT_col[:, c * TT:(c + 1) * TT],
                               wait=[last_tt] + ([tok_col] if c == 0 else []))
                self.bank_done(bi, et)
                evacs.append(et)
            tok_stage = last_tt
            sp_toks.append(p.dma(
                "dq_s",
                ds["x0T"].rearrange("(c p) s -> p c s", p=128)[:, :, t * TT:(t + 1) * TT],
                xT_col.rearrange("p (c q) -> p c q", c=NC), wait=evacs))
            stok = self.split(xT_col, xT_hi, xT_lo, wait=evacs)
            sp_toks.append(p.dma(
                "dq_s",
                ds["x0Thi"].rearrange("(c p) s -> p c s", p=128)[:, :, t * TT:(t + 1) * TT],
                xT_hi.rearrange("p (c q) -> p c q", c=NC), wait=[stok]))
            sp_toks.append(p.dma(
                "dq_s",
                ds["x0Tlo"].rearrange("(c p) s -> p c s", p=128)[:, :, t * TT:(t + 1) * TT],
                xT_lo.rearrange("p (c q) -> p c q", c=NC), wait=[stok]))
            tok_col = sp_toks[-1]
        self.tok_p0 = sp_toks[-3:]

    def prep_wT(self, wname):
        p = self.p
        wT_hi = self.fB(16384, 8192)
        wT_lo = self.fB(24576, 8192)
        wpan_h = [self.fB(32768, 1024), self.fB(33792, 1024)]
        wpan_l = [self.fB(34816, 1024), self.fB(35840, 1024)]
        wpan_f = [self.fF(0, 1024), self.fF(1024, 1024)]
        if not hasattr(self, "wpan_tok"):
            self.wpan_tok = [None, None]
        toks = []
        for ic in range(NC):
            pg = ic % 2
            src = self.din[wname][ic * 128:(ic + 1) * 128, :]
            tok_f = p.dma("dq_s", wpan_f[pg], src, wait=[self.wpan_tok[pg]])
            tok_h = p.dma("dq_g", wpan_h[pg], src, wait=[self.wpan_tok[pg]])
            stok = p.do("dve",
                        lambda E, o=wpan_l[pg], a=wpan_f[pg], b=wpan_h[pg]:
                        E.tensor_sub(o, a, b),
                        wait=[tok_f, tok_h], sig=True)
            last = None
            for dc in range(NC):
                for (srcb, dstb, tw) in ((wpan_h[pg], wT_hi, tok_h),
                                         (wpan_l[pg], wT_lo, stok)):
                    ttok = self.transpose(self.pbank_bf[0:128, 0:128],
                                          srcb[:, dc * 128:(dc + 1) * 128], True,
                                          wait=[tw, self.bank_tok[7]] + self.tok_setup)
                    etok = p.do("dve",
                                lambda E,
                                o=dstb[:, dc * 1024 + ic * 128: dc * 1024 + (ic + 1) * 128],
                                i=self.pbank_bf[0:128, 0:128]:
                                E.tensor_copy(out=o, in_=i),
                                wait=[ttok], sig=True)
                    self.bank_tok[7] = etok
                    last = etok
            self.wpan_tok[pg] = last
            toks.append(last)
        return wT_hi, wT_lo, toks[-2:]

    def p1a(self):
        p = self.p
        ds = self.ds
        self.xs_hi = [self.fB(0, NC * TB), self.fB(8192, NC * TB)]
        self.xs_lo = [self.fB(4096, NC * TB), self.fB(12288, NC * TB)]
        self.xs_tok = [None, None]
        self.prj_tok = [None, None]

        wT_hi, wT_lo, tw = self.prep_wT("Wq")
        self._proj_fm(ds["x0Thi"], ds["x0Tlo"], ds["qhi"], ds["qlo"],
                      wT_hi, wT_lo, tw, self.bq8, 0.125, self.tok_p0, True)
        wT_hi, wT_lo, tw = self.prep_wT("Wk")
        self._proj_fm(ds["x0Thi"], ds["x0Tlo"], ds["khi"], ds["klo"],
                      wT_hi, wT_lo, tw, self.bias["bk"], 1.0, [], True)
        wT_hi, wT_lo, tw = self.prep_wT("Wv")
        bvr = self.fF(2048, D)[0:1, :]
        tbv = p.dma("dq_s", bvr, self.din["bv"])
        self._vtok(ds["x0Thi"], ds["x0Tlo"], wT_hi, wT_lo, tw, bvr, [tbv],
                   trans_w=True)
        wT_hi, wT_lo, tw = self.prep_wT("Wo")
        t1 = p.dma("dq_s", ds["woThi"].rearrange("(c p) i -> p c i", p=128),
                   wT_hi.rearrange("p (c i) -> p c i", c=NC), wait=tw)
        t2 = p.dma("dq_s", ds["woTlo"].rearrange("(c p) i -> p c i", p=128),
                   wT_lo.rearrange("p (c i) -> p c i", c=NC), wait=tw)
        self.tok_woT = [t1, t2]

    def x_block(self, dhi, dlo, tb, sg, extra_wait=()):
        p = self.p
        t1 = p.dma("dq_s", self.xs_hi[sg].rearrange("p (c w) -> p c w", c=NC),
                   dhi.rearrange("(c p) s -> p c s", p=128)
                   [:, :, tb * TB:(tb + 1) * TB],
                   wait=[self.xs_tok[sg]] + list(extra_wait))
        t2 = p.dma("dq_s", self.xs_lo[sg].rearrange("p (c w) -> p c w", c=NC),
                   dlo.rearrange("(c p) s -> p c s", p=128)
                   [:, :, tb * TB:(tb + 1) * TB])
        return [t1, t2]

    def _proj_fm(self, src_hi, src_lo, dsthi, dstlo, w_hi, w_lo, wtoks,
                 bias_tile, scale, first_wait, trans_w):
        p = self.p
        for tb in range(NTB):
            sg = tb % 2
            xt = self.x_block(src_hi, src_lo, tb, sg,
                              extra_wait=(first_wait if tb == 0 else []))
            for oc in range(NC):
                bi, pb = self.bank()
                pairs = []
                for dc in range(NC):
                    if trans_w:
                        wh = w_hi[:, dc * 1024 + oc * 128: dc * 1024 + (oc + 1) * 128]
                        wl = w_lo[:, dc * 1024 + oc * 128: dc * 1024 + (oc + 1) * 128]
                    else:
                        wh = w_hi[:, dc, oc * 128:(oc + 1) * 128]
                        wl = w_lo[:, dc, oc * 128:(oc + 1) * 128]
                    pairs.append((
                        wh, wl,
                        self.xs_hi[sg][:, dc * TB:(dc + 1) * TB],
                        self.xs_lo[sg][:, dc * TB:(dc + 1) * TB]))
                mtok = self.mm3(pb[:], pairs, wait=list(wtoks) + xt)
                self.xs_tok[sg] = mtok
                psg = (tb * NC + oc) % 2
                fsl = self.prjf[:, psg * TB:(psg + 1) * TB]
                et = self.evac(pb[:], fsl, bias=bias_tile[:, oc:oc + 1],
                               scale=scale,
                               wait=[mtok, self.prj_tok[psg]] + self.tok_const)
                self.bank_done(bi, et)
                stok = self.split(fsl, self.prjh[:, psg * TB:(psg + 1) * TB],
                                  self.prjl[:, psg * TB:(psg + 1) * TB],
                                  wait=[et])
                p.dma("dq_s",
                      dsthi[oc * 128:(oc + 1) * 128, tb * TB:(tb + 1) * TB],
                      self.prjh[:, psg * TB:(psg + 1) * TB], wait=[stok])
                t2 = p.dma("dq_s",
                           dstlo[oc * 128:(oc + 1) * 128, tb * TB:(tb + 1) * TB],
                           self.prjl[:, psg * TB:(psg + 1) * TB], wait=[stok])
                self.prj_tok[psg] = t2

    def _vtok(self, src_hi, src_lo, w_hi, w_lo, wtoks, brow, btoks, trans_w):
        p = self.p
        ds = self.ds
        for tb in range(NTB):
            sg = tb % 2
            xt = self.x_block(src_hi, src_lo, tb, sg)
            for tt4 in range(4):
                tt = tb * 4 + tt4
                for ib in range(2):
                    bi, pb = self.bank()
                    pairs = []
                    for dc in range(NC):
                        if trans_w:
                            wh = w_hi[:, dc * 1024 + ib * 512: dc * 1024 + (ib + 1) * 512]
                            wl = w_lo[:, dc * 1024 + ib * 512: dc * 1024 + (ib + 1) * 512]
                        else:
                            wh = w_hi[ib][:, dc, :]
                            wl = w_lo[ib][:, dc, :]
                        pairs.append((
                            self.xs_hi[sg][:, dc * TB + tt4 * TT: dc * TB + (tt4 + 1) * TT],
                            self.xs_lo[sg][:, dc * TB + tt4 * TT: dc * TB + (tt4 + 1) * TT],
                            wh, wl))
                    mtok = self.mm3(
                        pb[:], pairs,
                        extra_f32=[(self.ones1[0:1, 0:128],
                                    brow[0:1, ib * 512:(ib + 1) * 512])],
                        wait=list(wtoks) + xt + list(btoks) + self.tok_setup)
                    self.xs_tok[sg] = mtok
                    psg = (tt * 2 + ib) % 2
                    fsl = self.prjf[:, psg * TB:(psg + 1) * TB]
                    et = self.evac(pb[:], fsl, wait=[mtok, self.prj_tok[psg]])
                    self.bank_done(bi, et)
                    stok = self.split(fsl, self.prjh[:, psg * TB:(psg + 1) * TB],
                                      self.prjl[:, psg * TB:(psg + 1) * TB],
                                      wait=[et])
                    sp = None
                    for j in range(8):
                        h = ib * 8 + j
                        sp = p.dma("dq_s",
                                   ds["vhi"][h, tt * TT:(tt + 1) * TT, :],
                                   self.prjh[:, psg * TB + j * 64: psg * TB + (j + 1) * 64],
                                   wait=[stok])
                        sp = p.dma("dq_s",
                                   ds["vlo"][h, tt * TT:(tt + 1) * TT, :],
                                   self.prjl[:, psg * TB + j * 64: psg * TB + (j + 1) * 64])
                    self.prj_tok[psg] = sp

    def attn_heads(self, sparse):
        p = self.p
        ds = self.ds
        dqh, dql = ds["qhi"], ds["qlo"]
        dkh, dkl = ds["khi"], ds["klo"]
        doh, dol = ds["oThi"], ds["oTlo"]
        sc = [self.fF(0, S), self.fF(2048, S), self.fF(4096, S)]
        wq_s = self.fF(6144, TB)
        wq_m = self.fF(6656, TB)
        wq_e = self.fF(7168, TB)
        wq_w = self.fF(7680, TB)
        bc_m = self.fF(8192, TB)
        bc_t = self.fF(8704, TB)
        av_f = self.fF(9216, TB, p=64)
        rb = self.fF(9728, TB, p=64)
        den_t = self.fF(10240, TB)
        rec_t = self.fF(10752, TB)
        vh_b = [self.fB(0, NTT * VW65), self.fB(2080, NTT * VW65)]
        vl_b = [self.fB(1040, NTT * VW65), self.fB(3120, NTT * VW65)]
        qt_h = self.fB(4224, 2 * S, p=64)
        kt_h = self.fB(8320, 2 * S, p=64)
        wT_h2 = self.fB(12416, NTT * TB)
        wT_l2 = self.fB(20608, NTT * TB)
        oT_h = self.fB(28800, TB, p=64)
        oT_l = self.fB(29312, TB, p=64)

        st0 = getattr(self, "_ah_state", None)
        if st0 is None:
            self._ah_state = {
                "tok_qk": None, "tok_sc": [None, None, None], "tok_mt": None,
                "tok_tside": None, "tok_w": None, "tok_av": [None, None],
                "tok_oT": [None, None], "tok_bc": None, "tok_v": [None, None]}
        stt = self._ah_state
        entry = [self.xs_tok[0], self.xs_tok[1], stt["tok_qk"],
                 stt["tok_v"][0], stt["tok_v"][1]]
        tons = []
        for g in range(2):
            t1 = p.do("dve", lambda E, o=vh_b[g].rearrange(
                "p (t j) -> p t j", t=NTT)[:, :, 64:65]: E.memset(o, 1.0),
                wait=entry, sig=True)
            t2 = p.do("dve", lambda E, o=vl_b[g].rearrange(
                "p (t j) -> p t j", t=NTT)[:, :, 64:65]: E.memset(o, 0.0),
                sig=True)
            tons += [t1, t2]
        stt["tok_ones"] = tons
        stt["tok_v"] = [tons[1], tons[3]]
        for h in range(H):
            hg = h % 2
            tq1 = p.dma("dq_s", qt_h[:, 0:S], dqh[h * 64:(h + 1) * 64, :],
                        wait=[stt["tok_qk"]])
            tq2 = p.dma("dq_s", qt_h[:, S:2 * S], dql[h * 64:(h + 1) * 64, :])
            tk1 = p.dma("dq_s", kt_h[:, 0:S], dkh[h * 64:(h + 1) * 64, :])
            tk2 = p.dma("dq_s", kt_h[:, S:2 * S], dkl[h * 64:(h + 1) * 64, :])
            tqk = [tq1, tq2, tk1, tk2]
            tv1 = p.dma("dq_s",
                        vh_b[hg].rearrange("p (t j) -> p t j", t=NTT)[:, :, 0:64],
                        ds["vhi"][h].rearrange("(t p) j -> p t j", p=128),
                        wait=[stt["tok_v"][hg]])
            tv2 = p.dma("dq_s",
                        vl_b[hg].rearrange("p (t j) -> p t j", t=NTT)[:, :, 0:64],
                        ds["vlo"][h].rearrange("(t p) j -> p t j", p=128))
            tvv = [tv1, tv2] + stt["tok_ones"]
            last_qk_use = None
            for qb in range(NTB):
                for qt4 in range(4):
                    qt = qb * 4 + qt4
                    sbi = qt % 2
                    sbuf = sc[sbi]
                    ev_toks = []
                    for kb in range(NTB):
                        bi, pb = self.bank()
                        pairs = [(
                            qt_h[:, qt * 128:(qt + 1) * 128],
                            qt_h[:, S + qt * 128: S + (qt + 1) * 128],
                            kt_h[:, kb * TB:(kb + 1) * TB],
                            kt_h[:, S + kb * TB: S + (kb + 1) * TB])]
                        mtok = self.mm3(pb[:], pairs, wait=tqk)
                        last_qk_use = mtok
                        et = self.evac(pb[:], sbuf[:, kb * TB:(kb + 1) * TB],
                                       wait=[mtok] +
                                       ([stt["tok_sc"][sbi]] if kb == 0 else []))
                        self.bank_done(bi, et)
                        ev_toks.append(et)
                    if sparse:
                        cur = sbuf
                        nxt = sc[2]
                        p._wait("dve", ev_toks + [stt["tok_sc"][2], stt["tok_mt"]])
                        for r in range(8):
                            tok_max = p.do("dve",
                                 lambda E, o=self.ex[:, r * 8:(r + 1) * 8], i=cur:
                                 E.max(out=o, in_=i[:]), sig=True)
                            if r < 7:
                                p.do("dve",
                                     lambda E, o=nxt, v=self.ex[:, r * 8:(r + 1) * 8],
                                     i=cur: E.match_replace(
                                         out=o[:], in_to_replace=v, in_values=i[:],
                                         imm_value=NEG), wait=[tok_max])
                                cur, nxt = nxt, cur
                        p.do("dve", lambda E, o=self.mt[:, 2 * qt4:2 * qt4 + 1],
                             i=self.ex[:, 0:1]: E.tensor_scalar_mul(o, i, -1.0),
                             wait=[tok_max])
                        tmt = p.do("dve", lambda E, o=self.mt[:, 2 * qt4 + 1:2 * qt4 + 2],
                                   i=self.ex[:, 63:64]: E.tensor_copy(out=o, in_=i),
                                   sig=True)
                        stt["tok_sc"][sbi] = tmt
                        stt["tok_sc"][2] = tmt
                        stt["tok_mt"] = tmt
                        if self.stop_after == "p1b":
                            p.dma("dq_s", ds["dbg_ex"][h, qt], self.ex[:],
                                  wait=[tmt])
                    else:
                        tmt = p.do("dve",
                                   lambda E, o=self.mt[:, 2 * qt4:2 * qt4 + 1], i=sbuf:
                                   E.tensor_reduce(o, i[:], AXX, ALU.max, negate=True),
                                   wait=ev_toks + [stt["tok_mt"]], sig=True)
                        stt["tok_sc"][sbi] = tmt
                        stt["tok_mt"] = tmt
                rows_toks = []
                for j in range(4):
                    for which in range(2 if sparse else 1):
                        bi, pb = self.bank()
                        ttr = self.transpose(pb[0:1, 0:128],
                                             self.mt[:, 2 * j + which:2 * j + which + 1],
                                             False, wait=[stt["tok_mt"], stt["tok_bc"]])
                        dst = (self.mrow if which == 0 else self.trow)
                        rt = p.do("dve",
                                  lambda E, o=dst[0:1, j * 128:(j + 1) * 128],
                                  i=pb[0:1, 0:128]: E.tensor_copy(out=o, in_=i),
                                  wait=[ttr], sig=True)
                        self.bank_done(bi, rt)
                        rows_toks.append(rt)
                bi, pb = self.bank()
                tb1 = self.mm3(pb[:], [], extra_f32=[(self.ones1[:], self.mrow[:])],
                               wait=rows_toks + self.tok_setup)
                eb1 = self.evac(pb[:], bc_m, wait=[tb1, stt["tok_tside"]])
                self.bank_done(bi, eb1)
                eb2 = None
                if sparse:
                    bi2, pb2 = self.bank()
                    tb2 = self.mm3(pb2[:], [], extra_f32=[(self.ones1[:], self.trow[:])],
                                   wait=rows_toks)
                    eb2 = self.evac(pb2[:], bc_t, wait=[tb2, stt["tok_tside"]])
                    self.bank_done(bi2, eb2)
                stt["tok_bc"] = eb2 if sparse else eb1
                t_e_prev = None
                t_w_prev = None
                split_prev = None
                for kt in range(NTT):
                    bi, pb = self.bank()
                    pairs = [(
                        kt_h[:, kt * 128:(kt + 1) * 128],
                        kt_h[:, S + kt * 128: S + (kt + 1) * 128],
                        qt_h[:, qb * TB:(qb + 1) * TB],
                        qt_h[:, S + qb * TB: S + (qb + 1) * TB])]
                    mtok = self.mm3(pb[:], pairs, wait=tqk)
                    last_qk_use = mtok
                    t_s = p.do("dve", lambda E, o=wq_s, a=pb[:], b=bc_m:
                               E.tensor_add(o, a, b),
                               wait=[mtok, eb1] + ([eb2] if eb2 else []) +
                               [t_e_prev, stt["tok_w"]], sig=True)
                    if sparse:
                        t_m = p.do("dve", lambda E, o=wq_m, a=pb[:], b=bc_t:
                                   E.tensor_tensor(o, a, b, ALU.is_ge),
                                   sig=True)
                        bank_rel = t_m
                    else:
                        bank_rel = t_s
                    self.bank_done(bi, bank_rel)
                    last_tside = bank_rel
                    t_e = p.do("act", lambda E, o=wq_e, i=wq_s:
                               E.activation(out=o, in_=i, func=AF.Exp),
                               wait=[t_s, t_w_prev if sparse else split_prev],
                               sig=True)
                    t_e_prev = t_e
                    if sparse:
                        t_w = p.do("dve", lambda E, o=wq_w, a=wq_e, b=wq_m:
                                   E.tensor_mul(o, a, b), wait=[t_e], sig=True)
                        t_w_prev = t_w
                        wsrc = wq_w
                        wtok = t_w
                    else:
                        wsrc = wq_e
                        wtok = t_e
                    split_prev = self.split(
                        wsrc, wT_h2[:, kt * TB:(kt + 1) * TB],
                        wT_l2[:, kt * TB:(kt + 1) * TB],
                        wait=[wtok, stt["tok_w"]])
                if self.stop_after == "p1b" and sparse and h == 4 and qb == 1:
                    p.dma("dq_s", ds["dbg_w"][0].rearrange("(t p) w -> p t w", p=128),
                          wT_h2.rearrange("p (t w) -> p t w", t=NTT),
                          wait=[split_prev])
                    p.dma("dq_s", ds["dbg_w"][1].rearrange("(t p) w -> p t w", p=128),
                          wT_l2.rearrange("p (t w) -> p t w", t=NTT))
                bi, pb = self.bank()
                pairs = []
                for kt in range(NTT):
                    pairs.append((
                        vh_b[hg][:, kt * VW65:(kt + 1) * VW65],
                        vl_b[hg][:, kt * VW65:(kt + 1) * VW65],
                        wT_h2[:, kt * TB:(kt + 1) * TB],
                        wT_l2[:, kt * TB:(kt + 1) * TB]))
                avtok = self.mm3(pb[0:65, :], pairs, wait=[split_prev] + tvv)
                stt["tok_w"] = avtok
                stt["tok_v"][hg] = avtok
                e_den = self.evac(pb[64:65, :], den_t[64:65, :],
                                  wait=[avtok, stt["tok_av"][0]])
                tr = p.do("dve", lambda E, o=rec_t[64:65, :], i=den_t[64:65, :]:
                          E.reciprocal(out=o, in_=i), wait=[e_den], sig=True)
                bi2, pb2 = self.bank()
                tb3 = self.mm3(pb2[0:64, :], [],
                               extra_f32=[(self.ones_sq[64:65, 0:64],
                                           rec_t[64:65, :])],
                               wait=[tr])
                erb = self.evac(pb2[0:64, :], rb,
                                wait=[tb3, stt["tok_av"][0], stt["tok_av"][1]])
                self.bank_done(bi2, erb)
                t_av = p.do("dve", lambda E, o=av_f, a=pb[0:64, :], b=rb:
                            E.tensor_mul(o, a, b),
                            wait=[erb, avtok], sig=True)
                self.bank_done(bi, t_av)
                sto = self.split(av_f, oT_h, oT_l, wait=[t_av])
                o1 = p.dma("dq_s", doh[h * 64:(h + 1) * 64, qb * TB:(qb + 1) * TB],
                           oT_h, wait=[sto])
                o2 = p.dma("dq_s", dol[h * 64:(h + 1) * 64, qb * TB:(qb + 1) * TB],
                           oT_l, wait=[sto])
                stt["tok_av"] = [t_av, o2]
                stt["tok_tside"] = last_tside
                stt["tok_oT"] = [o1, o2]
            stt["tok_qk"] = last_qk_use
        return stt["tok_oT"]

    def proj_residual_ln(self, which):
        p = self.p
        ds = self.ds
        xpre = self.fF(0, NC * S)
        res_f = self.fF(16384, TB)
        sq = self.fF(16896, TB)
        bc_nm = self.fF(17408, TB)
        bc_rs = self.fF(17920, TB)
        y = self.fF(18432, TB)
        hbuf = self.fF(18944, TB)
        x1rel = self.fF(19456, TB)
        sbuf2 = self.fF(19968, TB)
        sev = self.fF(20480, 128)
        oTs_hi = [self.fB(0, NC * TB), self.fB(8192, NC * TB)]
        oTs_lo = [self.fB(4096, NC * TB), self.fB(12288, NC * TB)]
        hst_h = self.fB(32768, TB)
        hst_l = self.fB(33280, TB)
        oTs_tok = [None, None]
        res_tok = getattr(self, "tok_presplit", None)
        if which == "attn1":
            wT_hi = self.fB(16384, 8192)
            wT_lo = self.fB(24576, 8192)
            t1 = p.dma("dq_s", wT_hi.rearrange("p (c i) -> p c i", c=NC),
                       ds["woThi"].rearrange("(c p) i -> p c i", p=128),
                       wait=self.tok_woT + self._ah_state["tok_oT"] +
                       [self._ah_state["tok_w"]])
            t2 = p.dma("dq_s", wT_lo.rearrange("p (c i) -> p c i", c=NC),
                       ds["woTlo"].rearrange("(c p) i -> p c i", p=128))
            wtok = [t1, t2]
            bias_tile = self.bias["bo"]
            res_dram = ds["x0T"]
        else:
            whi, wlo = self._wo_views
            wtok = self.wo_tok
            bias_tile = self.bias["b_o"]
            res_dram = ds["x1T"]

        sum_x, sum_x2, nmean, rstd = self.srow
        tpre = getattr(self, "tok_presplit", None)
        stat_tok = [tpre]
        prev_bc = [None]
        self.rr_n = 5
        ln_toks = []
        for tb in range(NTB):
            sg = tb % 2
            to1 = p.dma("dq_s", oTs_hi[sg].rearrange("p (c w) -> p c w", c=NC),
                        ds["oThi"].rearrange("(c p) s -> p c s", p=128)
                        [:, :, tb * TB:(tb + 1) * TB],
                        wait=[oTs_tok[sg]] + (self._ah_state["tok_oT"] if tb == 0 else []))
            to2 = p.dma("dq_s", oTs_lo[sg].rearrange("p (c w) -> p c w", c=NC),
                        ds["oTlo"].rearrange("(c p) s -> p c s", p=128)
                        [:, :, tb * TB:(tb + 1) * TB])
            sq_toks = []
            bi1, pbs1 = self.bank_fixed(5)
            bi2, pbs2 = self.bank_fixed(6)
            m1 = m2 = None
            for oc in range(NC):
                bi, pb = self.bank()
                pairs = []
                for dc in range(NC):
                    if which == "attn1":
                        wh = wT_hi[:, dc * 1024 + oc * 128: dc * 1024 + (oc + 1) * 128]
                        wl = wT_lo[:, dc * 1024 + oc * 128: dc * 1024 + (oc + 1) * 128]
                    else:
                        wh = whi[:, dc, oc * 128:(oc + 1) * 128]
                        wl = wlo[:, dc, oc * 128:(oc + 1) * 128]
                    pairs.append((
                        wh, wl,
                        oTs_hi[sg][:, dc * TB:(dc + 1) * TB],
                        oTs_lo[sg][:, dc * TB:(dc + 1) * TB]))
                mtok = self.mm3(pb[:], pairs, wait=list(wtok) + [to1, to2])
                oTs_tok[sg] = mtok
                rt = p.dma("dq_s", res_f,
                           res_dram[oc * 128:(oc + 1) * 128, tb * TB:(tb + 1) * TB],
                           wait=[res_tok])
                psg = (tb * NC + oc) % 2
                fsl = self.prjf[:, psg * TB:(psg + 1) * TB]
                et = self.evac(pb[:], fsl, bias=bias_tile[:, oc:oc + 1],
                               wait=[mtok, getattr(self, "_prl_tok", None)] + self.tok_const)
                self.bank_done(bi, et)
                xsl = xpre[:, oc * S + tb * TB: oc * S + tb * TB + TB]
                ta = p.do("dve", lambda E, o=xsl, a=fsl, b=res_f:
                          E.tensor_add(o, a, b), wait=[et, rt], sig=True)
                res_tok = ta
                self._prl_tok = ta
                tsq = p.do("act", lambda E, o=sq, i=xsl:
                           E.activation(out=o, in_=i, func=AF.Square),
                           wait=[ta, stat_tok[0]], sig=True)
                st_ = (oc == 0)
                sp_ = (oc == NC - 1)
                m1 = p.do("pe", lambda E, o=pbs1[0:1, :], a=self.ones_sq[0:128, 0:1],
                          b=xsl, st=st_, sp=sp_:
                          E.matmul(o, a, b, start=st, stop=sp),
                          wait=[ta], sig=sp_)
                m2 = p.do("pe", lambda E, o=pbs2[0:1, :], a=self.ones_sq[0:128, 0:1],
                          b=sq, st=st_, sp=sp_:
                          E.matmul(o, a, b, start=st, stop=sp),
                          wait=[tsq], sig=sp_)
                sq_toks.append(tsq)
            e1 = self.evac(pbs1[0:1, :], sum_x[:], wait=[m1])
            e2 = self.evac(pbs2[0:1, :], sum_x2[:], wait=[m2])
            self.bank_done(bi1, e1)
            self.bank_done(bi2, e2)
            t1_ = p.do("dve", lambda E: E.tensor_scalar_mul(nmean[:], sum_x[:],
                                                            -1.0 / 1024.0),
                       wait=[e1, prev_bc[0]], sig=True)
            t2_ = p.do("dve", lambda E: E.tensor_tensor(rstd[:], nmean[:], nmean[:],
                                                        ALU.mult),
                       wait=[t1_], sig=True)
            t3_ = p.do("dve", lambda E: E.scalar_tensor_tensor(
                rstd[:], sum_x2[:], 1.0 / 1024.0, rstd[:],
                ALU.mult, ALU.subtract), wait=[e2, t2_], sig=True)
            t3b = p.do("dve", lambda E: E.tensor_scalar_add(rstd[:], rstd[:], EPS),
                       wait=[t3_], sig=True)
            t4_ = p.do("act", lambda E: E.activation(out=rstd[:], in_=rstd[:],
                                                     func=AF.Sqrt),
                       wait=[t3b], sig=True)
            t5_ = p.do("dve", lambda E: E.reciprocal(out=rstd[:], in_=rstd[:]),
                       wait=[t4_], sig=True)
            bim, pbm = self.bank()
            tbm = self.mm3(pbm[:], [], extra_f32=[(self.ones1[:], nmean[:])],
                           wait=[t1_, t2_])
            ebm = self.evac(pbm[:], bc_nm, wait=[tbm, stat_tok[0]])
            self.bank_done(bim, ebm)
            bir, pbr = self.bank()
            tbr = self.mm3(pbr[:], [], extra_f32=[(self.ones1[:], rstd[:])],
                           wait=[t5_])
            ebr = self.evac(pbr[:], bc_rs, wait=[tbr, stat_tok[0]])
            self.bank_done(bir, ebr)
            prev_bc[0] = tbr
            ln_toks = []
            for oc in range(NC):
                xsl = xpre[:, oc * S + tb * TB: oc * S + tb * TB + TB]
                ta1 = p.do("dve", lambda E, o=y, a=xsl, b=bc_nm:
                           E.tensor_add(o, a, b),
                           wait=[ebm, ebr, getattr(self, "_ln_tok", None),
                                 getattr(self, "_y_spill", None)] + sq_toks,
                           sig=True)
                ta2 = p.do("dve", lambda E, o=y, a=y, b=bc_rs:
                           E.tensor_mul(o, a, b), sig=True)
                self._ln_tok = ta2
                if which == "attn1":
                    s1 = p.dma("dq_s",
                               ds["x1T"][oc * 128:(oc + 1) * 128, tb * TB:(tb + 1) * TB],
                               y, wait=[ta2])
                    self._y_spill = s1
                    g = self.bias["ln1_g"][:, oc:oc + 1]
                    b = self.bias["ln1_b"][:, oc:oc + 1]
                    dh, dl = ds["h1hi"], ds["h1lo"]
                else:
                    s1 = None
                    g = self.bias["ln2_g"][:, oc:oc + 1]
                    b = self.bias["ln2_b"][:, oc:oc + 1]
                    dh, dl = ds["h2hi"], ds["h2lo"]
                th = p.do("dve", lambda E, o=hbuf, a=y, g=g, b=b:
                          E.tensor_scalar(o, a, g, b, ALU.mult, ALU.add),
                          wait=[ta2] + self.tok_const, sig=True)
                stok = self.split(hbuf, hst_h, hst_l,
                                  wait=[th, getattr(self, "_h_spill", None)])
                sh1 = p.dma("dq_s",
                            dh[oc * 128:(oc + 1) * 128, tb * TB:(tb + 1) * TB],
                            hst_h, wait=[stok])
                sh2 = p.dma("dq_s",
                            dl[oc * 128:(oc + 1) * 128, tb * TB:(tb + 1) * TB],
                            hst_l, wait=[stok])
                self._h_spill = sh2
                ln_toks.append(sh2)
                if s1 is not None:
                    ln_toks.append(s1)
                if which == "attn2":
                    rt2 = p.dma("dq_s", x1rel,
                                ds["x1T"][oc * 128:(oc + 1) * 128, tb * TB:(tb + 1) * TB],
                                wait=[getattr(self, "_s_tok", None)])
                    ts_ = p.do("dve", lambda E, o=sbuf2, a=xsl, b=x1rel:
                               E.tensor_add(o, a, b),
                               wait=[rt2, getattr(self, "_s_tt", None)] + sq_toks,
                               sig=True)
                    for j in range(4):
                        bi, pb = self.bank()
                        ttok = self.transpose(pb[0:128, 0:128],
                                              sbuf2[:, j * 128:(j + 1) * 128],
                                              False, wait=[ts_])
                        es = self.evac(pb[0:128, 0:128], sev,
                                       wait=[ttok, getattr(self, "_s_ev", None)])
                        self.bank_done(bi, es)
                        sspill = p.dma(
                            "dq_s",
                            ds["s_tok"][tb * TB + j * 128: tb * TB + (j + 1) * 128,
                                        oc * 128:(oc + 1) * 128],
                            sev, wait=[es])
                        self._s_ev = sspill
                        self._s_tt = ttok
                    self._s_tok = sspill
                    ln_toks.append(sspill)
            stat_tok[0] = ln_toks[-1] if ln_toks else None
        self.rr_n = 7
        self._prl_done = ln_toks

    def load_w_panels(self, dram_ap, c0, width, slot, extra_wait=()):
        p = self.p
        whi = self.fB(16384 + slot * 4096, NC * width).rearrange(
            "p (c w) -> p c w", c=NC)
        wlo = self.fB(24576 + slot * 4096, NC * width).rearrange(
            "p (c w) -> p c w", c=NC)
        src = dram_ap[:, c0:c0 + width].rearrange("(c p) w -> p c w", p=128)
        t_h = p.dma("dq_g", whi, src,
                    wait=[self.wslot_tok[slot]] + list(extra_wait))
        toks = []
        hw = width // 2
        for half in range(2):
            srch = dram_ap[:, c0 + half * hw:c0 + (half + 1) * hw] \
                .rearrange("(c p) w -> p c w", p=128)
            fstg = self.fF(0, NC * hw).rearrange("p (c w) -> p c w", c=NC)
            t_f = p.dma("dq_s", fstg, srch,
                        wait=[getattr(self, "_wpf_tok", None)])
            sl_hi = whi[:, :, half * hw:(half + 1) * hw]
            sl_lo = wlo[:, :, half * hw:(half + 1) * hw]
            t_l = p.do("dve",
                       lambda E, o=sl_lo, a=fstg, b=sl_hi: E.tensor_sub(o, a, b),
                       wait=[t_f, t_h], sig=True)
            self._wpf_tok = t_l
            toks.append(t_l)
        return whi, wlo, toks

    def p2a(self):
        p = self.p
        ds = self.ds
        t0 = self._prl_done[-1] if getattr(self, "_prl_done", None) else None
        self.wslot_tok = [t0, t0]
        self.xs_tok = [t0, t0]
        self.prj_tok = [None, None]
        for (part, dsthi, dstlo, btile, sc_) in (
                (0, ds["qhi"], ds["qlo"], self.bqkv8, 0.125),
                (1, ds["khi"], ds["klo"], None, 1.0)):
            for oc2 in range(2):
                c0 = part * D + oc2 * 512
                slot = oc2 % 2
                whi, wlo, wt = self.load_w_panels(
                    self.din["w_qkv"], c0, 512, slot,
                    extra_wait=[getattr(self, "tok_presplit", None)])
                for tb in range(NTB):
                    sg = tb % 2
                    ht = self.x_block(ds["h1hi"], ds["h1lo"], tb, sg)
                    for oq in range(4):
                        oc = oc2 * 4 + oq
                        bi, pb = self.bank()
                        pairs = []
                        for dc in range(NC):
                            pairs.append((
                                whi[:, dc, oq * 128:(oq + 1) * 128],
                                wlo[:, dc, oq * 128:(oq + 1) * 128],
                                self.xs_hi[sg][:, dc * TB:(dc + 1) * TB],
                                self.xs_lo[sg][:, dc * TB:(dc + 1) * TB]))
                        mtok = self.mm3(pb[:], pairs, wait=wt + ht)
                        self.xs_tok[sg] = mtok
                        self.wslot_tok[slot] = mtok
                        psg = (tb * 4 + oq) % 2
                        fsl = self.prjf[:, psg * TB:(psg + 1) * TB]
                        if part == 0:
                            bslice = self.bqkv8[:, oc:oc + 1]
                        else:
                            bslice = self.bias["b_qkv"][:, NC + oc:NC + oc + 1]
                        et = self.evac(pb[:], fsl, bias=bslice, scale=sc_,
                                       wait=[mtok, self.prj_tok[psg]] + self.tok_const)
                        self.bank_done(bi, et)
                        stok = self.split(fsl, self.prjh[:, psg * TB:(psg + 1) * TB],
                                          self.prjl[:, psg * TB:(psg + 1) * TB],
                                          wait=[et])
                        p.dma("dq_s",
                              dsthi[oc * 128:(oc + 1) * 128, tb * TB:(tb + 1) * TB],
                              self.prjh[:, psg * TB:(psg + 1) * TB], wait=[stok])
                        t2 = p.dma("dq_s",
                                   dstlo[oc * 128:(oc + 1) * 128, tb * TB:(tb + 1) * TB],
                                   self.prjl[:, psg * TB:(psg + 1) * TB], wait=[stok])
                        self.prj_tok[psg] = t2
        bvr = self.fF(2048, D)[0:1, :]
        tbv = p.dma("dq_s", bvr, self.din["b_qkv"][:, 2 * D:3 * D],
                    wait=[getattr(self, "_wpf_tok", None)])
        self._wpf_tok = tbv
        w_views = [None, None]
        wt_all = []
        for ib in range(2):
            whi, wlo, wt = self.load_w_panels(self.din["w_qkv"], 2 * D + ib * 512,
                                              512, ib)
            w_views[ib] = (whi, wlo)
            wt_all += wt
        self._vtok(ds["h1hi"], ds["h1lo"],
                   [w_views[0][0], w_views[1][0]],
                   [w_views[0][1], w_views[1][1]],
                   wt_all, bvr, [tbv], trans_w=False)
        self._ah_state["tok_qk"] = self.xs_tok[0]
        self._ah_state["tok_qk"] = self.xs_tok[1]

    def load_wo(self):
        p = self.p
        whi = self.fB(16384, 8192).rearrange("p (c i) -> p c i", c=NC)
        wlo = self.fB(24576, 8192).rearrange("p (c i) -> p c i", c=NC)
        src = self.din["w_o"].rearrange("(c p) i -> p c i", p=128)
        t_h = p.dma("dq_g", whi, src, wait=self._ah_state["tok_oT"] +
                    [self._ah_state["tok_w"], self._ah_state["tok_qk"]])
        toks = []
        for qtr in range(4):
            srch = self.din["w_o"][:, qtr * 256:(qtr + 1) * 256] \
                .rearrange("(c p) w -> p c w", p=128)
            fstg = self.fF(0, NC * 256).rearrange("p (c w) -> p c w", c=NC)
            t_f = p.dma("dq_s", fstg, srch,
                        wait=[getattr(self, "_wpf_tok", None),
                              self._ah_state["tok_sc"][0],
                              self._ah_state["tok_sc"][1],
                              self._ah_state["tok_sc"][2]])
            sl_hi = whi[:, :, qtr * 256:(qtr + 1) * 256]
            sl_lo = wlo[:, :, qtr * 256:(qtr + 1) * 256]
            t_l = p.do("dve",
                       lambda E, o=sl_lo, a=fstg, b=sl_hi: E.tensor_sub(o, a, b),
                       wait=[t_f, t_h], sig=True)
            self._wpf_tok = t_l
            toks.append(t_l)
        self._wo_views = (whi, wlo)
        self.wo_tok = toks[-1:]

    def p3_gating(self):
        p = self.p
        ds = self.ds
        gw_hi = self.fB(2048, NC * NE)
        gw_lo = self.fB(2112, NC * NE)
        h2t_hi = self.fB(0, NC * TT)
        h2t_lo = self.fB(1024, NC * TT)
        gsrc = self.din["gate_w"].rearrange("(c p) e -> p c e", p=128)
        t_prev = self._prl_done[-1] if getattr(self, "_prl_done", None) else None
        tgh = p.dma("dq_g", gw_hi.rearrange("p (c e) -> p c e", c=NC), gsrc,
                    wait=[t_prev], slow_ok=True)
        tgf = p.dma("dq_s", self.fF(0, NC * NE).rearrange("p (c e) -> p c e", c=NC),
                    gsrc, wait=[getattr(self, "_wpf_tok", None), t_prev],
                    slow_ok=True)
        tgl = p.do("dve", lambda E: E.tensor_sub(gw_lo, self.fF(0, NC * NE), gw_hi),
                   wait=[tgh, tgf], sig=True)
        htok = tgl
        ew4 = self.ew.rearrange("p (t e) -> p t e", t=NTT)
        gl4 = self.glog.rearrange("p (t e) -> p t e", t=NTT)
        prev_gate = None
        prev_te = None
        prev_tw8 = None
        for tt in range(NTT):
            t1 = p.dma("dq_s", h2t_hi.rearrange("p (c w) -> p c w", c=NC),
                       ds["h2hi"].rearrange("(c p) s -> p c s", p=128)
                       [:, :, tt * TT:(tt + 1) * TT],
                       wait=[htok])
            t2 = p.dma("dq_s", h2t_lo.rearrange("p (c w) -> p c w", c=NC),
                       ds["h2lo"].rearrange("(c p) s -> p c s", p=128)
                       [:, :, tt * TT:(tt + 1) * TT])
            bi, pb = self.bank()
            pairs = []
            for dc in range(NC):
                pairs.append((
                    h2t_hi.rearrange("p (c w) -> p c w", c=NC)[:, dc, :],
                    h2t_lo.rearrange("p (c w) -> p c w", c=NC)[:, dc, :],
                    gw_hi.rearrange("p (c e) -> p c e", c=NC)[:, dc, :],
                    gw_lo.rearrange("p (c e) -> p c e", c=NC)[:, dc, :]))
            mtok = self.mm3(pb[0:128, 0:NE], pairs,
                            extra_f32=[(self.ones1[0:1, 0:128], self.gate_b_row[:])],
                            wait=[t1, t2, tgl] + self.tok_const)
            htok = mtok
            et = self.evac(pb[0:128, 0:NE], gl4[:, tt, :], wait=[mtok, prev_gate])
            self.bank_done(bi, et)
            tok_gmax = p.do("dve", lambda E, o=self.gsc[:, 0:8], i=gl4[:, tt, :]:
                            E.max(out=o, in_=i), wait=[et], sig=True)
            t_n = p.do("dve", lambda E: E.tensor_scalar_mul(
                self.gsc[:, 8:9], self.gsc[:, 0:1], -1.0),
                wait=[prev_te, tok_gmax], sig=True)
            t_e = p.do("act", lambda E, g=gl4[:, tt, :]: E.activation(
                out=self.gsc[:, 16:24], in_=g, func=AF.Exp,
                bias=self.gsc[:, 8:9]), wait=[t_n, prev_tw8], sig=True)
            prev_te = t_e
            t_m8 = p.do("dve", lambda E, g=gl4[:, tt, :], w=ew4[:, tt, :]:
                E.tensor_tensor(w, g, self.gsc[:, 1:2].to_broadcast([128, NE]),
                                ALU.is_ge),
                wait=[t_e], sig=True)
            t_w8 = p.do("dve", lambda E, w=ew4[:, tt, :]: E.tensor_tensor(
                w, w, self.gsc[:, 16:24], ALU.mult), wait=[t_m8], sig=True)
            prev_tw8 = t_w8
            if "dbg_g2" in self.debug_outputs:
                p.dma("dq_s", ds["dbg_g2"][tt, 0], self.gsc[:, 16:24], wait=[t_e])
                p.dma("dq_s", ds["dbg_g2"][tt, 1], ew4[:, tt, :], wait=[t_m8])
                p.dma("dq_s", ds["dbg_g2"][tt, 2], ew4[:, tt, :], wait=[t_w8])
            t_su = p.do("dve", lambda E, w=ew4[:, tt, :]: E.tensor_reduce(
                self.gsc[:, 10:11], w, AXX, ALU.add), wait=[t_w8], sig=True)
            t_r = p.do("dve", lambda E: E.reciprocal(
                out=self.gsc[:, 11:12], in_=self.gsc[:, 10:11]), wait=[t_su],
                sig=True)
            prev_gate = p.do("dve", lambda E, w=ew4[:, tt, :]:
                E.tensor_tensor(w, w, self.gsc[:, 11:12].to_broadcast([128, NE]),
                                ALU.mult),
                wait=[t_r], sig=True)
            if "dbg_g2" in self.debug_outputs:
                p.dma("dq_s", ds["dbg_g2"][tt, 3], ew4[:, tt, :], wait=[prev_gate])
        self.tok_gate = prev_gate
        self.ewT = self.fF(16384, 2048, p=NE)
        self.b2sb = self.fF(18432, D, p=NE)
        ew_toks = []
        for tt in range(NTT):
            bi, pb = self.bank()
            ttok = self.transpose(pb[0:NE, 0:128], ew4[:, tt, :], False,
                                  wait=[self.tok_gate, self._ln_tok])
            et = p.do("dve", lambda E, o=self.ewT[:, tt * TT:(tt + 1) * TT],
                      i=pb[0:NE, 0:128]: E.tensor_copy(out=o, in_=i),
                      wait=[ttok], sig=True)
            self.bank_done(bi, et)
            ew_toks.append(et)
        self.tok_ewT = ew_toks[-1:]
        if "dbg_gate" in self.debug_outputs:
            p.dma("dq_s", ds["dbg_gate"][0], self.glog[:], wait=[self.tok_gate])
            p.dma("dq_s", ds["dbg_gate"][1], self.ew[:], wait=[self.tok_gate])

    def presplit_moe_weights(self):
        ds = self.ds
        tok = self.wpan_tok[1] if hasattr(self, "wpan_tok") else None
        for e in range(NE):
            for rc in range(NC):
                for chalf in range(2):
                    src = self.din["W1"][e][rc * 128:(rc + 1) * 128,
                                            chalf * 2048:(chalf + 1) * 2048]
                    dst_h = ds["W1hi"][e][rc * 128:(rc + 1) * 128,
                                          chalf * 2048:(chalf + 1) * 2048]
                    dst_l = ds["W1lo"][e][rc * 128:(rc + 1) * 128,
                                          chalf * 2048:(chalf + 1) * 2048]
                    tok = self._presplit_panel(src, dst_h, dst_l, tok)
        for e in range(NE):
            for rc in range(DFF // 128):
                src = self.din["W2"][e][rc * 128:(rc + 1) * 128, :]
                dst_h = ds["W2hi"][e][rc * 128:(rc + 1) * 128, :]
                dst_l = ds["W2lo"][e][rc * 128:(rc + 1) * 128, :]
                tok = self._presplit_panel(src, dst_h, dst_l, tok)
        self.tok_presplit = tok
        self._wpf_tok = tok

    def _presplit_panel(self, src, dst_h, dst_l, tok):
        p = self.p
        w = src.shape[-1]
        pf = self.fF(16384, w)
        ph = self.fB(36864, w)
        pl = self.fB(38912, w)
        w0 = self.wpan_tok if hasattr(self, "wpan_tok") else [None, None]
        t_f = p.dma("dq_a", pf, src, wait=[tok, w0[0]])
        t_h = p.dma("dq_g", ph, src, wait=[tok, w0[1]])
        t_l = p.do("gp", lambda E: E.tensor_tensor(pl, pf, ph, ALU.subtract),
                   wait=[t_f, t_h], sig=True)
        p.dma("dq_a", dst_h, ph, wait=[t_h, t_l])
        t2 = p.dma("dq_g", dst_l, pl, wait=[t_l])
        return t2

    def p5_moe(self):
        p = self.p
        ds = self.ds
        acc = self.fF(0, NTT * D)
        acc4 = acc.rearrange("p (t d) -> p t d", t=NTT)
        tb2 = p.dma("dq_s", self.b2sb, self.din["b2"],
                    wait=[getattr(self, "_s_tok", None)])
        init_toks = []
        for tt in range(NTT):
            for db in range(2):
                bi, pb = self.bank()
                mtok = self.mm3(pb[:], [],
                                extra_f32=[(self.ewT[:, tt * TT:(tt + 1) * TT],
                                            self.b2sb[:, db * 512:(db + 1) * 512])],
                                wait=[tb2] + self.tok_ewT)
                et = p.do("dve", lambda E, o=acc4[:, tt, db * 512:(db + 1) * 512],
                          i=pb[:]: E.tensor_copy(out=o, in_=i),
                          wait=[mtok, self._ln_tok, getattr(self, "_s_tok", None)],
                          sig=True)
                self.bank_done(bi, et)
                init_toks.append(et)
        acc_tok = init_toks[-1]

        W1h = self.fB(0, NC * 512).rearrange("p (c w) -> p c w", c=NC)
        W1l = self.fB(4096, NC * 512).rearrange("p (c w) -> p c w", c=NC)
        W2 = [
            (self.fB(8192, 4 * 1024).rearrange("p (c w) -> p c w", c=4),
             self.fB(12288, 4 * 1024).rearrange("p (c w) -> p c w", c=4)),
            (self.fB(16384, 4 * 1024).rearrange("p (c w) -> p c w", c=4),
             self.fB(20480, 4 * 1024).rearrange("p (c w) -> p c w", c=4)),
        ]
        hid_h = [self.fB(24576, 4 * 512).rearrange("p (c w) -> p c w", c=4),
                 self.fB(28672, 4 * 512).rearrange("p (c w) -> p c w", c=4)]
        hid_l = [self.fB(26624, 4 * 512).rearrange("p (c w) -> p c w", c=4),
                 self.fB(30720, 4 * 512).rearrange("p (c w) -> p c w", c=4)]
        h2h = self.fB(32768, NC * 512).rearrange("p (c w) -> p c w", c=NC)
        h2l = self.fB(36864, NC * 512).rearrange("p (c w) -> p c w", c=NC)
        b1s = self.bias["b1"].rearrange("p (e c) -> p e c", e=NE)

        w1_tok = self._ah_state["tok_qk"]
        w2_tok = [w1_tok, w1_tok]
        h2_tok = self.tok_presplit
        hid_tok = [None, None]
        for e in range(NE):
            for pair in range(4):
                wt2 = []
                for half in range(2):
                    f8 = 2 * pair + half
                    t_a = p.dma("dq_s", W2[half][0],
                                ds["W2hi"][e].rearrange("(c p) d -> p c d", p=128)
                                [:, f8 * 4:(f8 + 1) * 4, :],
                                wait=[w2_tok[half], self.tok_presplit])
                    t_b = p.dma("dq_s", W2[half][1],
                                ds["W2lo"][e].rearrange("(c p) d -> p c d", p=128)
                                [:, f8 * 4:(f8 + 1) * 4, :])
                    wt2 += [t_a, t_b]
                for tb in range(NTB):
                    t_h2a = p.dma("dq_s", h2h,
                                  ds["h2hi"].rearrange("(c p) s -> p c s", p=128)
                                  [:, :, tb * TB:(tb + 1) * TB], wait=[h2_tok])
                    t_h2b = p.dma("dq_s", h2l,
                                  ds["h2lo"].rearrange("(c p) s -> p c s", p=128)
                                  [:, :, tb * TB:(tb + 1) * TB])
                    ht = [t_h2a, t_h2b]
                    for half in range(2):
                        f8 = 2 * pair + half
                        t_w1h = p.dma("dq_s", W1h,
                                      ds["W1hi"][e].rearrange("(c p) f -> p c f", p=128)
                                      [:, :, f8 * 512:(f8 + 1) * 512],
                                      wait=[w1_tok, self.tok_presplit])
                        t_w1l = p.dma("dq_s", W1l,
                                      ds["W1lo"][e].rearrange("(c p) f -> p c f", p=128)
                                      [:, :, f8 * 512:(f8 + 1) * 512])
                        wt1 = [t_w1h, t_w1l]
                        for ft in range(4):
                            bi, pb = self.bank()
                            pairs = []
                            for dc in range(NC):
                                pairs.append((
                                    W1h[:, dc, ft * 128:(ft + 1) * 128],
                                    W1l[:, dc, ft * 128:(ft + 1) * 128],
                                    h2h[:, dc, :], h2l[:, dc, :]))
                            mtok = self.mm3(pb[:], pairs, wait=wt1 + ht)
                            h2_tok = mtok
                            w1_tok = mtok
                            bap = b1s[:, e, f8 * 4 + ft: f8 * 4 + ft + 1]
                            fsl = self.prjf[:, (ft % 2) * TB:(ft % 2 + 1) * TB]
                            et = self.evac(pb[:], fsl, func=AF.Relu, bias=bap,
                                           wait=[mtok, hid_tok[half]] + self.tok_const)
                            self.bank_done(bi, et)
                            stok = self.split(fsl, hid_h[half][:, ft, :],
                                              hid_l[half][:, ft, :], wait=[et])
                            hid_tok[half] = stok
                    for ttj in range(4):
                        for db in range(2):
                            bi, pb = self.bank()
                            pairs = []
                            for half in range(2):
                                for fc in range(4):
                                    pairs.append((
                                        hid_h[half][:, fc, ttj * 128:(ttj + 1) * 128],
                                        hid_l[half][:, fc, ttj * 128:(ttj + 1) * 128],
                                        W2[half][0][:, fc, db * 512:(db + 1) * 512],
                                        W2[half][1][:, fc, db * 512:(db + 1) * 512]))
                            mt2 = self.mm3(pb[:], pairs,
                                           wait=[hid_tok[0], hid_tok[1]] + wt2)
                            w2_tok = [mt2, mt2]
                            hid_tok = [mt2, mt2]
                            ewc = self.ew.rearrange("p (t e) -> p t e", t=NTT) \
                                [:, tb * 4 + ttj, e:e + 1]
                            at = p.do("dve", lambda E,
                                      o=acc4[:, tb * 4 + ttj, db * 512:(db + 1) * 512],
                                      i=pb[:], s=ewc:
                                      E.scalar_tensor_tensor(
                                          o, i, s, o, ALU.mult, ALU.add),
                                      wait=[mt2, acc_tok, self.tok_gate], sig=True)
                            self.bank_done(bi, at)
                            acc_tok = at
        self.tok_moe = acc_tok
        if "dbg_acc" in self.debug_outputs:
            p.dma("dq_s", ds["dbg_acc"].rearrange("(t p) d -> p t d", p=128),
                  acc4, wait=[acc_tok])

    def p6_final(self):
        p = self.p
        ds = self.ds
        acc4 = self.fF(0, NTT * D).rearrange("p (t d) -> p t d", t=NTT)
        y = self.fF(16384, D)
        sqv = self.fF(17408, D)
        rel = self.fF(18432, D)
        st_tok = None
        out_toks = []
        for tt in range(NTT):
            rt = p.dma("dq_s", rel, ds["s_tok"][tt * TT:(tt + 1) * TT, :],
                       wait=[st_tok, self.tok_moe])
            t_y = p.do("dve", lambda E, o=y, a=acc4[:, tt, :], b=rel:
                       E.tensor_add(o, a, b), wait=[rt, self.tok_moe], sig=True)
            t_m = p.do("dve", lambda E: E.tensor_reduce(
                self.gsc[:, 0:1], y, AXX, ALU.add), wait=[t_y], sig=True)
            t_nm = p.do("dve", lambda E: E.tensor_scalar_mul(
                self.gsc[:, 1:2], self.gsc[:, 0:1], -1.0 / 1024.0), wait=[t_m],
                sig=True)
            t_sq = p.do("act", lambda E, o=sqv, i=y:
                        E.activation(out=o, in_=i, func=AF.Square),
                        wait=[t_y, st_tok], sig=True)
            t_s2 = p.do("dve", lambda E: E.tensor_reduce(
                self.gsc[:, 2:3], sqv, AXX, ALU.add), wait=[t_sq], sig=True)
            t_v1 = p.do("dve", lambda E: E.tensor_tensor(
                self.gsc[:, 3:4], self.gsc[:, 1:2], self.gsc[:, 1:2], ALU.mult),
                wait=[t_nm], sig=True)
            t_v2 = p.do("dve", lambda E: E.scalar_tensor_tensor(
                self.gsc[:, 4:5], self.gsc[:, 2:3], 1.0 / 1024.0, self.gsc[:, 3:4],
                ALU.mult, ALU.subtract), wait=[t_s2, t_v1], sig=True)
            t_v3 = p.do("dve", lambda E: E.tensor_scalar_add(
                self.gsc[:, 4:5], self.gsc[:, 4:5], EPS), wait=[t_v2], sig=True)
            t_sqr = p.do("act", lambda E: E.activation(
                out=self.gsc[:, 5:6], in_=self.gsc[:, 4:5], func=AF.Sqrt),
                wait=[t_v3], sig=True)
            t_r = p.do("dve", lambda E: E.reciprocal(
                out=self.gsc[:, 6:7], in_=self.gsc[:, 5:6]), wait=[t_sqr], sig=True)
            t_o = p.do("dve", lambda E, o=rel, a=y:
                       E.tensor_scalar(o, a, self.gsc[:, 1:2], self.gsc[:, 6:7],
                                       ALU.add, ALU.mult),
                       wait=[t_r], sig=True)
            st_tok = p.dma("dq_s", self.dout[tt * TT:(tt + 1) * TT, :], rel,
                           wait=[t_o])
            out_toks.append(st_tok)
        p._wait("sync", out_toks)

    def build(self):
        self.setup()
        self.p0_input()
        if self.stop_after == "p0":
            self.p.finalize()
            return self.nc
        self.p1a()
        if self.stop_after == "p1a":
            self.p.finalize()
            return self.nc
        self.presplit_moe_weights()
        self.attn_heads(True)
        if self.stop_after == "p1b":
            self.p.finalize()
            return self.nc
        self.proj_residual_ln("attn1")
        if self.stop_after == "p1c":
            self.p.finalize()
            return self.nc
        self.p2a()
        if self.stop_after == "p2a":
            self.p.finalize()
            return self.nc
        self.attn_heads(False)
        if self.stop_after == "p2b":
            self.p.finalize()
            return self.nc
        self.load_wo()
        self.proj_residual_ln("attn2")
        if self.stop_after == "p2c":
            self.p.finalize()
            return self.nc
        self.p3_gating()
        if self.stop_after == "p3":
            self.p.finalize()
            return self.nc
        self.p5_moe()
        if self.stop_after == "p5":
            self.p.finalize()
            return self.nc
        self.p6_final()
        import sys
        print(f"[kernel-build] sbuf_bytes_remaining={self.nc.sbuf_bytes_remaining}",
              file=sys.stderr)
        self.p.finalize()
        return self.nc


_CACHED = {}


def _get_nc(stop_after=None):
    key = stop_after
    if key not in _CACHED:
        _CACHED[key] = Builder(stop_after=stop_after).build()
    return _CACHED[key]


def kernel(**inputs):
    nc = _get_nc()
    x = np.ascontiguousarray(np.asarray(inputs["L0_attn_x"], dtype=np.float32))
    wmap = {
        "Wq": "L0_attn_Wq", "bq": "L0_attn_bq", "Wk": "L0_attn_Wk",
        "bk": "L0_attn_bk", "Wv": "L0_attn_Wv", "bv": "L0_attn_bv",
        "Wo": "L0_attn_Wo", "bo": "L0_attn_bo",
        "ln1_g": "L0_ffn_ln1_g", "ln1_b": "L0_ffn_ln1_b",
        "w_qkv": "L0_ffn_w_qkv", "b_qkv": "L0_ffn_b_qkv",
        "w_o": "L0_ffn_w_o", "b_o": "L0_ffn_b_o",
        "ln2_g": "L0_ffn_ln2_g", "ln2_b": "L0_ffn_ln2_b",
        "gate_w": "L0_ffn_gate_w", "gate_b": "L0_ffn_gate_b",
        "W1": "L0_ffn_W1", "b1": "L0_ffn_b1",
        "W2": "L0_ffn_W2", "b2": "L0_ffn_b2",
    }
    base = {}
    for k, srcn in wmap.items():
        a = np.ascontiguousarray(np.asarray(inputs[srcn], dtype=np.float32))
        if a.ndim == 1:
            a = a.reshape(1, -1)
        base[k] = a
    in_maps = []
    for b in range(8):
        m = {"x": np.ascontiguousarray(x[b])}
        m.update(base)
        in_maps.append(m)
    res = run_bass_kernel_spmd(nc, in_maps, core_ids=list(range(8)))
    return np.stack([res.results[b]["out"] for b in range(8)], axis=0)


# version 07
# speedup vs baseline: 1.1166x; 1.1166x over previous
import numpy as np
import concourse.bass as bass
import concourse.mybir as mybir
from concourse.bass_utils import run_bass_kernel_spmd

F32 = mybir.dt.float32
BF16 = mybir.dt.bfloat16
AF = mybir.ActivationFunctionType
ALU = mybir.AluOpType
AXX = mybir.AxisListType.X

S = 2048
D = 1024
H = 16
HD = 64
DFF = 4096
NE = 8
NC = 8
TB = 512
NTB = S // TB
TT = 128
NTT = S // TT
NEG = -30000.0
EPS = 1e-5
VW65 = 65

ENGINES = ("pe", "dve", "act", "gp", "sync")
DQ = {"dq_s": "sync", "dq_g": "gp", "dq_a": "act"}


class P:

    def __init__(self, nc):
        self.nc = nc
        self.q = {e: [] for e in ENGINES}
        self.sem = {}
        self.cnt = {}
        for s in ENGINES + tuple(DQ):
            self.sem[s] = nc.alloc_semaphore(f"S_{s}")
            self.cnt[s] = 0
        self.waited = {e: {s: 0 for s in self.sem} for e in ENGINES}

    def _wait(self, eng, toks):
        for t in toks:
            if t is None:
                continue
            sname, v = t
            if v > self.waited[eng][sname]:
                self.waited[eng][sname] = v
                sem = self.sem[sname]
                self.q[eng].append(lambda E, sem=sem, v=v: E.wait_ge(sem, v))

    def do(self, eng, fn, wait=(), sig=False):
        self._wait(eng, wait)
        if sig:
            self.cnt[eng] += 1
            v = self.cnt[eng]
            sem = self.sem[eng]
            self.q[eng].append(lambda E, fn=fn, sem=sem: fn(E).then_inc(sem, 1))
            return (eng, v)
        self.q[eng].append(fn)
        return None

    def dma(self, qn, out, in_, wait=(), slow_ok=False):
        eng = DQ[qn]
        self._wait(eng, wait)
        self.cnt[qn] += 16
        v = self.cnt[qn]
        sem = self.sem[qn]
        self.q[eng].append(
            lambda E, out=out, in_=in_, sem=sem, so=slow_ok:
            E.dma_start(out=out, in_=in_, allow_slow_non_contiguous=so)
            .then_inc(sem, 16))
        return (qn, v)

    def drain(self, eng):
        self.q[eng].append(lambda E: E.drain())

    def finalize(self):
        for qn, eng in DQ.items():
            if self.cnt[qn]:
                self._wait(eng, [(qn, self.cnt[qn])])
        with self.nc.Block() as blk:
            @blk.tensor
            def _(E):
                for fn in self.q["pe"]:
                    fn(E)

            @blk.vector
            def _(E):
                for fn in self.q["dve"]:
                    fn(E)

            @blk.scalar
            def _(E):
                for fn in self.q["act"]:
                    fn(E)

            @blk.gpsimd
            def _(E):
                for fn in self.q["gp"]:
                    fn(E)

            @blk.sync
            def _(E):
                for fn in self.q["sync"]:
                    fn(E)


class Builder:
    def __init__(self, stop_after=None, debug_outputs=()):
        self.stop_after = stop_after
        self.debug_outputs = set(debug_outputs)
        nc = bass.Bass("TRN2", target_bir_lowering=False,
                       detect_race_conditions=False)
        self.nc = nc
        self.p = P(nc)
        self.pbank = [nc.alloc_psum_tensor(f"pb{i}", [128, 512], F32)
                      for i in range(7)]
        self.pbank_bf = nc.alloc_psum_tensor("pbt", [128, 1024], BF16)
        self.bank_tok = [None] * 8
        self._rr = 0
        self.rr_n = 7
        self.din = {}
        self.ds = {}
        self._declare_dram()
        self._alloc_sbuf()

    def _declare_dram(self):
        nc = self.nc
        self.din["x"] = nc.dram_tensor("x", [S, D], F32, kind="ExternalInput").ap()
        wspecs = {
            "Wq": [D, D], "bq": [1, D], "Wk": [D, D], "bk": [1, D],
            "Wv": [D, D], "bv": [1, D], "Wo": [D, D], "bo": [1, D],
            "ln1_g": [1, D], "ln1_b": [1, D],
            "w_qkv": [D, 3 * D], "b_qkv": [1, 3 * D],
            "w_o": [D, D], "b_o": [1, D],
            "ln2_g": [1, D], "ln2_b": [1, D],
            "gate_w": [D, NE], "gate_b": [1, NE],
            "W1": [NE, D, DFF], "b1": [NE, DFF],
            "W2": [NE, DFF, D], "b2": [NE, D],
        }
        for k, shp in wspecs.items():
            self.din[k] = nc.dram_tensor(k, shp, F32, kind="ExternalInput").ap()
        self.dout = nc.dram_tensor("out", [S, D], F32, kind="ExternalOutput").ap()
        scr = {
            "dbg_ex": ([H, NTT, 128, 64], F32),
            "dbg_w": ([2, S, TB], BF16),
            "dbg_gate": ([3, 128, NTT * NE], F32),
            "dbg_g2": ([NTT, 4, 128, NE], F32),
            "dbg_acc": ([S, D], F32),
            "x0T": ([D, S], F32), "x0Thi": ([D, S], BF16), "x0Tlo": ([D, S], BF16),
            "qhi": ([D, S], BF16), "qlo": ([D, S], BF16),
            "khi": ([D, S], BF16), "klo": ([D, S], BF16),
            "vhi": ([H, S, HD], BF16), "vlo": ([H, S, HD], BF16),
            "woThi": ([D, D], BF16), "woTlo": ([D, D], BF16),
            "oThi": ([D, S], BF16), "oTlo": ([D, S], BF16),
            "x1T": ([D, S], F32),
            "h1hi": ([D, S], BF16), "h1lo": ([D, S], BF16),
            "s_tok": ([S, D], F32),
            "h2hi": ([D, S], BF16), "h2lo": ([D, S], BF16),
            "W1hi": ([NE, D, DFF], BF16), "W1lo": ([NE, D, DFF], BF16),
            "W2hi": ([NE, DFF, D], BF16), "W2lo": ([NE, DFF, D], BF16),
        }
        for k, (shp, dt) in scr.items():
            kind = "ExternalOutput" if k in self.debug_outputs else "Internal"
            self.ds[k] = nc.dram_tensor("z_" + k, shp, dt, kind=kind).ap()

    def _alloc_sbuf(self):
        nc = self.nc

        def real(name, shape, dt):
            return nc.alloc_sbuf_tensor(name, shape, dt)

        self.arF = real("arF", [128, 20736], F32)
        self.arB = real("arB", [128, 40960], BF16)
        self.prjf = real("prjf", [128, 1024], F32)
        self.prjh = real("prjh", [128, 1024], BF16)
        self.prjl = real("prjl", [128, 1024], BF16)
        self.ones1 = real("ones1", [1, 128], F32)
        self.ones_sq = real("ones_sq", [128, 128], F32)
        self.identf = real("identf", [128, 128], F32)
        self.identb = real("identb", [128, 128], BF16)
        self.ex = real("ex", [128, 64], F32)
        self.mt = real("mt", [128, 8], F32)
        self.mrow = real("mrow", [1, 512], F32)
        self.trow = real("trow", [1, 512], F32)
        self.srow = [real(f"srow{i}", [1, 512], F32) for i in range(4)]
        self.bias = {}
        for nm in ("bq", "bk", "bv_fm", "bo", "b_o", "ln1_g", "ln1_b",
                   "ln2_g", "ln2_b"):
            self.bias[nm] = real(f"bt_{nm}", [128, NC], F32)
        self.bias["b_qkv"] = real("bt_b_qkv", [128, 3 * NC], F32)
        self.bias["b1"] = real("bt_b1", [128, NE * (DFF // 128)], F32)
        self.bq8 = real("bq8", [128, NC], F32)
        self.bqkv8 = real("bqkv8", [128, NC], F32)
        self.gate_b_row = real("gate_b_row", [1, NE], F32)
        self.glog = real("glog", [128, NTT * NE], F32)
        self.ew = real("ew", [128, NTT * NE], F32)
        self.gsc = real("gsc", [128, 24], F32)

    def fF(self, off, w, p=128):
        return self.arF[0:p, off:off + w]

    def fB(self, off, w, p=128):
        return self.arB[0:p, off:off + w]

    def bank(self, wait_extra=()):
        i = self._rr
        self._rr = (self._rr + 1) % self.rr_n
        if self.bank_tok[i] is not None:
            self.p._wait("pe", [self.bank_tok[i]])
        self.p._wait("pe", wait_extra)
        return i, self.pbank[i]

    def bank_fixed(self, i):
        if self.bank_tok[i] is not None:
            self.p._wait("pe", [self.bank_tok[i]])
        return i, self.pbank[i]

    def bank_done(self, i, tok):
        self.bank_tok[i] = tok

    def mm3(self, psum_ap, pairs, extra_f32=(), wait=()):
        p = self.p
        ops = []
        for (ah, al, bh, bl) in pairs:
            ops += [(ah, bh), (ah, bl), (al, bh)]
        ops += list(extra_f32)
        tok = None
        for k, (a, b) in enumerate(ops):
            st = (k == 0)
            sp = (k == len(ops) - 1)
            tok = p.do("pe",
                       lambda E, o=psum_ap, a=a, b=b, st=st, sp=sp:
                       E.matmul(o, a, b, start=st, stop=sp),
                       wait=(wait if k == 0 else ()), sig=sp)
        return tok

    def evac(self, psum_ap, out_ap, func=AF.Identity, bias=0.0, scale=1.0, wait=()):
        return self.p.do("act",
                         lambda E, o=out_ap, i=psum_ap, f=func, b=bias, s=scale:
                         E.activation(out=o, in_=i, func=f, bias=b, scale=s),
                         wait=wait, sig=True)

    def split(self, src, hi, lo, wait=()):
        p = self.p
        p.do("dve", lambda E, o=hi, i=src: E.tensor_copy(out=o, in_=i), wait=wait)
        return p.do("dve", lambda E, o=lo, a=src, b=hi: E.tensor_sub(o, a, b),
                    sig=True)

    def transpose(self, psum_ap, in_ap, bf16, wait=()):
        ident = self.identb if bf16 else self.identf
        return self.p.do("pe",
                         lambda E, o=psum_ap, i=in_ap, id_=ident:
                         E.transpose(o, i, id_[:]),
                         wait=wait, sig=True)

    def setup(self):
        p = self.p
        t0 = p.do("dve", lambda E: E.memset(self.ones1[:], 1.0), sig=True)
        t1 = p.do("dve", lambda E: E.memset(self.ones_sq[:], 1.0), sig=True)
        t2 = p.do("dve", lambda E: E.memset(self.identf[:], 1.0), sig=True)
        t3 = p.do("gp", lambda E: E.affine_select(
            out=self.identf[:], in_=self.identf[:], pattern=[[1, 128]],
            compare_op=ALU.is_equal, fill=0.0, base=0, channel_multiplier=-1),
            wait=[t2], sig=True)
        t4 = p.do("dve", lambda E: E.tensor_copy(out=self.identb[:], in_=self.identf[:]),
                  wait=[t3], sig=True)
        self.tok_setup = [t0, t1, t3, t4]
        self.tok_const = list(self.tok_setup)
        for nm, srcn in (("bq", "bq"), ("bk", "bk"), ("bv_fm", "bv"), ("bo", "bo"),
                         ("b_o", "b_o"), ("ln1_g", "ln1_g"), ("ln1_b", "ln1_b"),
                         ("ln2_g", "ln2_g"), ("ln2_b", "ln2_b")):
            self.tok_const.append(p.dma(
                "dq_s", self.bias[nm][:],
                self.din[srcn].rearrange("o (c p) -> p (o c)", p=128), slow_ok=True))
        self.tok_const.append(p.dma(
            "dq_s", self.bias["b_qkv"][:],
            self.din["b_qkv"].rearrange("o (c p) -> p (o c)", p=128), slow_ok=True))
        self.tok_const.append(p.dma(
            "dq_s", self.bias["b1"][:],
            self.din["b1"].rearrange("e (c p) -> p (e c)", p=128), slow_ok=True))
        self.tok_const.append(p.dma("dq_s", self.gate_b_row[:], self.din["gate_b"]))
        tbq8 = p.do("dve",
                    lambda E: E.tensor_scalar_mul(self.bq8[:], self.bias["bq"][:], 0.125),
                    wait=list(self.tok_const), sig=True)
        tbq82 = p.do("dve",
                     lambda E: E.tensor_scalar_mul(
                         self.bqkv8[:], self.bias["b_qkv"][:, 0:NC], 0.125),
                     sig=True)
        self.tok_const += [tbq8, tbq82]

    def p0_input(self):
        p = self.p
        ds = self.ds
        xstage = self.fF(0, D)
        xT_col = self.fF(1024, NC * TT)
        xT_hi = self.fB(0, NC * TT)
        xT_lo = self.fB(1024, NC * TT)
        tok_stage = None
        tok_col = None
        sp_toks = []
        for t in range(NTT):
            tok_ld = p.dma("dq_s", xstage, self.din["x"][t * TT:(t + 1) * TT, :],
                           wait=[tok_stage])
            evacs = []
            last_tt = None
            for c in range(NC):
                bi, pb = self.bank()
                last_tt = self.transpose(pb[0:128, 0:128],
                                         xstage[:, c * 128:(c + 1) * 128], False,
                                         wait=[tok_ld] + self.tok_setup)
                et = self.evac(pb[0:128, 0:128], xT_col[:, c * TT:(c + 1) * TT],
                               wait=[last_tt] + ([tok_col] if c == 0 else []))
                self.bank_done(bi, et)
                evacs.append(et)
            tok_stage = last_tt
            sp_toks.append(p.dma(
                "dq_s",
                ds["x0T"].rearrange("(c p) s -> p c s", p=128)[:, :, t * TT:(t + 1) * TT],
                xT_col.rearrange("p (c q) -> p c q", c=NC), wait=evacs))
            stok = self.split(xT_col, xT_hi, xT_lo, wait=evacs)
            sp_toks.append(p.dma(
                "dq_s",
                ds["x0Thi"].rearrange("(c p) s -> p c s", p=128)[:, :, t * TT:(t + 1) * TT],
                xT_hi.rearrange("p (c q) -> p c q", c=NC), wait=[stok]))
            sp_toks.append(p.dma(
                "dq_s",
                ds["x0Tlo"].rearrange("(c p) s -> p c s", p=128)[:, :, t * TT:(t + 1) * TT],
                xT_lo.rearrange("p (c q) -> p c q", c=NC), wait=[stok]))
            tok_col = sp_toks[-1]
        self.tok_p0 = sp_toks[-3:]

    def prep_wT(self, wname):
        p = self.p
        wT_hi = self.fB(16384, 8192)
        wT_lo = self.fB(24576, 8192)
        wpan_h = [self.fB(32768, 1024), self.fB(33792, 1024)]
        wpan_l = [self.fB(34816, 1024), self.fB(35840, 1024)]
        wpan_f = [self.fF(0, 1024), self.fF(1024, 1024)]
        if not hasattr(self, "wpan_tok"):
            self.wpan_tok = [None, None]
        toks = []
        for ic in range(NC):
            pg = ic % 2
            src = self.din[wname][ic * 128:(ic + 1) * 128, :]
            tok_f = p.dma("dq_s", wpan_f[pg], src, wait=[self.wpan_tok[pg]])
            tok_h = p.dma("dq_g", wpan_h[pg], src, wait=[self.wpan_tok[pg]])
            stok = p.do("dve",
                        lambda E, o=wpan_l[pg], a=wpan_f[pg], b=wpan_h[pg]:
                        E.tensor_sub(o, a, b),
                        wait=[tok_f, tok_h], sig=True)
            last = None
            for dc in range(NC):
                for (srcb, dstb, tw) in ((wpan_h[pg], wT_hi, tok_h),
                                         (wpan_l[pg], wT_lo, stok)):
                    ttok = self.transpose(self.pbank_bf[0:128, 0:128],
                                          srcb[:, dc * 128:(dc + 1) * 128], True,
                                          wait=[tw, self.bank_tok[7]] + self.tok_setup)
                    etok = p.do("dve",
                                lambda E,
                                o=dstb[:, dc * 1024 + ic * 128: dc * 1024 + (ic + 1) * 128],
                                i=self.pbank_bf[0:128, 0:128]:
                                E.tensor_copy(out=o, in_=i),
                                wait=[ttok], sig=True)
                    self.bank_tok[7] = etok
                    last = etok
            self.wpan_tok[pg] = last
            toks.append(last)
        return wT_hi, wT_lo, toks[-2:]

    def p1a(self):
        p = self.p
        ds = self.ds
        self.xs_hi = [self.fB(0, NC * TB), self.fB(8192, NC * TB)]
        self.xs_lo = [self.fB(4096, NC * TB), self.fB(12288, NC * TB)]
        self.xs_tok = [None, None]
        self.prj_tok = [None, None]

        wT_hi, wT_lo, tw = self.prep_wT("Wq")
        self._proj_fm(ds["x0Thi"], ds["x0Tlo"], ds["qhi"], ds["qlo"],
                      wT_hi, wT_lo, tw, self.bq8, 0.125, self.tok_p0, True)
        wT_hi, wT_lo, tw = self.prep_wT("Wk")
        self._proj_fm(ds["x0Thi"], ds["x0Tlo"], ds["khi"], ds["klo"],
                      wT_hi, wT_lo, tw, self.bias["bk"], 1.0, [], True)
        wT_hi, wT_lo, tw = self.prep_wT("Wv")
        bvr = self.fF(2048, D)[0:1, :]
        tbv = p.dma("dq_s", bvr, self.din["bv"])
        self._vtok(ds["x0Thi"], ds["x0Tlo"], wT_hi, wT_lo, tw, bvr, [tbv],
                   trans_w=True)
        wT_hi, wT_lo, tw = self.prep_wT("Wo")
        t1 = p.dma("dq_s", ds["woThi"].rearrange("(c p) i -> p c i", p=128),
                   wT_hi.rearrange("p (c i) -> p c i", c=NC), wait=tw)
        t2 = p.dma("dq_s", ds["woTlo"].rearrange("(c p) i -> p c i", p=128),
                   wT_lo.rearrange("p (c i) -> p c i", c=NC), wait=tw)
        self.tok_woT = [t1, t2]

    def x_block(self, dhi, dlo, tb, sg, extra_wait=()):
        p = self.p
        t1 = p.dma("dq_s", self.xs_hi[sg].rearrange("p (c w) -> p c w", c=NC),
                   dhi.rearrange("(c p) s -> p c s", p=128)
                   [:, :, tb * TB:(tb + 1) * TB],
                   wait=[self.xs_tok[sg]] + list(extra_wait))
        t2 = p.dma("dq_s", self.xs_lo[sg].rearrange("p (c w) -> p c w", c=NC),
                   dlo.rearrange("(c p) s -> p c s", p=128)
                   [:, :, tb * TB:(tb + 1) * TB])
        return [t1, t2]

    def _proj_fm(self, src_hi, src_lo, dsthi, dstlo, w_hi, w_lo, wtoks,
                 bias_tile, scale, first_wait, trans_w):
        p = self.p
        for tb in range(NTB):
            sg = tb % 2
            xt = self.x_block(src_hi, src_lo, tb, sg,
                              extra_wait=(first_wait if tb == 0 else []))
            for oc in range(NC):
                bi, pb = self.bank()
                pairs = []
                for dc in range(NC):
                    if trans_w:
                        wh = w_hi[:, dc * 1024 + oc * 128: dc * 1024 + (oc + 1) * 128]
                        wl = w_lo[:, dc * 1024 + oc * 128: dc * 1024 + (oc + 1) * 128]
                    else:
                        wh = w_hi[:, dc, oc * 128:(oc + 1) * 128]
                        wl = w_lo[:, dc, oc * 128:(oc + 1) * 128]
                    pairs.append((
                        wh, wl,
                        self.xs_hi[sg][:, dc * TB:(dc + 1) * TB],
                        self.xs_lo[sg][:, dc * TB:(dc + 1) * TB]))
                mtok = self.mm3(pb[:], pairs, wait=list(wtoks) + xt)
                self.xs_tok[sg] = mtok
                psg = (tb * NC + oc) % 2
                fsl = self.prjf[:, psg * TB:(psg + 1) * TB]
                et = self.evac(pb[:], fsl, bias=bias_tile[:, oc:oc + 1],
                               scale=scale,
                               wait=[mtok, self.prj_tok[psg]] + self.tok_const)
                self.bank_done(bi, et)
                stok = self.split(fsl, self.prjh[:, psg * TB:(psg + 1) * TB],
                                  self.prjl[:, psg * TB:(psg + 1) * TB],
                                  wait=[et])
                p.dma("dq_s",
                      dsthi[oc * 128:(oc + 1) * 128, tb * TB:(tb + 1) * TB],
                      self.prjh[:, psg * TB:(psg + 1) * TB], wait=[stok])
                t2 = p.dma("dq_s",
                           dstlo[oc * 128:(oc + 1) * 128, tb * TB:(tb + 1) * TB],
                           self.prjl[:, psg * TB:(psg + 1) * TB], wait=[stok])
                self.prj_tok[psg] = t2

    def _vtok(self, src_hi, src_lo, w_hi, w_lo, wtoks, brow, btoks, trans_w):
        p = self.p
        ds = self.ds
        for tb in range(NTB):
            sg = tb % 2
            xt = self.x_block(src_hi, src_lo, tb, sg)
            for tt4 in range(4):
                tt = tb * 4 + tt4
                for ib in range(2):
                    bi, pb = self.bank()
                    pairs = []
                    for dc in range(NC):
                        if trans_w:
                            wh = w_hi[:, dc * 1024 + ib * 512: dc * 1024 + (ib + 1) * 512]
                            wl = w_lo[:, dc * 1024 + ib * 512: dc * 1024 + (ib + 1) * 512]
                        else:
                            wh = w_hi[ib][:, dc, :]
                            wl = w_lo[ib][:, dc, :]
                        pairs.append((
                            self.xs_hi[sg][:, dc * TB + tt4 * TT: dc * TB + (tt4 + 1) * TT],
                            self.xs_lo[sg][:, dc * TB + tt4 * TT: dc * TB + (tt4 + 1) * TT],
                            wh, wl))
                    mtok = self.mm3(
                        pb[:], pairs,
                        extra_f32=[(self.ones1[0:1, 0:128],
                                    brow[0:1, ib * 512:(ib + 1) * 512])],
                        wait=list(wtoks) + xt + list(btoks) + self.tok_setup)
                    self.xs_tok[sg] = mtok
                    psg = (tt * 2 + ib) % 2
                    fsl = self.prjf[:, psg * TB:(psg + 1) * TB]
                    et = self.evac(pb[:], fsl, wait=[mtok, self.prj_tok[psg]])
                    self.bank_done(bi, et)
                    stok = self.split(fsl, self.prjh[:, psg * TB:(psg + 1) * TB],
                                      self.prjl[:, psg * TB:(psg + 1) * TB],
                                      wait=[et])
                    sp = None
                    for j in range(8):
                        h = ib * 8 + j
                        sp = p.dma("dq_s",
                                   ds["vhi"][h, tt * TT:(tt + 1) * TT, :],
                                   self.prjh[:, psg * TB + j * 64: psg * TB + (j + 1) * 64],
                                   wait=[stok])
                        sp = p.dma("dq_s",
                                   ds["vlo"][h, tt * TT:(tt + 1) * TT, :],
                                   self.prjl[:, psg * TB + j * 64: psg * TB + (j + 1) * 64])
                    self.prj_tok[psg] = sp

    def attn_heads(self, sparse):
        p = self.p
        ds = self.ds
        dqh, dql = ds["qhi"], ds["qlo"]
        dkh, dkl = ds["khi"], ds["klo"]
        doh, dol = ds["oThi"], ds["oTlo"]
        sc = [self.fF(0, S), self.fF(2048, S), self.fF(4096, S)]
        wq_s = self.fF(6144, TB)
        wq_m = self.fF(6656, TB)
        wq_e = self.fF(7168, TB)
        wq_w = self.fF(7680, TB)
        bc_m = self.fF(8192, TB)
        bc_t = self.fF(8704, TB)
        av_f = self.fF(9216, TB, p=64)
        rb = self.fF(9728, TB, p=64)
        den_t = self.fF(10240, TB)
        rec_t = self.fF(10752, TB)
        vh_b = [self.fB(0, NTT * VW65), self.fB(2080, NTT * VW65)]
        vl_b = [self.fB(1040, NTT * VW65), self.fB(3120, NTT * VW65)]
        qt_h = self.fB(4224, 2 * S, p=64)
        kt_h = self.fB(8320, 2 * S, p=64)
        wT_h2 = self.fB(12416, NTT * TB)
        wT_l2 = self.fB(20608, NTT * TB)
        oT_h = self.fB(28800, TB, p=64)
        oT_l = self.fB(29312, TB, p=64)

        st0 = getattr(self, "_ah_state", None)
        if st0 is None:
            self._ah_state = {
                "tok_qk": None, "tok_sc": [None, None, None], "tok_mt": None,
                "tok_tside": None, "tok_w": None, "tok_av": [None, None],
                "tok_oT": [None, None], "tok_bc": None, "tok_v": [None, None]}
        stt = self._ah_state
        entry = [self.xs_tok[0], self.xs_tok[1], stt["tok_qk"],
                 stt["tok_v"][0], stt["tok_v"][1]]
        tons = []
        for g in range(2):
            t1 = p.do("dve", lambda E, o=vh_b[g].rearrange(
                "p (t j) -> p t j", t=NTT)[:, :, 64:65]: E.memset(o, 1.0),
                wait=entry, sig=True)
            t2 = p.do("dve", lambda E, o=vl_b[g].rearrange(
                "p (t j) -> p t j", t=NTT)[:, :, 64:65]: E.memset(o, 0.0),
                sig=True)
            tons += [t1, t2]
        stt["tok_ones"] = tons
        stt["tok_v"] = [tons[1], tons[3]]
        for h in range(H):
            hg = h % 2
            tq1 = p.dma("dq_s", qt_h[:, 0:S], dqh[h * 64:(h + 1) * 64, :],
                        wait=[stt["tok_qk"]])
            tq2 = p.dma("dq_s", qt_h[:, S:2 * S], dql[h * 64:(h + 1) * 64, :])
            tk1 = p.dma("dq_s", kt_h[:, 0:S], dkh[h * 64:(h + 1) * 64, :])
            tk2 = p.dma("dq_s", kt_h[:, S:2 * S], dkl[h * 64:(h + 1) * 64, :])
            tqk = [tq1, tq2, tk1, tk2]
            tv1 = p.dma("dq_s",
                        vh_b[hg].rearrange("p (t j) -> p t j", t=NTT)[:, :, 0:64],
                        ds["vhi"][h].rearrange("(t p) j -> p t j", p=128),
                        wait=[stt["tok_v"][hg]])
            tv2 = p.dma("dq_s",
                        vl_b[hg].rearrange("p (t j) -> p t j", t=NTT)[:, :, 0:64],
                        ds["vlo"][h].rearrange("(t p) j -> p t j", p=128))
            tvv = [tv1, tv2] + stt["tok_ones"]
            last_qk_use = None
            for qb in range(NTB):
                for qt4 in range(4):
                    qt = qb * 4 + qt4
                    sbi = qt % 2
                    sbuf = sc[sbi]
                    ev_toks = []
                    for kb in range(NTB):
                        bi, pb = self.bank()
                        pairs = [(
                            qt_h[:, qt * 128:(qt + 1) * 128],
                            qt_h[:, S + qt * 128: S + (qt + 1) * 128],
                            kt_h[:, kb * TB:(kb + 1) * TB],
                            kt_h[:, S + kb * TB: S + (kb + 1) * TB])]
                        mtok = self.mm3(pb[:], pairs, wait=tqk)
                        last_qk_use = mtok
                        et = self.evac(pb[:], sbuf[:, kb * TB:(kb + 1) * TB],
                                       wait=[mtok] +
                                       ([stt["tok_sc"][sbi]] if kb == 0 else []))
                        self.bank_done(bi, et)
                        ev_toks.append(et)
                    if sparse:
                        cur = sbuf
                        nxt = sc[2]
                        p._wait("dve", ev_toks + [stt["tok_sc"][2], stt["tok_mt"]])
                        for r in range(8):
                            tok_max = p.do("dve",
                                 lambda E, o=self.ex[:, r * 8:(r + 1) * 8], i=cur:
                                 E.max(out=o, in_=i[:]), sig=True)
                            if r < 7:
                                p.do("dve",
                                     lambda E, o=nxt, v=self.ex[:, r * 8:(r + 1) * 8],
                                     i=cur: E.match_replace(
                                         out=o[:], in_to_replace=v, in_values=i[:],
                                         imm_value=NEG), wait=[tok_max])
                                cur, nxt = nxt, cur
                        p.do("dve", lambda E, o=self.mt[:, 2 * qt4:2 * qt4 + 1],
                             i=self.ex[:, 0:1]: E.tensor_scalar_mul(o, i, -1.0),
                             wait=[tok_max])
                        tmt = p.do("dve", lambda E, o=self.mt[:, 2 * qt4 + 1:2 * qt4 + 2],
                                   i=self.ex[:, 63:64]: E.tensor_copy(out=o, in_=i),
                                   sig=True)
                        stt["tok_sc"][sbi] = tmt
                        stt["tok_sc"][2] = tmt
                        stt["tok_mt"] = tmt
                        if self.stop_after == "p1b":
                            p.dma("dq_s", ds["dbg_ex"][h, qt], self.ex[:],
                                  wait=[tmt])
                    else:
                        tmt = p.do("dve",
                                   lambda E, o=self.mt[:, 2 * qt4:2 * qt4 + 1], i=sbuf:
                                   E.tensor_reduce(o, i[:], AXX, ALU.max, negate=True),
                                   wait=ev_toks + [stt["tok_mt"]], sig=True)
                        stt["tok_sc"][sbi] = tmt
                        stt["tok_mt"] = tmt
                rows_toks = []
                for j in range(4):
                    for which in range(2 if sparse else 1):
                        bi, pb = self.bank()
                        ttr = self.transpose(pb[0:1, 0:128],
                                             self.mt[:, 2 * j + which:2 * j + which + 1],
                                             False, wait=[stt["tok_mt"], stt["tok_bc"]])
                        dst = (self.mrow if which == 0 else self.trow)
                        rt = p.do("dve",
                                  lambda E, o=dst[0:1, j * 128:(j + 1) * 128],
                                  i=pb[0:1, 0:128]: E.tensor_copy(out=o, in_=i),
                                  wait=[ttr], sig=True)
                        self.bank_done(bi, rt)
                        rows_toks.append(rt)
                bi, pb = self.bank()
                tb1 = self.mm3(pb[:], [], extra_f32=[(self.ones1[:], self.mrow[:])],
                               wait=rows_toks + self.tok_setup)
                eb1 = self.evac(pb[:], bc_m, wait=[tb1, stt["tok_tside"]])
                self.bank_done(bi, eb1)
                eb2 = None
                if sparse:
                    bi2, pb2 = self.bank()
                    tb2 = self.mm3(pb2[:], [], extra_f32=[(self.ones1[:], self.trow[:])],
                                   wait=rows_toks)
                    eb2 = self.evac(pb2[:], bc_t, wait=[tb2, stt["tok_tside"]])
                    self.bank_done(bi2, eb2)
                stt["tok_bc"] = eb2 if sparse else eb1
                t_e_prev = None
                t_w_prev = None
                split_prev = None
                for kt in range(NTT):
                    bi, pb = self.bank()
                    pairs = [(
                        kt_h[:, kt * 128:(kt + 1) * 128],
                        kt_h[:, S + kt * 128: S + (kt + 1) * 128],
                        qt_h[:, qb * TB:(qb + 1) * TB],
                        qt_h[:, S + qb * TB: S + (qb + 1) * TB])]
                    mtok = self.mm3(pb[:], pairs, wait=tqk)
                    last_qk_use = mtok
                    t_s = p.do("dve", lambda E, o=wq_s, a=pb[:], b=bc_m:
                               E.tensor_add(o, a, b),
                               wait=[mtok, eb1] + ([eb2] if eb2 else []) +
                               [t_e_prev, stt["tok_w"]], sig=True)
                    if sparse:
                        t_m = p.do("dve", lambda E, o=wq_m, a=pb[:], b=bc_t:
                                   E.tensor_tensor(o, a, b, ALU.is_ge),
                                   sig=True)
                        bank_rel = t_m
                    else:
                        bank_rel = t_s
                    self.bank_done(bi, bank_rel)
                    last_tside = bank_rel
                    t_e = p.do("act", lambda E, o=wq_e, i=wq_s:
                               E.activation(out=o, in_=i, func=AF.Exp),
                               wait=[t_s, t_w_prev if sparse else split_prev],
                               sig=True)
                    t_e_prev = t_e
                    if sparse:
                        t_w = p.do("dve", lambda E, o=wq_w, a=wq_e, b=wq_m:
                                   E.tensor_mul(o, a, b), wait=[t_e], sig=True)
                        t_w_prev = t_w
                        wsrc = wq_w
                        wtok = t_w
                    else:
                        wsrc = wq_e
                        wtok = t_e
                    split_prev = self.split(
                        wsrc, wT_h2[:, kt * TB:(kt + 1) * TB],
                        wT_l2[:, kt * TB:(kt + 1) * TB],
                        wait=[wtok, stt["tok_w"]])
                if self.stop_after == "p1b" and sparse and h == 4 and qb == 1:
                    p.dma("dq_s", ds["dbg_w"][0].rearrange("(t p) w -> p t w", p=128),
                          wT_h2.rearrange("p (t w) -> p t w", t=NTT),
                          wait=[split_prev])
                    p.dma("dq_s", ds["dbg_w"][1].rearrange("(t p) w -> p t w", p=128),
                          wT_l2.rearrange("p (t w) -> p t w", t=NTT))
                bi, pb = self.bank()
                pairs = []
                for kt in range(NTT):
                    pairs.append((
                        vh_b[hg][:, kt * VW65:(kt + 1) * VW65],
                        vl_b[hg][:, kt * VW65:(kt + 1) * VW65],
                        wT_h2[:, kt * TB:(kt + 1) * TB],
                        wT_l2[:, kt * TB:(kt + 1) * TB]))
                avtok = self.mm3(pb[0:65, :], pairs, wait=[split_prev] + tvv)
                stt["tok_w"] = avtok
                stt["tok_v"][hg] = avtok
                e_den = self.evac(pb[64:65, :], den_t[64:65, :],
                                  wait=[avtok, stt["tok_av"][0]])
                tr = p.do("dve", lambda E, o=rec_t[64:65, :], i=den_t[64:65, :]:
                          E.reciprocal(out=o, in_=i), wait=[e_den], sig=True)
                bi2, pb2 = self.bank()
                tb3 = self.mm3(pb2[0:64, :], [],
                               extra_f32=[(self.ones_sq[64:65, 0:64],
                                           rec_t[64:65, :])],
                               wait=[tr])
                erb = self.evac(pb2[0:64, :], rb,
                                wait=[tb3, stt["tok_av"][0], stt["tok_av"][1]])
                self.bank_done(bi2, erb)
                t_av = p.do("dve", lambda E, o=av_f, a=pb[0:64, :], b=rb:
                            E.tensor_mul(o, a, b),
                            wait=[erb, avtok], sig=True)
                self.bank_done(bi, t_av)
                sto = self.split(av_f, oT_h, oT_l, wait=[t_av])
                o1 = p.dma("dq_s", doh[h * 64:(h + 1) * 64, qb * TB:(qb + 1) * TB],
                           oT_h, wait=[sto])
                o2 = p.dma("dq_s", dol[h * 64:(h + 1) * 64, qb * TB:(qb + 1) * TB],
                           oT_l, wait=[sto])
                stt["tok_av"] = [t_av, o2]
                stt["tok_tside"] = last_tside
                stt["tok_oT"] = [o1, o2]
            stt["tok_qk"] = last_qk_use
        return stt["tok_oT"]

    def proj_residual_ln(self, which):
        p = self.p
        ds = self.ds
        xpre = self.fF(0, NC * S)
        res_f = self.fF(16384, TB)
        sq = self.fF(16896, TB)
        bc_nm = self.fF(17408, TB)
        bc_rs = self.fF(17920, TB)
        y = self.fF(18432, TB)
        hbuf = self.fF(18944, TB)
        x1rel = self.fF(19456, TB)
        sbuf2 = self.fF(19968, TB)
        sev = self.fF(20480, 128)
        oTs_hi = [self.fB(0, NC * TB), self.fB(8192, NC * TB)]
        oTs_lo = [self.fB(4096, NC * TB), self.fB(12288, NC * TB)]
        hst_h = self.fB(32768, TB)
        hst_l = self.fB(33280, TB)
        oTs_tok = [None, None]
        res_tok = getattr(self, "tok_presplit", None)
        if which == "attn1":
            wT_hi = self.fB(16384, 8192)
            wT_lo = self.fB(24576, 8192)
            t1 = p.dma("dq_s", wT_hi.rearrange("p (c i) -> p c i", c=NC),
                       ds["woThi"].rearrange("(c p) i -> p c i", p=128),
                       wait=self.tok_woT + self._ah_state["tok_oT"] +
                       [self._ah_state["tok_w"]])
            t2 = p.dma("dq_s", wT_lo.rearrange("p (c i) -> p c i", c=NC),
                       ds["woTlo"].rearrange("(c p) i -> p c i", p=128))
            wtok = [t1, t2]
            bias_tile = self.bias["bo"]
            res_dram = ds["x0T"]
        else:
            whi, wlo = self._wo_views
            wtok = self.wo_tok
            bias_tile = self.bias["b_o"]
            res_dram = ds["x1T"]

        sum_x, sum_x2, nmean, rstd = self.srow
        tpre = getattr(self, "tok_presplit", None)
        stat_tok = [tpre]
        prev_bc = [None]
        self.rr_n = 5
        ln_toks = []
        for tb in range(NTB):
            sg = tb % 2
            to1 = p.dma("dq_s", oTs_hi[sg].rearrange("p (c w) -> p c w", c=NC),
                        ds["oThi"].rearrange("(c p) s -> p c s", p=128)
                        [:, :, tb * TB:(tb + 1) * TB],
                        wait=[oTs_tok[sg]] + (self._ah_state["tok_oT"] if tb == 0 else []))
            to2 = p.dma("dq_s", oTs_lo[sg].rearrange("p (c w) -> p c w", c=NC),
                        ds["oTlo"].rearrange("(c p) s -> p c s", p=128)
                        [:, :, tb * TB:(tb + 1) * TB])
            sq_toks = []
            bi1, pbs1 = self.bank_fixed(5)
            bi2, pbs2 = self.bank_fixed(6)
            m1 = m2 = None
            for oc in range(NC):
                bi, pb = self.bank()
                pairs = []
                for dc in range(NC):
                    if which == "attn1":
                        wh = wT_hi[:, dc * 1024 + oc * 128: dc * 1024 + (oc + 1) * 128]
                        wl = wT_lo[:, dc * 1024 + oc * 128: dc * 1024 + (oc + 1) * 128]
                    else:
                        wh = whi[:, dc, oc * 128:(oc + 1) * 128]
                        wl = wlo[:, dc, oc * 128:(oc + 1) * 128]
                    pairs.append((
                        wh, wl,
                        oTs_hi[sg][:, dc * TB:(dc + 1) * TB],
                        oTs_lo[sg][:, dc * TB:(dc + 1) * TB]))
                mtok = self.mm3(pb[:], pairs, wait=list(wtok) + [to1, to2])
                oTs_tok[sg] = mtok
                rt = p.dma("dq_s", res_f,
                           res_dram[oc * 128:(oc + 1) * 128, tb * TB:(tb + 1) * TB],
                           wait=[res_tok])
                psg = (tb * NC + oc) % 2
                fsl = self.prjf[:, psg * TB:(psg + 1) * TB]
                et = self.evac(pb[:], fsl, bias=bias_tile[:, oc:oc + 1],
                               wait=[mtok, getattr(self, "_prl_tok", None)] + self.tok_const)
                self.bank_done(bi, et)
                xsl = xpre[:, oc * S + tb * TB: oc * S + tb * TB + TB]
                ta = p.do("dve", lambda E, o=xsl, a=fsl, b=res_f:
                          E.tensor_add(o, a, b), wait=[et, rt], sig=True)
                res_tok = ta
                self._prl_tok = ta
                tsq = p.do("act", lambda E, o=sq, i=xsl:
                           E.activation(out=o, in_=i, func=AF.Square),
                           wait=[ta, stat_tok[0]], sig=True)
                st_ = (oc == 0)
                sp_ = (oc == NC - 1)
                m1 = p.do("pe", lambda E, o=pbs1[0:1, :], a=self.ones_sq[0:128, 0:1],
                          b=xsl, st=st_, sp=sp_:
                          E.matmul(o, a, b, start=st, stop=sp),
                          wait=[ta], sig=sp_)
                m2 = p.do("pe", lambda E, o=pbs2[0:1, :], a=self.ones_sq[0:128, 0:1],
                          b=sq, st=st_, sp=sp_:
                          E.matmul(o, a, b, start=st, stop=sp),
                          wait=[tsq], sig=sp_)
                sq_toks.append(tsq)
            e1 = self.evac(pbs1[0:1, :], sum_x[:], wait=[m1])
            e2 = self.evac(pbs2[0:1, :], sum_x2[:], wait=[m2])
            self.bank_done(bi1, e1)
            self.bank_done(bi2, e2)
            t1_ = p.do("dve", lambda E: E.tensor_scalar_mul(nmean[:], sum_x[:],
                                                            -1.0 / 1024.0),
                       wait=[e1, prev_bc[0]], sig=True)
            t2_ = p.do("dve", lambda E: E.tensor_tensor(rstd[:], nmean[:], nmean[:],
                                                        ALU.mult),
                       wait=[t1_], sig=True)
            t3_ = p.do("dve", lambda E: E.scalar_tensor_tensor(
                rstd[:], sum_x2[:], 1.0 / 1024.0, rstd[:],
                ALU.mult, ALU.subtract), wait=[e2, t2_], sig=True)
            t3b = p.do("dve", lambda E: E.tensor_scalar_add(rstd[:], rstd[:], EPS),
                       wait=[t3_], sig=True)
            t4_ = p.do("act", lambda E: E.activation(out=rstd[:], in_=rstd[:],
                                                     func=AF.Sqrt),
                       wait=[t3b], sig=True)
            t5_ = p.do("dve", lambda E: E.reciprocal(out=rstd[:], in_=rstd[:]),
                       wait=[t4_], sig=True)
            bim, pbm = self.bank()
            tbm = self.mm3(pbm[:], [], extra_f32=[(self.ones1[:], nmean[:])],
                           wait=[t1_, t2_])
            ebm = self.evac(pbm[:], bc_nm, wait=[tbm, stat_tok[0]])
            self.bank_done(bim, ebm)
            bir, pbr = self.bank()
            tbr = self.mm3(pbr[:], [], extra_f32=[(self.ones1[:], rstd[:])],
                           wait=[t5_])
            ebr = self.evac(pbr[:], bc_rs, wait=[tbr, stat_tok[0]])
            self.bank_done(bir, ebr)
            prev_bc[0] = tbr
            ln_toks = []
            for oc in range(NC):
                xsl = xpre[:, oc * S + tb * TB: oc * S + tb * TB + TB]
                ta1 = p.do("dve", lambda E, o=y, a=xsl, b=bc_nm:
                           E.tensor_add(o, a, b),
                           wait=[ebm, ebr, getattr(self, "_ln_tok", None),
                                 getattr(self, "_y_spill", None)] + sq_toks,
                           sig=True)
                ta2 = p.do("dve", lambda E, o=y, a=y, b=bc_rs:
                           E.tensor_mul(o, a, b), sig=True)
                self._ln_tok = ta2
                if which == "attn1":
                    s1 = p.dma("dq_s",
                               ds["x1T"][oc * 128:(oc + 1) * 128, tb * TB:(tb + 1) * TB],
                               y, wait=[ta2])
                    self._y_spill = s1
                    g = self.bias["ln1_g"][:, oc:oc + 1]
                    b = self.bias["ln1_b"][:, oc:oc + 1]
                    dh, dl = ds["h1hi"], ds["h1lo"]
                else:
                    s1 = None
                    g = self.bias["ln2_g"][:, oc:oc + 1]
                    b = self.bias["ln2_b"][:, oc:oc + 1]
                    dh, dl = ds["h2hi"], ds["h2lo"]
                th = p.do("dve", lambda E, o=hbuf, a=y, g=g, b=b:
                          E.tensor_scalar(o, a, g, b, ALU.mult, ALU.add),
                          wait=[ta2] + self.tok_const, sig=True)
                stok = self.split(hbuf, hst_h, hst_l,
                                  wait=[th, getattr(self, "_h_spill", None)])
                sh1 = p.dma("dq_s",
                            dh[oc * 128:(oc + 1) * 128, tb * TB:(tb + 1) * TB],
                            hst_h, wait=[stok])
                sh2 = p.dma("dq_s",
                            dl[oc * 128:(oc + 1) * 128, tb * TB:(tb + 1) * TB],
                            hst_l, wait=[stok])
                self._h_spill = sh2
                ln_toks.append(sh2)
                if s1 is not None:
                    ln_toks.append(s1)
                if which == "attn2":
                    rt2 = p.dma("dq_s", x1rel,
                                ds["x1T"][oc * 128:(oc + 1) * 128, tb * TB:(tb + 1) * TB],
                                wait=[getattr(self, "_s_tok", None)])
                    ts_ = p.do("dve", lambda E, o=sbuf2, a=xsl, b=x1rel:
                               E.tensor_add(o, a, b),
                               wait=[rt2, getattr(self, "_s_tt", None)] + sq_toks,
                               sig=True)
                    for j in range(4):
                        bi, pb = self.bank()
                        ttok = self.transpose(pb[0:128, 0:128],
                                              sbuf2[:, j * 128:(j + 1) * 128],
                                              False, wait=[ts_])
                        es = self.evac(pb[0:128, 0:128], sev,
                                       wait=[ttok, getattr(self, "_s_ev", None)])
                        self.bank_done(bi, es)
                        sspill = p.dma(
                            "dq_s",
                            ds["s_tok"][tb * TB + j * 128: tb * TB + (j + 1) * 128,
                                        oc * 128:(oc + 1) * 128],
                            sev, wait=[es])
                        self._s_ev = sspill
                        self._s_tt = ttok
                    self._s_tok = sspill
                    ln_toks.append(sspill)
            stat_tok[0] = ln_toks[-1] if ln_toks else None
        self.rr_n = 7
        self._prl_done = ln_toks

    def load_w_panels(self, dram_ap, c0, width, slot, extra_wait=()):
        p = self.p
        whi = self.fB(16384 + slot * 4096, NC * width).rearrange(
            "p (c w) -> p c w", c=NC)
        wlo = self.fB(24576 + slot * 4096, NC * width).rearrange(
            "p (c w) -> p c w", c=NC)
        src = dram_ap[:, c0:c0 + width].rearrange("(c p) w -> p c w", p=128)
        t_h = p.dma("dq_g", whi, src,
                    wait=[self.wslot_tok[slot]] + list(extra_wait))
        toks = []
        hw = width // 2
        for half in range(2):
            srch = dram_ap[:, c0 + half * hw:c0 + (half + 1) * hw] \
                .rearrange("(c p) w -> p c w", p=128)
            fstg = self.fF(0, NC * hw).rearrange("p (c w) -> p c w", c=NC)
            t_f = p.dma("dq_s", fstg, srch,
                        wait=[getattr(self, "_wpf_tok", None)])
            sl_hi = whi[:, :, half * hw:(half + 1) * hw]
            sl_lo = wlo[:, :, half * hw:(half + 1) * hw]
            t_l = p.do("dve",
                       lambda E, o=sl_lo, a=fstg, b=sl_hi: E.tensor_sub(o, a, b),
                       wait=[t_f, t_h], sig=True)
            self._wpf_tok = t_l
            toks.append(t_l)
        return whi, wlo, toks

    def p2a(self):
        p = self.p
        ds = self.ds
        t0 = self._prl_done[-1] if getattr(self, "_prl_done", None) else None
        self.wslot_tok = [t0, t0]
        self.xs_tok = [t0, t0]
        self.prj_tok = [None, None]
        for (part, dsthi, dstlo, btile, sc_) in (
                (0, ds["qhi"], ds["qlo"], self.bqkv8, 0.125),
                (1, ds["khi"], ds["klo"], None, 1.0)):
            for oc2 in range(2):
                c0 = part * D + oc2 * 512
                slot = oc2 % 2
                whi, wlo, wt = self.load_w_panels(
                    self.din["w_qkv"], c0, 512, slot,
                    extra_wait=[getattr(self, "tok_presplit", None)])
                for tb in range(NTB):
                    sg = tb % 2
                    ht = self.x_block(ds["h1hi"], ds["h1lo"], tb, sg)
                    for oq in range(4):
                        oc = oc2 * 4 + oq
                        bi, pb = self.bank()
                        pairs = []
                        for dc in range(NC):
                            pairs.append((
                                whi[:, dc, oq * 128:(oq + 1) * 128],
                                wlo[:, dc, oq * 128:(oq + 1) * 128],
                                self.xs_hi[sg][:, dc * TB:(dc + 1) * TB],
                                self.xs_lo[sg][:, dc * TB:(dc + 1) * TB]))
                        mtok = self.mm3(pb[:], pairs, wait=wt + ht)
                        self.xs_tok[sg] = mtok
                        self.wslot_tok[slot] = mtok
                        psg = (tb * 4 + oq) % 2
                        fsl = self.prjf[:, psg * TB:(psg + 1) * TB]
                        if part == 0:
                            bslice = self.bqkv8[:, oc:oc + 1]
                        else:
                            bslice = self.bias["b_qkv"][:, NC + oc:NC + oc + 1]
                        et = self.evac(pb[:], fsl, bias=bslice, scale=sc_,
                                       wait=[mtok, self.prj_tok[psg]] + self.tok_const)
                        self.bank_done(bi, et)
                        stok = self.split(fsl, self.prjh[:, psg * TB:(psg + 1) * TB],
                                          self.prjl[:, psg * TB:(psg + 1) * TB],
                                          wait=[et])
                        p.dma("dq_s",
                              dsthi[oc * 128:(oc + 1) * 128, tb * TB:(tb + 1) * TB],
                              self.prjh[:, psg * TB:(psg + 1) * TB], wait=[stok])
                        t2 = p.dma("dq_s",
                                   dstlo[oc * 128:(oc + 1) * 128, tb * TB:(tb + 1) * TB],
                                   self.prjl[:, psg * TB:(psg + 1) * TB], wait=[stok])
                        self.prj_tok[psg] = t2
        bvr = self.fF(2048, D)[0:1, :]
        tbv = p.dma("dq_s", bvr, self.din["b_qkv"][:, 2 * D:3 * D],
                    wait=[getattr(self, "_wpf_tok", None)])
        self._wpf_tok = tbv
        w_views = [None, None]
        wt_all = []
        for ib in range(2):
            whi, wlo, wt = self.load_w_panels(self.din["w_qkv"], 2 * D + ib * 512,
                                              512, ib)
            w_views[ib] = (whi, wlo)
            wt_all += wt
        self._vtok(ds["h1hi"], ds["h1lo"],
                   [w_views[0][0], w_views[1][0]],
                   [w_views[0][1], w_views[1][1]],
                   wt_all, bvr, [tbv], trans_w=False)
        self._ah_state["tok_qk"] = self.xs_tok[0]
        self._ah_state["tok_qk"] = self.xs_tok[1]

    def load_wo(self):
        p = self.p
        whi = self.fB(16384, 8192).rearrange("p (c i) -> p c i", c=NC)
        wlo = self.fB(24576, 8192).rearrange("p (c i) -> p c i", c=NC)
        src = self.din["w_o"].rearrange("(c p) i -> p c i", p=128)
        t_h = p.dma("dq_g", whi, src, wait=self._ah_state["tok_oT"] +
                    [self._ah_state["tok_w"], self._ah_state["tok_qk"]])
        toks = []
        for qtr in range(4):
            srch = self.din["w_o"][:, qtr * 256:(qtr + 1) * 256] \
                .rearrange("(c p) w -> p c w", p=128)
            fstg = self.fF(0, NC * 256).rearrange("p (c w) -> p c w", c=NC)
            t_f = p.dma("dq_s", fstg, srch,
                        wait=[getattr(self, "_wpf_tok", None),
                              self._ah_state["tok_sc"][0],
                              self._ah_state["tok_sc"][1],
                              self._ah_state["tok_sc"][2]])
            sl_hi = whi[:, :, qtr * 256:(qtr + 1) * 256]
            sl_lo = wlo[:, :, qtr * 256:(qtr + 1) * 256]
            t_l = p.do("dve",
                       lambda E, o=sl_lo, a=fstg, b=sl_hi: E.tensor_sub(o, a, b),
                       wait=[t_f, t_h], sig=True)
            self._wpf_tok = t_l
            toks.append(t_l)
        self._wo_views = (whi, wlo)
        self.wo_tok = toks[-1:]

    def p3_gating(self):
        p = self.p
        ds = self.ds
        gw_hi = self.fB(2048, NC * NE)
        gw_lo = self.fB(2112, NC * NE)
        h2t_hi = self.fB(0, NC * TT)
        h2t_lo = self.fB(1024, NC * TT)
        gsrc = self.din["gate_w"].rearrange("(c p) e -> p c e", p=128)
        t_prev = self._prl_done[-1] if getattr(self, "_prl_done", None) else None
        tgh = p.dma("dq_g", gw_hi.rearrange("p (c e) -> p c e", c=NC), gsrc,
                    wait=[t_prev], slow_ok=True)
        tgf = p.dma("dq_s", self.fF(0, NC * NE).rearrange("p (c e) -> p c e", c=NC),
                    gsrc, wait=[getattr(self, "_wpf_tok", None), t_prev],
                    slow_ok=True)
        tgl = p.do("dve", lambda E: E.tensor_sub(gw_lo, self.fF(0, NC * NE), gw_hi),
                   wait=[tgh, tgf], sig=True)
        htok = tgl
        ew4 = self.ew.rearrange("p (t e) -> p t e", t=NTT)
        gl4 = self.glog.rearrange("p (t e) -> p t e", t=NTT)
        prev_gate = None
        prev_te = None
        prev_tw8 = None
        for tt in range(NTT):
            t1 = p.dma("dq_s", h2t_hi.rearrange("p (c w) -> p c w", c=NC),
                       ds["h2hi"].rearrange("(c p) s -> p c s", p=128)
                       [:, :, tt * TT:(tt + 1) * TT],
                       wait=[htok])
            t2 = p.dma("dq_s", h2t_lo.rearrange("p (c w) -> p c w", c=NC),
                       ds["h2lo"].rearrange("(c p) s -> p c s", p=128)
                       [:, :, tt * TT:(tt + 1) * TT])
            bi, pb = self.bank()
            pairs = []
            for dc in range(NC):
                pairs.append((
                    h2t_hi.rearrange("p (c w) -> p c w", c=NC)[:, dc, :],
                    h2t_lo.rearrange("p (c w) -> p c w", c=NC)[:, dc, :],
                    gw_hi.rearrange("p (c e) -> p c e", c=NC)[:, dc, :],
                    gw_lo.rearrange("p (c e) -> p c e", c=NC)[:, dc, :]))
            mtok = self.mm3(pb[0:128, 0:NE], pairs,
                            extra_f32=[(self.ones1[0:1, 0:128], self.gate_b_row[:])],
                            wait=[t1, t2, tgl] + self.tok_const)
            htok = mtok
            et = self.evac(pb[0:128, 0:NE], gl4[:, tt, :], wait=[mtok, prev_gate])
            self.bank_done(bi, et)
            tok_gmax = p.do("dve", lambda E, o=self.gsc[:, 0:8], i=gl4[:, tt, :]:
                            E.max(out=o, in_=i), wait=[et], sig=True)
            t_n = p.do("dve", lambda E: E.tensor_scalar_mul(
                self.gsc[:, 8:9], self.gsc[:, 0:1], -1.0),
                wait=[prev_te, tok_gmax], sig=True)
            t_e = p.do("act", lambda E, g=gl4[:, tt, :]: E.activation(
                out=self.gsc[:, 16:24], in_=g, func=AF.Exp,
                bias=self.gsc[:, 8:9]), wait=[t_n, prev_tw8], sig=True)
            prev_te = t_e
            t_m8 = p.do("dve", lambda E, g=gl4[:, tt, :], w=ew4[:, tt, :]:
                E.tensor_tensor(w, g, self.gsc[:, 1:2].to_broadcast([128, NE]),
                                ALU.is_ge),
                wait=[t_e], sig=True)
            t_w8 = p.do("dve", lambda E, w=ew4[:, tt, :]: E.tensor_tensor(
                w, w, self.gsc[:, 16:24], ALU.mult), wait=[t_m8], sig=True)
            prev_tw8 = t_w8
            if "dbg_g2" in self.debug_outputs:
                p.dma("dq_s", ds["dbg_g2"][tt, 0], self.gsc[:, 16:24], wait=[t_e])
                p.dma("dq_s", ds["dbg_g2"][tt, 1], ew4[:, tt, :], wait=[t_m8])
                p.dma("dq_s", ds["dbg_g2"][tt, 2], ew4[:, tt, :], wait=[t_w8])
            t_su = p.do("dve", lambda E, w=ew4[:, tt, :]: E.tensor_reduce(
                self.gsc[:, 10:11], w, AXX, ALU.add), wait=[t_w8], sig=True)
            t_r = p.do("dve", lambda E: E.reciprocal(
                out=self.gsc[:, 11:12], in_=self.gsc[:, 10:11]), wait=[t_su],
                sig=True)
            prev_gate = p.do("dve", lambda E, w=ew4[:, tt, :]:
                E.tensor_tensor(w, w, self.gsc[:, 11:12].to_broadcast([128, NE]),
                                ALU.mult),
                wait=[t_r], sig=True)
            if "dbg_g2" in self.debug_outputs:
                p.dma("dq_s", ds["dbg_g2"][tt, 3], ew4[:, tt, :], wait=[prev_gate])
        self.tok_gate = prev_gate
        self.ewT = self.fF(16384, 2048, p=NE)
        self.b2sb = self.fF(18432, D, p=NE)
        ew_toks = []
        for tt in range(NTT):
            bi, pb = self.bank()
            ttok = self.transpose(pb[0:NE, 0:128], ew4[:, tt, :], False,
                                  wait=[self.tok_gate, self._ln_tok])
            et = p.do("dve", lambda E, o=self.ewT[:, tt * TT:(tt + 1) * TT],
                      i=pb[0:NE, 0:128]: E.tensor_copy(out=o, in_=i),
                      wait=[ttok], sig=True)
            self.bank_done(bi, et)
            ew_toks.append(et)
        self.tok_ewT = ew_toks[-1:]
        if "dbg_gate" in self.debug_outputs:
            p.dma("dq_s", ds["dbg_gate"][0], self.glog[:], wait=[self.tok_gate])
            p.dma("dq_s", ds["dbg_gate"][1], self.ew[:], wait=[self.tok_gate])

    def presplit_moe_weights(self):
        ds = self.ds
        tok = self.wpan_tok[1] if hasattr(self, "wpan_tok") else None
        for e in range(NE):
            for rc in range(NC):
                for chalf in range(2):
                    src = self.din["W1"][e][rc * 128:(rc + 1) * 128,
                                            chalf * 2048:(chalf + 1) * 2048]
                    dst_h = ds["W1hi"][e][rc * 128:(rc + 1) * 128,
                                          chalf * 2048:(chalf + 1) * 2048]
                    dst_l = ds["W1lo"][e][rc * 128:(rc + 1) * 128,
                                          chalf * 2048:(chalf + 1) * 2048]
                    tok = self._presplit_panel(src, dst_h, dst_l, tok)
        for e in range(NE):
            for rc in range(DFF // 128):
                src = self.din["W2"][e][rc * 128:(rc + 1) * 128, :]
                dst_h = ds["W2hi"][e][rc * 128:(rc + 1) * 128, :]
                dst_l = ds["W2lo"][e][rc * 128:(rc + 1) * 128, :]
                tok = self._presplit_panel(src, dst_h, dst_l, tok)
        self.tok_presplit = tok
        self._wpf_tok = tok

    def _presplit_panel(self, src, dst_h, dst_l, tok):
        p = self.p
        w = src.shape[-1]
        pf = self.fF(16384, w)
        ph = self.fB(36864, w)
        pl = self.fB(38912, w)
        w0 = self.wpan_tok if hasattr(self, "wpan_tok") else [None, None]
        t_f = p.dma("dq_g", pf, src, wait=[tok, w0[0], w0[1]])
        t_h = p.dma("dq_g", ph, src)
        t_l = p.do("gp", lambda E: E.tensor_tensor(pl, pf, ph, ALU.subtract),
                   wait=[t_f, t_h], sig=True)
        p.dma("dq_g", dst_h, ph, wait=[t_l])
        t2 = p.dma("dq_g", dst_l, pl, wait=[t_l])
        return t2

    def p5_moe(self):
        p = self.p
        ds = self.ds
        acc = self.fF(0, NTT * D)
        acc4 = acc.rearrange("p (t d) -> p t d", t=NTT)
        tb2 = p.dma("dq_s", self.b2sb, self.din["b2"],
                    wait=[getattr(self, "_s_tok", None)])
        init_toks = []
        for tt in range(NTT):
            for db in range(2):
                bi, pb = self.bank()
                mtok = self.mm3(pb[:], [],
                                extra_f32=[(self.ewT[:, tt * TT:(tt + 1) * TT],
                                            self.b2sb[:, db * 512:(db + 1) * 512])],
                                wait=[tb2] + self.tok_ewT)
                et = p.do("dve", lambda E, o=acc4[:, tt, db * 512:(db + 1) * 512],
                          i=pb[:]: E.tensor_copy(out=o, in_=i),
                          wait=[mtok, self._ln_tok, getattr(self, "_s_tok", None)],
                          sig=True)
                self.bank_done(bi, et)
                init_toks.append(et)
        acc_tok = init_toks[-1]

        W1h = self.fB(0, NC * 512).rearrange("p (c w) -> p c w", c=NC)
        W1l = self.fB(4096, NC * 512).rearrange("p (c w) -> p c w", c=NC)
        W2 = [
            (self.fB(8192, 4 * 1024).rearrange("p (c w) -> p c w", c=4),
             self.fB(12288, 4 * 1024).rearrange("p (c w) -> p c w", c=4)),
            (self.fB(16384, 4 * 1024).rearrange("p (c w) -> p c w", c=4),
             self.fB(20480, 4 * 1024).rearrange("p (c w) -> p c w", c=4)),
        ]
        hid_h = [self.fB(24576, 4 * 512).rearrange("p (c w) -> p c w", c=4),
                 self.fB(28672, 4 * 512).rearrange("p (c w) -> p c w", c=4)]
        hid_l = [self.fB(26624, 4 * 512).rearrange("p (c w) -> p c w", c=4),
                 self.fB(30720, 4 * 512).rearrange("p (c w) -> p c w", c=4)]
        h2h = self.fB(32768, NC * 512).rearrange("p (c w) -> p c w", c=NC)
        h2l = self.fB(36864, NC * 512).rearrange("p (c w) -> p c w", c=NC)
        b1s = self.bias["b1"].rearrange("p (e c) -> p e c", e=NE)

        w1_tok = self._ah_state["tok_qk"]
        w2_tok = [w1_tok, w1_tok]
        h2_tok = self.tok_presplit
        hid_tok = [None, None]
        for e in range(NE):
            for pair in range(4):
                wt2 = []
                for half in range(2):
                    f8 = 2 * pair + half
                    t_a = p.dma("dq_s", W2[half][0],
                                ds["W2hi"][e].rearrange("(c p) d -> p c d", p=128)
                                [:, f8 * 4:(f8 + 1) * 4, :],
                                wait=[w2_tok[half], self.tok_presplit])
                    t_b = p.dma("dq_s", W2[half][1],
                                ds["W2lo"][e].rearrange("(c p) d -> p c d", p=128)
                                [:, f8 * 4:(f8 + 1) * 4, :])
                    wt2 += [t_a, t_b]
                for tb in range(NTB):
                    t_h2a = p.dma("dq_s", h2h,
                                  ds["h2hi"].rearrange("(c p) s -> p c s", p=128)
                                  [:, :, tb * TB:(tb + 1) * TB], wait=[h2_tok])
                    t_h2b = p.dma("dq_s", h2l,
                                  ds["h2lo"].rearrange("(c p) s -> p c s", p=128)
                                  [:, :, tb * TB:(tb + 1) * TB])
                    ht = [t_h2a, t_h2b]
                    for half in range(2):
                        f8 = 2 * pair + half
                        t_w1h = p.dma("dq_s", W1h,
                                      ds["W1hi"][e].rearrange("(c p) f -> p c f", p=128)
                                      [:, :, f8 * 512:(f8 + 1) * 512],
                                      wait=[w1_tok, self.tok_presplit])
                        t_w1l = p.dma("dq_s", W1l,
                                      ds["W1lo"][e].rearrange("(c p) f -> p c f", p=128)
                                      [:, :, f8 * 512:(f8 + 1) * 512])
                        wt1 = [t_w1h, t_w1l]
                        for ft in range(4):
                            bi, pb = self.bank()
                            pairs = []
                            for dc in range(NC):
                                pairs.append((
                                    W1h[:, dc, ft * 128:(ft + 1) * 128],
                                    W1l[:, dc, ft * 128:(ft + 1) * 128],
                                    h2h[:, dc, :], h2l[:, dc, :]))
                            mtok = self.mm3(pb[:], pairs, wait=wt1 + ht)
                            h2_tok = mtok
                            w1_tok = mtok
                            bap = b1s[:, e, f8 * 4 + ft: f8 * 4 + ft + 1]
                            fsl = self.prjf[:, (ft % 2) * TB:(ft % 2 + 1) * TB]
                            et = self.evac(pb[:], fsl, func=AF.Relu, bias=bap,
                                           wait=[mtok, hid_tok[half]] + self.tok_const)
                            self.bank_done(bi, et)
                            stok = self.split(fsl, hid_h[half][:, ft, :],
                                              hid_l[half][:, ft, :], wait=[et])
                            hid_tok[half] = stok
                    for ttj in range(4):
                        for db in range(2):
                            bi, pb = self.bank()
                            pairs = []
                            for half in range(2):
                                for fc in range(4):
                                    pairs.append((
                                        hid_h[half][:, fc, ttj * 128:(ttj + 1) * 128],
                                        hid_l[half][:, fc, ttj * 128:(ttj + 1) * 128],
                                        W2[half][0][:, fc, db * 512:(db + 1) * 512],
                                        W2[half][1][:, fc, db * 512:(db + 1) * 512]))
                            mt2 = self.mm3(pb[:], pairs,
                                           wait=[hid_tok[0], hid_tok[1]] + wt2)
                            w2_tok = [mt2, mt2]
                            hid_tok = [mt2, mt2]
                            ewc = self.ew.rearrange("p (t e) -> p t e", t=NTT) \
                                [:, tb * 4 + ttj, e:e + 1]
                            at = p.do("dve", lambda E,
                                      o=acc4[:, tb * 4 + ttj, db * 512:(db + 1) * 512],
                                      i=pb[:], s=ewc:
                                      E.scalar_tensor_tensor(
                                          o, i, s, o, ALU.mult, ALU.add),
                                      wait=[mt2, acc_tok, self.tok_gate], sig=True)
                            self.bank_done(bi, at)
                            acc_tok = at
        self.tok_moe = acc_tok
        if "dbg_acc" in self.debug_outputs:
            p.dma("dq_s", ds["dbg_acc"].rearrange("(t p) d -> p t d", p=128),
                  acc4, wait=[acc_tok])

    def p6_final(self):
        p = self.p
        ds = self.ds
        acc4 = self.fF(0, NTT * D).rearrange("p (t d) -> p t d", t=NTT)
        y = self.fF(16384, D)
        sqv = self.fF(17408, D)
        rel = self.fF(18432, D)
        st_tok = None
        out_toks = []
        for tt in range(NTT):
            rt = p.dma("dq_s", rel, ds["s_tok"][tt * TT:(tt + 1) * TT, :],
                       wait=[st_tok, self.tok_moe])
            t_y = p.do("dve", lambda E, o=y, a=acc4[:, tt, :], b=rel:
                       E.tensor_add(o, a, b), wait=[rt, self.tok_moe], sig=True)
            t_m = p.do("dve", lambda E: E.tensor_reduce(
                self.gsc[:, 0:1], y, AXX, ALU.add), wait=[t_y], sig=True)
            t_nm = p.do("dve", lambda E: E.tensor_scalar_mul(
                self.gsc[:, 1:2], self.gsc[:, 0:1], -1.0 / 1024.0), wait=[t_m],
                sig=True)
            t_sq = p.do("act", lambda E, o=sqv, i=y:
                        E.activation(out=o, in_=i, func=AF.Square),
                        wait=[t_y, st_tok], sig=True)
            t_s2 = p.do("dve", lambda E: E.tensor_reduce(
                self.gsc[:, 2:3], sqv, AXX, ALU.add), wait=[t_sq], sig=True)
            t_v1 = p.do("dve", lambda E: E.tensor_tensor(
                self.gsc[:, 3:4], self.gsc[:, 1:2], self.gsc[:, 1:2], ALU.mult),
                wait=[t_nm], sig=True)
            t_v2 = p.do("dve", lambda E: E.scalar_tensor_tensor(
                self.gsc[:, 4:5], self.gsc[:, 2:3], 1.0 / 1024.0, self.gsc[:, 3:4],
                ALU.mult, ALU.subtract), wait=[t_s2, t_v1], sig=True)
            t_v3 = p.do("dve", lambda E: E.tensor_scalar_add(
                self.gsc[:, 4:5], self.gsc[:, 4:5], EPS), wait=[t_v2], sig=True)
            t_sqr = p.do("act", lambda E: E.activation(
                out=self.gsc[:, 5:6], in_=self.gsc[:, 4:5], func=AF.Sqrt),
                wait=[t_v3], sig=True)
            t_r = p.do("dve", lambda E: E.reciprocal(
                out=self.gsc[:, 6:7], in_=self.gsc[:, 5:6]), wait=[t_sqr], sig=True)
            t_o = p.do("dve", lambda E, o=rel, a=y:
                       E.tensor_scalar(o, a, self.gsc[:, 1:2], self.gsc[:, 6:7],
                                       ALU.add, ALU.mult),
                       wait=[t_r], sig=True)
            st_tok = p.dma("dq_s", self.dout[tt * TT:(tt + 1) * TT, :], rel,
                           wait=[t_o])
            out_toks.append(st_tok)
        p._wait("sync", out_toks)

    def build(self):
        self.setup()
        self.p0_input()
        if self.stop_after == "p0":
            self.p.finalize()
            return self.nc
        self.p1a()
        if self.stop_after == "p1a":
            self.p.finalize()
            return self.nc
        self.presplit_moe_weights()
        self.attn_heads(True)
        if self.stop_after == "p1b":
            self.p.finalize()
            return self.nc
        self.proj_residual_ln("attn1")
        if self.stop_after == "p1c":
            self.p.finalize()
            return self.nc
        self.p2a()
        if self.stop_after == "p2a":
            self.p.finalize()
            return self.nc
        self.attn_heads(False)
        if self.stop_after == "p2b":
            self.p.finalize()
            return self.nc
        self.load_wo()
        self.proj_residual_ln("attn2")
        if self.stop_after == "p2c":
            self.p.finalize()
            return self.nc
        self.p3_gating()
        if self.stop_after == "p3":
            self.p.finalize()
            return self.nc
        self.p5_moe()
        if self.stop_after == "p5":
            self.p.finalize()
            return self.nc
        self.p6_final()
        import sys
        print(f"[kernel-build] sbuf_bytes_remaining={self.nc.sbuf_bytes_remaining}",
              file=sys.stderr)
        self.p.finalize()
        return self.nc


_CACHED = {}


def _get_nc(stop_after=None):
    key = stop_after
    if key not in _CACHED:
        _CACHED[key] = Builder(stop_after=stop_after).build()
    return _CACHED[key]


def kernel(**inputs):
    nc = _get_nc()
    x = np.ascontiguousarray(np.asarray(inputs["L0_attn_x"], dtype=np.float32))
    wmap = {
        "Wq": "L0_attn_Wq", "bq": "L0_attn_bq", "Wk": "L0_attn_Wk",
        "bk": "L0_attn_bk", "Wv": "L0_attn_Wv", "bv": "L0_attn_bv",
        "Wo": "L0_attn_Wo", "bo": "L0_attn_bo",
        "ln1_g": "L0_ffn_ln1_g", "ln1_b": "L0_ffn_ln1_b",
        "w_qkv": "L0_ffn_w_qkv", "b_qkv": "L0_ffn_b_qkv",
        "w_o": "L0_ffn_w_o", "b_o": "L0_ffn_b_o",
        "ln2_g": "L0_ffn_ln2_g", "ln2_b": "L0_ffn_ln2_b",
        "gate_w": "L0_ffn_gate_w", "gate_b": "L0_ffn_gate_b",
        "W1": "L0_ffn_W1", "b1": "L0_ffn_b1",
        "W2": "L0_ffn_W2", "b2": "L0_ffn_b2",
    }
    base = {}
    for k, srcn in wmap.items():
        a = np.ascontiguousarray(np.asarray(inputs[srcn], dtype=np.float32))
        if a.ndim == 1:
            a = a.reshape(1, -1)
        base[k] = a
    in_maps = []
    for b in range(8):
        m = {"x": np.ascontiguousarray(x[b])}
        m.update(base)
        in_maps.append(m)
    res = run_bass_kernel_spmd(nc, in_maps, core_ids=list(range(8)))
    return np.stack([res.results[b]["out"] for b in range(8)], axis=0)


# version 08
# speedup vs baseline: 1.1302x; 1.0122x over previous
import numpy as np
import concourse.bass as bass
import concourse.mybir as mybir
from concourse.bass_utils import run_bass_kernel_spmd

F32 = mybir.dt.float32
BF16 = mybir.dt.bfloat16
AF = mybir.ActivationFunctionType
ALU = mybir.AluOpType
AXX = mybir.AxisListType.X

S = 2048
D = 1024
H = 16
HD = 64
DFF = 4096
NE = 8
NC = 8
TB = 512
NTB = S // TB
TT = 128
NTT = S // TT
NEG = -30000.0
EPS = 1e-5
VW65 = 65

ENGINES = ("pe", "dve", "act", "gp", "sync")
DQ = {"dq_s": "sync", "dq_g": "gp", "dq_a": "act"}


class P:

    def __init__(self, nc):
        self.nc = nc
        self.q = {e: [] for e in ENGINES}
        self.sem = {}
        self.cnt = {}
        for s in ENGINES + tuple(DQ):
            self.sem[s] = nc.alloc_semaphore(f"S_{s}")
            self.cnt[s] = 0
        self.waited = {e: {s: 0 for s in self.sem} for e in ENGINES}

    def _wait(self, eng, toks):
        for t in toks:
            if t is None:
                continue
            sname, v = t
            if v > self.waited[eng][sname]:
                self.waited[eng][sname] = v
                sem = self.sem[sname]
                self.q[eng].append(lambda E, sem=sem, v=v: E.wait_ge(sem, v))

    def do(self, eng, fn, wait=(), sig=False):
        self._wait(eng, wait)
        if sig:
            self.cnt[eng] += 1
            v = self.cnt[eng]
            sem = self.sem[eng]
            self.q[eng].append(lambda E, fn=fn, sem=sem: fn(E).then_inc(sem, 1))
            return (eng, v)
        self.q[eng].append(fn)
        return None

    def dma(self, qn, out, in_, wait=(), slow_ok=False):
        eng = DQ[qn]
        self._wait(eng, wait)
        self.cnt[qn] += 16
        v = self.cnt[qn]
        sem = self.sem[qn]
        self.q[eng].append(
            lambda E, out=out, in_=in_, sem=sem, so=slow_ok:
            E.dma_start(out=out, in_=in_, allow_slow_non_contiguous=so)
            .then_inc(sem, 16))
        return (qn, v)

    def drain(self, eng):
        self.q[eng].append(lambda E: E.drain())

    def finalize(self):
        for qn, eng in DQ.items():
            if self.cnt[qn]:
                self._wait(eng, [(qn, self.cnt[qn])])
        with self.nc.Block() as blk:
            @blk.tensor
            def _(E):
                for fn in self.q["pe"]:
                    fn(E)

            @blk.vector
            def _(E):
                for fn in self.q["dve"]:
                    fn(E)

            @blk.scalar
            def _(E):
                for fn in self.q["act"]:
                    fn(E)

            @blk.gpsimd
            def _(E):
                for fn in self.q["gp"]:
                    fn(E)

            @blk.sync
            def _(E):
                for fn in self.q["sync"]:
                    fn(E)


class Builder:
    def __init__(self, stop_after=None, debug_outputs=()):
        self.stop_after = stop_after
        self.debug_outputs = set(debug_outputs)
        nc = bass.Bass("TRN2", target_bir_lowering=False,
                       detect_race_conditions=False)
        self.nc = nc
        self.p = P(nc)
        self.pbank = [nc.alloc_psum_tensor(f"pb{i}", [128, 512], F32)
                      for i in range(7)]
        self.pbank_bf = nc.alloc_psum_tensor("pbt", [128, 1024], BF16)
        self.bank_tok = [None] * 8
        self._rr = 0
        self.rr_n = 7
        self.din = {}
        self.ds = {}
        self._declare_dram()
        self._alloc_sbuf()

    def _declare_dram(self):
        nc = self.nc
        self.din["x"] = nc.dram_tensor("x", [S, D], F32, kind="ExternalInput").ap()
        wspecs = {
            "Wq": [D, D], "bq": [1, D], "Wk": [D, D], "bk": [1, D],
            "Wv": [D, D], "bv": [1, D], "Wo": [D, D], "bo": [1, D],
            "ln1_g": [1, D], "ln1_b": [1, D],
            "w_qkv": [D, 3 * D], "b_qkv": [1, 3 * D],
            "w_o": [D, D], "b_o": [1, D],
            "ln2_g": [1, D], "ln2_b": [1, D],
            "gate_w": [D, NE], "gate_b": [1, NE],
            "W1": [NE, D, DFF], "b1": [NE, DFF],
            "W2": [NE, DFF, D], "b2": [NE, D],
        }
        for k, shp in wspecs.items():
            self.din[k] = nc.dram_tensor(k, shp, F32, kind="ExternalInput").ap()
        self.dout = nc.dram_tensor("out", [S, D], F32, kind="ExternalOutput").ap()
        scr = {
            "dbg_ex": ([H, NTT, 128, 64], F32),
            "dbg_w": ([2, S, TB], BF16),
            "dbg_gate": ([3, 128, NTT * NE], F32),
            "dbg_g2": ([NTT, 4, 128, NE], F32),
            "dbg_acc": ([S, D], F32),
            "x0T": ([D, S], F32), "x0Thi": ([D, S], BF16), "x0Tlo": ([D, S], BF16),
            "qhi": ([D, S], BF16), "qlo": ([D, S], BF16),
            "khi": ([D, S], BF16), "klo": ([D, S], BF16),
            "vhi": ([H, S, HD], BF16), "vlo": ([H, S, HD], BF16),
            "woThi": ([D, D], BF16), "woTlo": ([D, D], BF16),
            "oThi": ([D, S], BF16), "oTlo": ([D, S], BF16),
            "x1T": ([D, S], F32),
            "h1hi": ([D, S], BF16), "h1lo": ([D, S], BF16),
            "s_tok": ([S, D], F32),
            "h2hi": ([D, S], BF16), "h2lo": ([D, S], BF16),
            "W1hi": ([NE, D, DFF], BF16), "W1lo": ([NE, D, DFF], BF16),
            "W2hi": ([NE, DFF, D], BF16), "W2lo": ([NE, DFF, D], BF16),
        }
        for k, (shp, dt) in scr.items():
            kind = "ExternalOutput" if k in self.debug_outputs else "Internal"
            self.ds[k] = nc.dram_tensor("z_" + k, shp, dt, kind=kind).ap()

    def _alloc_sbuf(self):
        nc = self.nc

        def real(name, shape, dt):
            return nc.alloc_sbuf_tensor(name, shape, dt)

        self.arF = real("arF", [128, 20736], F32)
        self.arB = real("arB", [128, 40960], BF16)
        self.arB2 = real("arB2", [128, 8192], BF16)
        self.prjf = real("prjf", [128, 1024], F32)
        self.prjh = real("prjh", [128, 1024], BF16)
        self.prjl = real("prjl", [128, 1024], BF16)
        self.ones1 = real("ones1", [1, 128], F32)
        self.ones_sq = real("ones_sq", [128, 128], F32)
        self.identf = real("identf", [128, 128], F32)
        self.identb = real("identb", [128, 128], BF16)
        self.ex = real("ex", [128, 64], F32)
        self.mt = real("mt", [128, 8], F32)
        self.mrow = real("mrow", [1, 512], F32)
        self.trow = real("trow", [1, 512], F32)
        self.srow = [real(f"srow{i}", [1, 512], F32) for i in range(4)]
        self.bias = {}
        for nm in ("bq", "bk", "bv_fm", "bo", "b_o", "ln1_g", "ln1_b",
                   "ln2_g", "ln2_b"):
            self.bias[nm] = real(f"bt_{nm}", [128, NC], F32)
        self.bias["b_qkv"] = real("bt_b_qkv", [128, 3 * NC], F32)
        self.bias["b1"] = real("bt_b1", [128, NE * (DFF // 128)], F32)
        self.bq8 = real("bq8", [128, NC], F32)
        self.bqkv8 = real("bqkv8", [128, NC], F32)
        self.gate_b_row = real("gate_b_row", [1, NE], F32)
        self.glog = real("glog", [128, NTT * NE], F32)
        self.ew = real("ew", [128, NTT * NE], F32)
        self.gsc = real("gsc", [128, 24], F32)

    def fF(self, off, w, p=128):
        return self.arF[0:p, off:off + w]

    def fB(self, off, w, p=128):
        return self.arB[0:p, off:off + w]

    def bank(self, wait_extra=()):
        i = self._rr
        self._rr = (self._rr + 1) % self.rr_n
        if self.bank_tok[i] is not None:
            self.p._wait("pe", [self.bank_tok[i]])
        self.p._wait("pe", wait_extra)
        return i, self.pbank[i]

    def bank_fixed(self, i):
        if self.bank_tok[i] is not None:
            self.p._wait("pe", [self.bank_tok[i]])
        return i, self.pbank[i]

    def bank_done(self, i, tok):
        self.bank_tok[i] = tok

    def mm3(self, psum_ap, pairs, extra_f32=(), wait=()):
        p = self.p
        ops = []
        for (ah, al, bh, bl) in pairs:
            ops += [(ah, bh), (ah, bl), (al, bh)]
        ops += list(extra_f32)
        tok = None
        for k, (a, b) in enumerate(ops):
            st = (k == 0)
            sp = (k == len(ops) - 1)
            tok = p.do("pe",
                       lambda E, o=psum_ap, a=a, b=b, st=st, sp=sp:
                       E.matmul(o, a, b, start=st, stop=sp),
                       wait=(wait if k == 0 else ()), sig=sp)
        return tok

    def evac(self, psum_ap, out_ap, func=AF.Identity, bias=0.0, scale=1.0, wait=()):
        return self.p.do("act",
                         lambda E, o=out_ap, i=psum_ap, f=func, b=bias, s=scale:
                         E.activation(out=o, in_=i, func=f, bias=b, scale=s),
                         wait=wait, sig=True)

    def split(self, src, hi, lo, wait=()):
        p = self.p
        p.do("dve", lambda E, o=hi, i=src: E.tensor_copy(out=o, in_=i), wait=wait)
        return p.do("dve", lambda E, o=lo, a=src, b=hi: E.tensor_sub(o, a, b),
                    sig=True)

    def transpose(self, psum_ap, in_ap, bf16, wait=()):
        ident = self.identb if bf16 else self.identf
        return self.p.do("pe",
                         lambda E, o=psum_ap, i=in_ap, id_=ident:
                         E.transpose(o, i, id_[:]),
                         wait=wait, sig=True)

    def setup(self):
        p = self.p
        t0 = p.do("dve", lambda E: E.memset(self.ones1[:], 1.0), sig=True)
        t1 = p.do("dve", lambda E: E.memset(self.ones_sq[:], 1.0), sig=True)
        t2 = p.do("dve", lambda E: E.memset(self.identf[:], 1.0), sig=True)
        t3 = p.do("gp", lambda E: E.affine_select(
            out=self.identf[:], in_=self.identf[:], pattern=[[1, 128]],
            compare_op=ALU.is_equal, fill=0.0, base=0, channel_multiplier=-1),
            wait=[t2], sig=True)
        t4 = p.do("dve", lambda E: E.tensor_copy(out=self.identb[:], in_=self.identf[:]),
                  wait=[t3], sig=True)
        self.tok_setup = [t0, t1, t3, t4]
        self.tok_const = list(self.tok_setup)
        for nm, srcn in (("bq", "bq"), ("bk", "bk"), ("bv_fm", "bv"), ("bo", "bo"),
                         ("b_o", "b_o"), ("ln1_g", "ln1_g"), ("ln1_b", "ln1_b"),
                         ("ln2_g", "ln2_g"), ("ln2_b", "ln2_b")):
            self.tok_const.append(p.dma(
                "dq_s", self.bias[nm][:],
                self.din[srcn].rearrange("o (c p) -> p (o c)", p=128), slow_ok=True))
        self.tok_const.append(p.dma(
            "dq_s", self.bias["b_qkv"][:],
            self.din["b_qkv"].rearrange("o (c p) -> p (o c)", p=128), slow_ok=True))
        self.tok_const.append(p.dma(
            "dq_s", self.bias["b1"][:],
            self.din["b1"].rearrange("e (c p) -> p (e c)", p=128), slow_ok=True))
        self.tok_const.append(p.dma("dq_s", self.gate_b_row[:], self.din["gate_b"]))
        tbq8 = p.do("dve",
                    lambda E: E.tensor_scalar_mul(self.bq8[:], self.bias["bq"][:], 0.125),
                    wait=list(self.tok_const), sig=True)
        tbq82 = p.do("dve",
                     lambda E: E.tensor_scalar_mul(
                         self.bqkv8[:], self.bias["b_qkv"][:, 0:NC], 0.125),
                     sig=True)
        self.tok_const += [tbq8, tbq82]

    def p0_input(self):
        p = self.p
        ds = self.ds
        xstage = self.fF(0, D)
        xT_col = self.fF(1024, NC * TT)
        xT_hi = self.fB(0, NC * TT)
        xT_lo = self.fB(1024, NC * TT)
        tok_stage = None
        tok_col = None
        sp_toks = []
        for t in range(NTT):
            tok_ld = p.dma("dq_s", xstage, self.din["x"][t * TT:(t + 1) * TT, :],
                           wait=[tok_stage])
            evacs = []
            last_tt = None
            for c in range(NC):
                bi, pb = self.bank()
                last_tt = self.transpose(pb[0:128, 0:128],
                                         xstage[:, c * 128:(c + 1) * 128], False,
                                         wait=[tok_ld] + self.tok_setup)
                et = self.evac(pb[0:128, 0:128], xT_col[:, c * TT:(c + 1) * TT],
                               wait=[last_tt] + ([tok_col] if c == 0 else []))
                self.bank_done(bi, et)
                evacs.append(et)
            tok_stage = last_tt
            sp_toks.append(p.dma(
                "dq_s",
                ds["x0T"].rearrange("(c p) s -> p c s", p=128)[:, :, t * TT:(t + 1) * TT],
                xT_col.rearrange("p (c q) -> p c q", c=NC), wait=evacs))
            stok = self.split(xT_col, xT_hi, xT_lo, wait=evacs)
            sp_toks.append(p.dma(
                "dq_s",
                ds["x0Thi"].rearrange("(c p) s -> p c s", p=128)[:, :, t * TT:(t + 1) * TT],
                xT_hi.rearrange("p (c q) -> p c q", c=NC), wait=[stok]))
            sp_toks.append(p.dma(
                "dq_s",
                ds["x0Tlo"].rearrange("(c p) s -> p c s", p=128)[:, :, t * TT:(t + 1) * TT],
                xT_lo.rearrange("p (c q) -> p c q", c=NC), wait=[stok]))
            tok_col = sp_toks[-1]
        self.tok_p0 = sp_toks[-3:]

    def prep_wT(self, wname):
        p = self.p
        wT_hi = self.fB(16384, 8192)
        wT_lo = self.fB(24576, 8192)
        wpan_h = [self.fB(32768, 1024), self.fB(33792, 1024)]
        wpan_l = [self.fB(34816, 1024), self.fB(35840, 1024)]
        wpan_f = [self.fF(0, 1024), self.fF(1024, 1024)]
        if not hasattr(self, "wpan_tok"):
            self.wpan_tok = [None, None]
        toks = []
        for ic in range(NC):
            pg = ic % 2
            src = self.din[wname][ic * 128:(ic + 1) * 128, :]
            tok_f = p.dma("dq_s", wpan_f[pg], src, wait=[self.wpan_tok[pg]])
            tok_h = p.dma("dq_g", wpan_h[pg], src, wait=[self.wpan_tok[pg]])
            stok = p.do("dve",
                        lambda E, o=wpan_l[pg], a=wpan_f[pg], b=wpan_h[pg]:
                        E.tensor_sub(o, a, b),
                        wait=[tok_f, tok_h], sig=True)
            last = None
            for dc in range(NC):
                for (srcb, dstb, tw) in ((wpan_h[pg], wT_hi, tok_h),
                                         (wpan_l[pg], wT_lo, stok)):
                    ttok = self.transpose(self.pbank_bf[0:128, 0:128],
                                          srcb[:, dc * 128:(dc + 1) * 128], True,
                                          wait=[tw, self.bank_tok[7]] + self.tok_setup)
                    etok = p.do("dve",
                                lambda E,
                                o=dstb[:, dc * 1024 + ic * 128: dc * 1024 + (ic + 1) * 128],
                                i=self.pbank_bf[0:128, 0:128]:
                                E.tensor_copy(out=o, in_=i),
                                wait=[ttok], sig=True)
                    self.bank_tok[7] = etok
                    last = etok
            self.wpan_tok[pg] = last
            toks.append(last)
        return wT_hi, wT_lo, toks[-2:]

    def p1a(self):
        p = self.p
        ds = self.ds
        self.xs_hi = [self.fB(0, NC * TB), self.fB(8192, NC * TB)]
        self.xs_lo = [self.fB(4096, NC * TB), self.fB(12288, NC * TB)]
        self.xs_tok = [None, None]
        self.prj_tok = [None, None]

        wT_hi, wT_lo, tw = self.prep_wT("Wq")
        self._proj_fm(ds["x0Thi"], ds["x0Tlo"], ds["qhi"], ds["qlo"],
                      wT_hi, wT_lo, tw, self.bq8, 0.125, self.tok_p0, True)
        wT_hi, wT_lo, tw = self.prep_wT("Wk")
        self._proj_fm(ds["x0Thi"], ds["x0Tlo"], ds["khi"], ds["klo"],
                      wT_hi, wT_lo, tw, self.bias["bk"], 1.0, [], True)
        wT_hi, wT_lo, tw = self.prep_wT("Wv")
        bvr = self.fF(2048, D)[0:1, :]
        tbv = p.dma("dq_s", bvr, self.din["bv"])
        self._vtok(ds["x0Thi"], ds["x0Tlo"], wT_hi, wT_lo, tw, bvr, [tbv],
                   trans_w=True)
        wT_hi, wT_lo, tw = self.prep_wT("Wo")
        t1 = p.dma("dq_s", ds["woThi"].rearrange("(c p) i -> p c i", p=128),
                   wT_hi.rearrange("p (c i) -> p c i", c=NC), wait=tw)
        t2 = p.dma("dq_s", ds["woTlo"].rearrange("(c p) i -> p c i", p=128),
                   wT_lo.rearrange("p (c i) -> p c i", c=NC), wait=tw)
        self.tok_woT = [t1, t2]

    def x_block(self, dhi, dlo, tb, sg, extra_wait=()):
        p = self.p
        t1 = p.dma("dq_s", self.xs_hi[sg].rearrange("p (c w) -> p c w", c=NC),
                   dhi.rearrange("(c p) s -> p c s", p=128)
                   [:, :, tb * TB:(tb + 1) * TB],
                   wait=[self.xs_tok[sg]] + list(extra_wait))
        t2 = p.dma("dq_s", self.xs_lo[sg].rearrange("p (c w) -> p c w", c=NC),
                   dlo.rearrange("(c p) s -> p c s", p=128)
                   [:, :, tb * TB:(tb + 1) * TB])
        return [t1, t2]

    def _proj_fm(self, src_hi, src_lo, dsthi, dstlo, w_hi, w_lo, wtoks,
                 bias_tile, scale, first_wait, trans_w):
        p = self.p
        for tb in range(NTB):
            sg = tb % 2
            xt = self.x_block(src_hi, src_lo, tb, sg,
                              extra_wait=(first_wait if tb == 0 else []))
            for oc in range(NC):
                bi, pb = self.bank()
                pairs = []
                for dc in range(NC):
                    if trans_w:
                        wh = w_hi[:, dc * 1024 + oc * 128: dc * 1024 + (oc + 1) * 128]
                        wl = w_lo[:, dc * 1024 + oc * 128: dc * 1024 + (oc + 1) * 128]
                    else:
                        wh = w_hi[:, dc, oc * 128:(oc + 1) * 128]
                        wl = w_lo[:, dc, oc * 128:(oc + 1) * 128]
                    pairs.append((
                        wh, wl,
                        self.xs_hi[sg][:, dc * TB:(dc + 1) * TB],
                        self.xs_lo[sg][:, dc * TB:(dc + 1) * TB]))
                mtok = self.mm3(pb[:], pairs, wait=list(wtoks) + xt)
                self.xs_tok[sg] = mtok
                psg = (tb * NC + oc) % 2
                fsl = self.prjf[:, psg * TB:(psg + 1) * TB]
                et = self.evac(pb[:], fsl, bias=bias_tile[:, oc:oc + 1],
                               scale=scale,
                               wait=[mtok, self.prj_tok[psg]] + self.tok_const)
                self.bank_done(bi, et)
                stok = self.split(fsl, self.prjh[:, psg * TB:(psg + 1) * TB],
                                  self.prjl[:, psg * TB:(psg + 1) * TB],
                                  wait=[et])
                p.dma("dq_s",
                      dsthi[oc * 128:(oc + 1) * 128, tb * TB:(tb + 1) * TB],
                      self.prjh[:, psg * TB:(psg + 1) * TB], wait=[stok])
                t2 = p.dma("dq_s",
                           dstlo[oc * 128:(oc + 1) * 128, tb * TB:(tb + 1) * TB],
                           self.prjl[:, psg * TB:(psg + 1) * TB], wait=[stok])
                self.prj_tok[psg] = t2

    def _vtok(self, src_hi, src_lo, w_hi, w_lo, wtoks, brow, btoks, trans_w):
        p = self.p
        ds = self.ds
        for tb in range(NTB):
            sg = tb % 2
            xt = self.x_block(src_hi, src_lo, tb, sg)
            for tt4 in range(4):
                tt = tb * 4 + tt4
                for ib in range(2):
                    bi, pb = self.bank()
                    pairs = []
                    for dc in range(NC):
                        if trans_w:
                            wh = w_hi[:, dc * 1024 + ib * 512: dc * 1024 + (ib + 1) * 512]
                            wl = w_lo[:, dc * 1024 + ib * 512: dc * 1024 + (ib + 1) * 512]
                        else:
                            wh = w_hi[ib][:, dc, :]
                            wl = w_lo[ib][:, dc, :]
                        pairs.append((
                            self.xs_hi[sg][:, dc * TB + tt4 * TT: dc * TB + (tt4 + 1) * TT],
                            self.xs_lo[sg][:, dc * TB + tt4 * TT: dc * TB + (tt4 + 1) * TT],
                            wh, wl))
                    mtok = self.mm3(
                        pb[:], pairs,
                        extra_f32=[(self.ones1[0:1, 0:128],
                                    brow[0:1, ib * 512:(ib + 1) * 512])],
                        wait=list(wtoks) + xt + list(btoks) + self.tok_setup)
                    self.xs_tok[sg] = mtok
                    psg = (tt * 2 + ib) % 2
                    fsl = self.prjf[:, psg * TB:(psg + 1) * TB]
                    et = self.evac(pb[:], fsl, wait=[mtok, self.prj_tok[psg]])
                    self.bank_done(bi, et)
                    stok = self.split(fsl, self.prjh[:, psg * TB:(psg + 1) * TB],
                                      self.prjl[:, psg * TB:(psg + 1) * TB],
                                      wait=[et])
                    sp = None
                    for j in range(8):
                        h = ib * 8 + j
                        sp = p.dma("dq_s",
                                   ds["vhi"][h, tt * TT:(tt + 1) * TT, :],
                                   self.prjh[:, psg * TB + j * 64: psg * TB + (j + 1) * 64],
                                   wait=[stok])
                        sp = p.dma("dq_s",
                                   ds["vlo"][h, tt * TT:(tt + 1) * TT, :],
                                   self.prjl[:, psg * TB + j * 64: psg * TB + (j + 1) * 64])
                    self.prj_tok[psg] = sp

    def attn_heads(self, sparse):
        p = self.p
        ds = self.ds
        dqh, dql = ds["qhi"], ds["qlo"]
        dkh, dkl = ds["khi"], ds["klo"]
        doh, dol = ds["oThi"], ds["oTlo"]
        sc = [self.fF(0, S), self.fF(2048, S), self.fF(4096, S)]
        wq_s = self.fF(6144, TB)
        wq_m = self.fF(6656, TB)
        wq_e = self.fF(7168, TB)
        wq_w = self.fF(7680, TB)
        bc_m = self.fF(8192, TB)
        bc_t = self.fF(8704, TB)
        av_f = self.fF(9216, TB, p=64)
        rb = self.fF(9728, TB, p=64)
        den_t = self.fF(10240, TB)
        rec_t = self.fF(10752, TB)
        vh_b = [self.fB(0, NTT * VW65), self.fB(2080, NTT * VW65)]
        vl_b = [self.fB(1040, NTT * VW65), self.fB(3120, NTT * VW65)]
        qt_h = self.fB(4224, 2 * S, p=64)
        kt_h = self.fB(8320, 2 * S, p=64)
        wT_h2 = self.fB(12416, NTT * TB)
        wT_l2 = self.fB(20608, NTT * TB)
        oT_h = self.fB(28800, TB, p=64)
        oT_l = self.fB(29312, TB, p=64)

        st0 = getattr(self, "_ah_state", None)
        if st0 is None:
            self._ah_state = {
                "tok_qk": None, "tok_sc": [None, None, None], "tok_mt": None,
                "tok_tside": None, "tok_w": None, "tok_av": [None, None],
                "tok_oT": [None, None], "tok_bc": None, "tok_v": [None, None]}
        stt = self._ah_state
        entry = [self.xs_tok[0], self.xs_tok[1], stt["tok_qk"],
                 stt["tok_v"][0], stt["tok_v"][1]]
        tons = []
        for g in range(2):
            t1 = p.do("dve", lambda E, o=vh_b[g].rearrange(
                "p (t j) -> p t j", t=NTT)[:, :, 64:65]: E.memset(o, 1.0),
                wait=entry, sig=True)
            t2 = p.do("dve", lambda E, o=vl_b[g].rearrange(
                "p (t j) -> p t j", t=NTT)[:, :, 64:65]: E.memset(o, 0.0),
                sig=True)
            tons += [t1, t2]
        stt["tok_ones"] = tons
        stt["tok_v"] = [tons[1], tons[3]]
        for h in range(H):
            hg = h % 2
            tq1 = p.dma("dq_s", qt_h[:, 0:S], dqh[h * 64:(h + 1) * 64, :],
                        wait=[stt["tok_qk"]])
            tq2 = p.dma("dq_s", qt_h[:, S:2 * S], dql[h * 64:(h + 1) * 64, :])
            tk1 = p.dma("dq_s", kt_h[:, 0:S], dkh[h * 64:(h + 1) * 64, :])
            tk2 = p.dma("dq_s", kt_h[:, S:2 * S], dkl[h * 64:(h + 1) * 64, :])
            tqk = [tq1, tq2, tk1, tk2]
            tv1 = p.dma("dq_s",
                        vh_b[hg].rearrange("p (t j) -> p t j", t=NTT)[:, :, 0:64],
                        ds["vhi"][h].rearrange("(t p) j -> p t j", p=128),
                        wait=[stt["tok_v"][hg]])
            tv2 = p.dma("dq_s",
                        vl_b[hg].rearrange("p (t j) -> p t j", t=NTT)[:, :, 0:64],
                        ds["vlo"][h].rearrange("(t p) j -> p t j", p=128))
            tvv = [tv1, tv2] + stt["tok_ones"]
            last_qk_use = None
            for qb in range(NTB):
                for qt4 in range(4):
                    qt = qb * 4 + qt4
                    sbi = qt % 2
                    sbuf = sc[sbi]
                    ev_toks = []
                    for kb in range(NTB):
                        bi, pb = self.bank()
                        pairs = [(
                            qt_h[:, qt * 128:(qt + 1) * 128],
                            qt_h[:, S + qt * 128: S + (qt + 1) * 128],
                            kt_h[:, kb * TB:(kb + 1) * TB],
                            kt_h[:, S + kb * TB: S + (kb + 1) * TB])]
                        mtok = self.mm3(pb[:], pairs, wait=tqk)
                        last_qk_use = mtok
                        et = self.evac(pb[:], sbuf[:, kb * TB:(kb + 1) * TB],
                                       wait=[mtok] +
                                       ([stt["tok_sc"][sbi]] if kb == 0 else []))
                        self.bank_done(bi, et)
                        ev_toks.append(et)
                    if sparse:
                        cur = sbuf
                        nxt = sc[2]
                        p._wait("dve", ev_toks + [stt["tok_sc"][2], stt["tok_mt"]])
                        for r in range(8):
                            tok_max = p.do("dve",
                                 lambda E, o=self.ex[:, r * 8:(r + 1) * 8], i=cur:
                                 E.max(out=o, in_=i[:]), sig=True)
                            if r < 7:
                                p.do("dve",
                                     lambda E, o=nxt, v=self.ex[:, r * 8:(r + 1) * 8],
                                     i=cur: E.match_replace(
                                         out=o[:], in_to_replace=v, in_values=i[:],
                                         imm_value=NEG), wait=[tok_max])
                                cur, nxt = nxt, cur
                        p.do("dve", lambda E, o=self.mt[:, 2 * qt4:2 * qt4 + 1],
                             i=self.ex[:, 0:1]: E.tensor_scalar_mul(o, i, -1.0),
                             wait=[tok_max])
                        tmt = p.do("dve", lambda E, o=self.mt[:, 2 * qt4 + 1:2 * qt4 + 2],
                                   i=self.ex[:, 63:64]: E.tensor_copy(out=o, in_=i),
                                   sig=True)
                        stt["tok_sc"][sbi] = tmt
                        stt["tok_sc"][2] = tmt
                        stt["tok_mt"] = tmt
                        if self.stop_after == "p1b":
                            p.dma("dq_s", ds["dbg_ex"][h, qt], self.ex[:],
                                  wait=[tmt])
                    else:
                        tmt = p.do("dve",
                                   lambda E, o=self.mt[:, 2 * qt4:2 * qt4 + 1], i=sbuf:
                                   E.tensor_reduce(o, i[:], AXX, ALU.max, negate=True),
                                   wait=ev_toks + [stt["tok_mt"]], sig=True)
                        stt["tok_sc"][sbi] = tmt
                        stt["tok_mt"] = tmt
                rows_toks = []
                for j in range(4):
                    for which in range(2 if sparse else 1):
                        bi, pb = self.bank()
                        ttr = self.transpose(pb[0:1, 0:128],
                                             self.mt[:, 2 * j + which:2 * j + which + 1],
                                             False, wait=[stt["tok_mt"], stt["tok_bc"]])
                        dst = (self.mrow if which == 0 else self.trow)
                        rt = p.do("dve",
                                  lambda E, o=dst[0:1, j * 128:(j + 1) * 128],
                                  i=pb[0:1, 0:128]: E.tensor_copy(out=o, in_=i),
                                  wait=[ttr], sig=True)
                        self.bank_done(bi, rt)
                        rows_toks.append(rt)
                bi, pb = self.bank()
                tb1 = self.mm3(pb[:], [], extra_f32=[(self.ones1[:], self.mrow[:])],
                               wait=rows_toks + self.tok_setup)
                eb1 = self.evac(pb[:], bc_m, wait=[tb1, stt["tok_tside"]])
                self.bank_done(bi, eb1)
                eb2 = None
                if sparse:
                    bi2, pb2 = self.bank()
                    tb2 = self.mm3(pb2[:], [], extra_f32=[(self.ones1[:], self.trow[:])],
                                   wait=rows_toks)
                    eb2 = self.evac(pb2[:], bc_t, wait=[tb2, stt["tok_tside"]])
                    self.bank_done(bi2, eb2)
                stt["tok_bc"] = eb2 if sparse else eb1
                t_e_prev = None
                t_w_prev = None
                split_prev = None
                for kt in range(NTT):
                    bi, pb = self.bank()
                    pairs = [(
                        kt_h[:, kt * 128:(kt + 1) * 128],
                        kt_h[:, S + kt * 128: S + (kt + 1) * 128],
                        qt_h[:, qb * TB:(qb + 1) * TB],
                        qt_h[:, S + qb * TB: S + (qb + 1) * TB])]
                    mtok = self.mm3(pb[:], pairs, wait=tqk)
                    last_qk_use = mtok
                    t_s = p.do("dve", lambda E, o=wq_s, a=pb[:], b=bc_m:
                               E.tensor_add(o, a, b),
                               wait=[mtok, eb1] + ([eb2] if eb2 else []) +
                               [t_e_prev, stt["tok_w"]], sig=True)
                    if sparse:
                        t_m = p.do("dve", lambda E, o=wq_m, a=pb[:], b=bc_t:
                                   E.tensor_tensor(o, a, b, ALU.is_ge),
                                   sig=True)
                        bank_rel = t_m
                    else:
                        bank_rel = t_s
                    self.bank_done(bi, bank_rel)
                    last_tside = bank_rel
                    t_e = p.do("act", lambda E, o=wq_e, i=wq_s:
                               E.activation(out=o, in_=i, func=AF.Exp),
                               wait=[t_s, t_w_prev if sparse else split_prev],
                               sig=True)
                    t_e_prev = t_e
                    if sparse:
                        t_w = p.do("dve", lambda E, o=wq_w, a=wq_e, b=wq_m:
                                   E.tensor_mul(o, a, b), wait=[t_e], sig=True)
                        t_w_prev = t_w
                        wsrc = wq_w
                        wtok = t_w
                    else:
                        wsrc = wq_e
                        wtok = t_e
                    split_prev = self.split(
                        wsrc, wT_h2[:, kt * TB:(kt + 1) * TB],
                        wT_l2[:, kt * TB:(kt + 1) * TB],
                        wait=[wtok, stt["tok_w"]])
                if self.stop_after == "p1b" and sparse and h == 4 and qb == 1:
                    p.dma("dq_s", ds["dbg_w"][0].rearrange("(t p) w -> p t w", p=128),
                          wT_h2.rearrange("p (t w) -> p t w", t=NTT),
                          wait=[split_prev])
                    p.dma("dq_s", ds["dbg_w"][1].rearrange("(t p) w -> p t w", p=128),
                          wT_l2.rearrange("p (t w) -> p t w", t=NTT))
                bi, pb = self.bank()
                pairs = []
                for kt in range(NTT):
                    pairs.append((
                        vh_b[hg][:, kt * VW65:(kt + 1) * VW65],
                        vl_b[hg][:, kt * VW65:(kt + 1) * VW65],
                        wT_h2[:, kt * TB:(kt + 1) * TB],
                        wT_l2[:, kt * TB:(kt + 1) * TB]))
                avtok = self.mm3(pb[0:65, :], pairs, wait=[split_prev] + tvv)
                stt["tok_w"] = avtok
                stt["tok_v"][hg] = avtok
                e_den = self.evac(pb[64:65, :], den_t[64:65, :],
                                  wait=[avtok, stt["tok_av"][0]])
                tr = p.do("dve", lambda E, o=rec_t[64:65, :], i=den_t[64:65, :]:
                          E.reciprocal(out=o, in_=i), wait=[e_den], sig=True)
                bi2, pb2 = self.bank()
                tb3 = self.mm3(pb2[0:64, :], [],
                               extra_f32=[(self.ones_sq[64:65, 0:64],
                                           rec_t[64:65, :])],
                               wait=[tr])
                erb = self.evac(pb2[0:64, :], rb,
                                wait=[tb3, stt["tok_av"][0], stt["tok_av"][1]])
                self.bank_done(bi2, erb)
                t_av = p.do("dve", lambda E, o=av_f, a=pb[0:64, :], b=rb:
                            E.tensor_mul(o, a, b),
                            wait=[erb, avtok], sig=True)
                self.bank_done(bi, t_av)
                sto = self.split(av_f, oT_h, oT_l, wait=[t_av])
                o1 = p.dma("dq_s", doh[h * 64:(h + 1) * 64, qb * TB:(qb + 1) * TB],
                           oT_h, wait=[sto])
                o2 = p.dma("dq_s", dol[h * 64:(h + 1) * 64, qb * TB:(qb + 1) * TB],
                           oT_l, wait=[sto])
                stt["tok_av"] = [t_av, o2]
                stt["tok_tside"] = last_tside
                stt["tok_oT"] = [o1, o2]
            stt["tok_qk"] = last_qk_use
        return stt["tok_oT"]

    def proj_residual_ln(self, which):
        p = self.p
        ds = self.ds
        xpre = self.fF(0, NC * S)
        res_f = self.fF(16384, TB)
        sq = self.fF(16896, TB)
        bc_nm = self.fF(17408, TB)
        bc_rs = self.fF(17920, TB)
        y = self.fF(18432, TB)
        hbuf = self.fF(18944, TB)
        x1rel = self.fF(19456, TB)
        sbuf2 = self.fF(19968, TB)
        sev = self.fF(20480, 128)
        oTs_hi = [self.fB(0, NC * TB), self.fB(8192, NC * TB)]
        oTs_lo = [self.fB(4096, NC * TB), self.fB(12288, NC * TB)]
        hst_h = self.fB(32768, TB)
        hst_l = self.fB(33280, TB)
        oTs_tok = [None, None]
        res_tok = getattr(self, "tok_presplit", None)
        if which == "attn1":
            wT_hi = self.fB(16384, 8192)
            wT_lo = self.fB(24576, 8192)
            t1 = p.dma("dq_s", wT_hi.rearrange("p (c i) -> p c i", c=NC),
                       ds["woThi"].rearrange("(c p) i -> p c i", p=128),
                       wait=self.tok_woT + self._ah_state["tok_oT"] +
                       [self._ah_state["tok_w"]])
            t2 = p.dma("dq_s", wT_lo.rearrange("p (c i) -> p c i", c=NC),
                       ds["woTlo"].rearrange("(c p) i -> p c i", p=128))
            wtok = [t1, t2]
            bias_tile = self.bias["bo"]
            res_dram = ds["x0T"]
        else:
            whi, wlo = self._wo_views
            wtok = self.wo_tok
            bias_tile = self.bias["b_o"]
            res_dram = ds["x1T"]

        sum_x, sum_x2, nmean, rstd = self.srow
        tpre = getattr(self, "tok_presplit", None)
        stat_tok = [tpre]
        prev_bc = [None]
        self.rr_n = 5
        ln_toks = []
        for tb in range(NTB):
            sg = tb % 2
            to1 = p.dma("dq_s", oTs_hi[sg].rearrange("p (c w) -> p c w", c=NC),
                        ds["oThi"].rearrange("(c p) s -> p c s", p=128)
                        [:, :, tb * TB:(tb + 1) * TB],
                        wait=[oTs_tok[sg]] + (self._ah_state["tok_oT"] if tb == 0 else []))
            to2 = p.dma("dq_s", oTs_lo[sg].rearrange("p (c w) -> p c w", c=NC),
                        ds["oTlo"].rearrange("(c p) s -> p c s", p=128)
                        [:, :, tb * TB:(tb + 1) * TB])
            sq_toks = []
            bi1, pbs1 = self.bank_fixed(5)
            bi2, pbs2 = self.bank_fixed(6)
            m1 = m2 = None
            for oc in range(NC):
                bi, pb = self.bank()
                pairs = []
                for dc in range(NC):
                    if which == "attn1":
                        wh = wT_hi[:, dc * 1024 + oc * 128: dc * 1024 + (oc + 1) * 128]
                        wl = wT_lo[:, dc * 1024 + oc * 128: dc * 1024 + (oc + 1) * 128]
                    else:
                        wh = whi[:, dc, oc * 128:(oc + 1) * 128]
                        wl = wlo[:, dc, oc * 128:(oc + 1) * 128]
                    pairs.append((
                        wh, wl,
                        oTs_hi[sg][:, dc * TB:(dc + 1) * TB],
                        oTs_lo[sg][:, dc * TB:(dc + 1) * TB]))
                mtok = self.mm3(pb[:], pairs, wait=list(wtok) + [to1, to2])
                oTs_tok[sg] = mtok
                rt = p.dma("dq_s", res_f,
                           res_dram[oc * 128:(oc + 1) * 128, tb * TB:(tb + 1) * TB],
                           wait=[res_tok])
                psg = (tb * NC + oc) % 2
                fsl = self.prjf[:, psg * TB:(psg + 1) * TB]
                et = self.evac(pb[:], fsl, bias=bias_tile[:, oc:oc + 1],
                               wait=[mtok, getattr(self, "_prl_tok", None)] + self.tok_const)
                self.bank_done(bi, et)
                xsl = xpre[:, oc * S + tb * TB: oc * S + tb * TB + TB]
                ta = p.do("dve", lambda E, o=xsl, a=fsl, b=res_f:
                          E.tensor_add(o, a, b), wait=[et, rt], sig=True)
                res_tok = ta
                self._prl_tok = ta
                tsq = p.do("act", lambda E, o=sq, i=xsl:
                           E.activation(out=o, in_=i, func=AF.Square),
                           wait=[ta, stat_tok[0]], sig=True)
                st_ = (oc == 0)
                sp_ = (oc == NC - 1)
                m1 = p.do("pe", lambda E, o=pbs1[0:1, :], a=self.ones_sq[0:128, 0:1],
                          b=xsl, st=st_, sp=sp_:
                          E.matmul(o, a, b, start=st, stop=sp),
                          wait=[ta], sig=sp_)
                m2 = p.do("pe", lambda E, o=pbs2[0:1, :], a=self.ones_sq[0:128, 0:1],
                          b=sq, st=st_, sp=sp_:
                          E.matmul(o, a, b, start=st, stop=sp),
                          wait=[tsq], sig=sp_)
                sq_toks.append(tsq)
            e1 = self.evac(pbs1[0:1, :], sum_x[:], wait=[m1])
            e2 = self.evac(pbs2[0:1, :], sum_x2[:], wait=[m2])
            self.bank_done(bi1, e1)
            self.bank_done(bi2, e2)
            t1_ = p.do("dve", lambda E: E.tensor_scalar_mul(nmean[:], sum_x[:],
                                                            -1.0 / 1024.0),
                       wait=[e1, prev_bc[0]], sig=True)
            t2_ = p.do("dve", lambda E: E.tensor_tensor(rstd[:], nmean[:], nmean[:],
                                                        ALU.mult),
                       wait=[t1_], sig=True)
            t3_ = p.do("dve", lambda E: E.scalar_tensor_tensor(
                rstd[:], sum_x2[:], 1.0 / 1024.0, rstd[:],
                ALU.mult, ALU.subtract), wait=[e2, t2_], sig=True)
            t3b = p.do("dve", lambda E: E.tensor_scalar_add(rstd[:], rstd[:], EPS),
                       wait=[t3_], sig=True)
            t4_ = p.do("act", lambda E: E.activation(out=rstd[:], in_=rstd[:],
                                                     func=AF.Sqrt),
                       wait=[t3b], sig=True)
            t5_ = p.do("dve", lambda E: E.reciprocal(out=rstd[:], in_=rstd[:]),
                       wait=[t4_], sig=True)
            bim, pbm = self.bank()
            tbm = self.mm3(pbm[:], [], extra_f32=[(self.ones1[:], nmean[:])],
                           wait=[t1_, t2_])
            ebm = self.evac(pbm[:], bc_nm, wait=[tbm, stat_tok[0]])
            self.bank_done(bim, ebm)
            bir, pbr = self.bank()
            tbr = self.mm3(pbr[:], [], extra_f32=[(self.ones1[:], rstd[:])],
                           wait=[t5_])
            ebr = self.evac(pbr[:], bc_rs, wait=[tbr, stat_tok[0]])
            self.bank_done(bir, ebr)
            prev_bc[0] = tbr
            ln_toks = []
            for oc in range(NC):
                xsl = xpre[:, oc * S + tb * TB: oc * S + tb * TB + TB]
                ta1 = p.do("dve", lambda E, o=y, a=xsl, b=bc_nm:
                           E.tensor_add(o, a, b),
                           wait=[ebm, ebr, getattr(self, "_ln_tok", None),
                                 getattr(self, "_y_spill", None)] + sq_toks,
                           sig=True)
                ta2 = p.do("dve", lambda E, o=y, a=y, b=bc_rs:
                           E.tensor_mul(o, a, b), sig=True)
                self._ln_tok = ta2
                if which == "attn1":
                    s1 = p.dma("dq_s",
                               ds["x1T"][oc * 128:(oc + 1) * 128, tb * TB:(tb + 1) * TB],
                               y, wait=[ta2])
                    self._y_spill = s1
                    g = self.bias["ln1_g"][:, oc:oc + 1]
                    b = self.bias["ln1_b"][:, oc:oc + 1]
                    dh, dl = ds["h1hi"], ds["h1lo"]
                else:
                    s1 = None
                    g = self.bias["ln2_g"][:, oc:oc + 1]
                    b = self.bias["ln2_b"][:, oc:oc + 1]
                    dh, dl = ds["h2hi"], ds["h2lo"]
                th = p.do("dve", lambda E, o=hbuf, a=y, g=g, b=b:
                          E.tensor_scalar(o, a, g, b, ALU.mult, ALU.add),
                          wait=[ta2] + self.tok_const, sig=True)
                stok = self.split(hbuf, hst_h, hst_l,
                                  wait=[th, getattr(self, "_h_spill", None)])
                sh1 = p.dma("dq_s",
                            dh[oc * 128:(oc + 1) * 128, tb * TB:(tb + 1) * TB],
                            hst_h, wait=[stok])
                sh2 = p.dma("dq_s",
                            dl[oc * 128:(oc + 1) * 128, tb * TB:(tb + 1) * TB],
                            hst_l, wait=[stok])
                self._h_spill = sh2
                ln_toks.append(sh2)
                if s1 is not None:
                    ln_toks.append(s1)
                if which == "attn2":
                    rt2 = p.dma("dq_s", x1rel,
                                ds["x1T"][oc * 128:(oc + 1) * 128, tb * TB:(tb + 1) * TB],
                                wait=[getattr(self, "_s_tok", None)])
                    ts_ = p.do("dve", lambda E, o=sbuf2, a=xsl, b=x1rel:
                               E.tensor_add(o, a, b),
                               wait=[rt2, getattr(self, "_s_tt", None)] + sq_toks,
                               sig=True)
                    for j in range(4):
                        bi, pb = self.bank()
                        ttok = self.transpose(pb[0:128, 0:128],
                                              sbuf2[:, j * 128:(j + 1) * 128],
                                              False, wait=[ts_])
                        es = self.evac(pb[0:128, 0:128], sev,
                                       wait=[ttok, getattr(self, "_s_ev", None)])
                        self.bank_done(bi, es)
                        sspill = p.dma(
                            "dq_s",
                            ds["s_tok"][tb * TB + j * 128: tb * TB + (j + 1) * 128,
                                        oc * 128:(oc + 1) * 128],
                            sev, wait=[es])
                        self._s_ev = sspill
                        self._s_tt = ttok
                    self._s_tok = sspill
                    ln_toks.append(sspill)
            stat_tok[0] = ln_toks[-1] if ln_toks else None
        self.rr_n = 7
        self._prl_done = ln_toks

    def load_w_panels(self, dram_ap, c0, width, slot, extra_wait=()):
        p = self.p
        whi = self.fB(16384 + slot * 4096, NC * width).rearrange(
            "p (c w) -> p c w", c=NC)
        wlo = self.fB(24576 + slot * 4096, NC * width).rearrange(
            "p (c w) -> p c w", c=NC)
        src = dram_ap[:, c0:c0 + width].rearrange("(c p) w -> p c w", p=128)
        t_h = p.dma("dq_g", whi, src,
                    wait=[self.wslot_tok[slot]] + list(extra_wait))
        toks = []
        hw = width // 2
        for half in range(2):
            srch = dram_ap[:, c0 + half * hw:c0 + (half + 1) * hw] \
                .rearrange("(c p) w -> p c w", p=128)
            fstg = self.fF(0, NC * hw).rearrange("p (c w) -> p c w", c=NC)
            t_f = p.dma("dq_s", fstg, srch,
                        wait=[getattr(self, "_wpf_tok", None)])
            sl_hi = whi[:, :, half * hw:(half + 1) * hw]
            sl_lo = wlo[:, :, half * hw:(half + 1) * hw]
            t_l = p.do("dve",
                       lambda E, o=sl_lo, a=fstg, b=sl_hi: E.tensor_sub(o, a, b),
                       wait=[t_f, t_h], sig=True)
            self._wpf_tok = t_l
            toks.append(t_l)
        return whi, wlo, toks

    def p2a(self):
        p = self.p
        ds = self.ds
        t0 = self._prl_done[-1] if getattr(self, "_prl_done", None) else None
        self.wslot_tok = [t0, t0]
        self.xs_tok = [t0, t0]
        self.prj_tok = [None, None]
        for (part, dsthi, dstlo, btile, sc_) in (
                (0, ds["qhi"], ds["qlo"], self.bqkv8, 0.125),
                (1, ds["khi"], ds["klo"], None, 1.0)):
            for oc2 in range(2):
                c0 = part * D + oc2 * 512
                slot = oc2 % 2
                whi, wlo, wt = self.load_w_panels(
                    self.din["w_qkv"], c0, 512, slot,
                    extra_wait=[getattr(self, "tok_presplit", None)])
                for tb in range(NTB):
                    sg = tb % 2
                    ht = self.x_block(ds["h1hi"], ds["h1lo"], tb, sg)
                    for oq in range(4):
                        oc = oc2 * 4 + oq
                        bi, pb = self.bank()
                        pairs = []
                        for dc in range(NC):
                            pairs.append((
                                whi[:, dc, oq * 128:(oq + 1) * 128],
                                wlo[:, dc, oq * 128:(oq + 1) * 128],
                                self.xs_hi[sg][:, dc * TB:(dc + 1) * TB],
                                self.xs_lo[sg][:, dc * TB:(dc + 1) * TB]))
                        mtok = self.mm3(pb[:], pairs, wait=wt + ht)
                        self.xs_tok[sg] = mtok
                        self.wslot_tok[slot] = mtok
                        psg = (tb * 4 + oq) % 2
                        fsl = self.prjf[:, psg * TB:(psg + 1) * TB]
                        if part == 0:
                            bslice = self.bqkv8[:, oc:oc + 1]
                        else:
                            bslice = self.bias["b_qkv"][:, NC + oc:NC + oc + 1]
                        et = self.evac(pb[:], fsl, bias=bslice, scale=sc_,
                                       wait=[mtok, self.prj_tok[psg]] + self.tok_const)
                        self.bank_done(bi, et)
                        stok = self.split(fsl, self.prjh[:, psg * TB:(psg + 1) * TB],
                                          self.prjl[:, psg * TB:(psg + 1) * TB],
                                          wait=[et])
                        p.dma("dq_s",
                              dsthi[oc * 128:(oc + 1) * 128, tb * TB:(tb + 1) * TB],
                              self.prjh[:, psg * TB:(psg + 1) * TB], wait=[stok])
                        t2 = p.dma("dq_s",
                                   dstlo[oc * 128:(oc + 1) * 128, tb * TB:(tb + 1) * TB],
                                   self.prjl[:, psg * TB:(psg + 1) * TB], wait=[stok])
                        self.prj_tok[psg] = t2
        bvr = self.fF(2048, D)[0:1, :]
        tbv = p.dma("dq_s", bvr, self.din["b_qkv"][:, 2 * D:3 * D],
                    wait=[getattr(self, "_wpf_tok", None)])
        self._wpf_tok = tbv
        w_views = [None, None]
        wt_all = []
        for ib in range(2):
            whi, wlo, wt = self.load_w_panels(self.din["w_qkv"], 2 * D + ib * 512,
                                              512, ib)
            w_views[ib] = (whi, wlo)
            wt_all += wt
        self._vtok(ds["h1hi"], ds["h1lo"],
                   [w_views[0][0], w_views[1][0]],
                   [w_views[0][1], w_views[1][1]],
                   wt_all, bvr, [tbv], trans_w=False)
        self._ah_state["tok_qk"] = self.xs_tok[0]
        self._ah_state["tok_qk"] = self.xs_tok[1]

    def load_wo(self):
        p = self.p
        whi = self.fB(16384, 8192).rearrange("p (c i) -> p c i", c=NC)
        wlo = self.fB(24576, 8192).rearrange("p (c i) -> p c i", c=NC)
        src = self.din["w_o"].rearrange("(c p) i -> p c i", p=128)
        t_h = p.dma("dq_g", whi, src, wait=self._ah_state["tok_oT"] +
                    [self._ah_state["tok_w"], self._ah_state["tok_qk"]])
        toks = []
        for qtr in range(4):
            srch = self.din["w_o"][:, qtr * 256:(qtr + 1) * 256] \
                .rearrange("(c p) w -> p c w", p=128)
            fstg = self.fF(0, NC * 256).rearrange("p (c w) -> p c w", c=NC)
            t_f = p.dma("dq_s", fstg, srch,
                        wait=[getattr(self, "_wpf_tok", None),
                              self._ah_state["tok_sc"][0],
                              self._ah_state["tok_sc"][1],
                              self._ah_state["tok_sc"][2]])
            sl_hi = whi[:, :, qtr * 256:(qtr + 1) * 256]
            sl_lo = wlo[:, :, qtr * 256:(qtr + 1) * 256]
            t_l = p.do("dve",
                       lambda E, o=sl_lo, a=fstg, b=sl_hi: E.tensor_sub(o, a, b),
                       wait=[t_f, t_h], sig=True)
            self._wpf_tok = t_l
            toks.append(t_l)
        self._wo_views = (whi, wlo)
        self.wo_tok = toks[-1:]

    def p3_gating(self):
        p = self.p
        ds = self.ds
        gw_hi = self.fB(2048, NC * NE)
        gw_lo = self.fB(2112, NC * NE)
        h2t_hi = self.fB(0, NC * TT)
        h2t_lo = self.fB(1024, NC * TT)
        gsrc = self.din["gate_w"].rearrange("(c p) e -> p c e", p=128)
        t_prev = self._prl_done[-1] if getattr(self, "_prl_done", None) else None
        tgh = p.dma("dq_g", gw_hi.rearrange("p (c e) -> p c e", c=NC), gsrc,
                    wait=[t_prev], slow_ok=True)
        tgf = p.dma("dq_s", self.fF(0, NC * NE).rearrange("p (c e) -> p c e", c=NC),
                    gsrc, wait=[getattr(self, "_wpf_tok", None), t_prev],
                    slow_ok=True)
        tgl = p.do("dve", lambda E: E.tensor_sub(gw_lo, self.fF(0, NC * NE), gw_hi),
                   wait=[tgh, tgf], sig=True)
        htok = tgl
        ew4 = self.ew.rearrange("p (t e) -> p t e", t=NTT)
        gl4 = self.glog.rearrange("p (t e) -> p t e", t=NTT)
        prev_gate = None
        prev_te = None
        prev_tw8 = None
        for tt in range(NTT):
            t1 = p.dma("dq_s", h2t_hi.rearrange("p (c w) -> p c w", c=NC),
                       ds["h2hi"].rearrange("(c p) s -> p c s", p=128)
                       [:, :, tt * TT:(tt + 1) * TT],
                       wait=[htok])
            t2 = p.dma("dq_s", h2t_lo.rearrange("p (c w) -> p c w", c=NC),
                       ds["h2lo"].rearrange("(c p) s -> p c s", p=128)
                       [:, :, tt * TT:(tt + 1) * TT])
            bi, pb = self.bank()
            pairs = []
            for dc in range(NC):
                pairs.append((
                    h2t_hi.rearrange("p (c w) -> p c w", c=NC)[:, dc, :],
                    h2t_lo.rearrange("p (c w) -> p c w", c=NC)[:, dc, :],
                    gw_hi.rearrange("p (c e) -> p c e", c=NC)[:, dc, :],
                    gw_lo.rearrange("p (c e) -> p c e", c=NC)[:, dc, :]))
            mtok = self.mm3(pb[0:128, 0:NE], pairs,
                            extra_f32=[(self.ones1[0:1, 0:128], self.gate_b_row[:])],
                            wait=[t1, t2, tgl] + self.tok_const)
            htok = mtok
            et = self.evac(pb[0:128, 0:NE], gl4[:, tt, :], wait=[mtok, prev_gate])
            self.bank_done(bi, et)
            tok_gmax = p.do("dve", lambda E, o=self.gsc[:, 0:8], i=gl4[:, tt, :]:
                            E.max(out=o, in_=i), wait=[et], sig=True)
            t_n = p.do("dve", lambda E: E.tensor_scalar_mul(
                self.gsc[:, 8:9], self.gsc[:, 0:1], -1.0),
                wait=[prev_te, tok_gmax], sig=True)
            t_e = p.do("act", lambda E, g=gl4[:, tt, :]: E.activation(
                out=self.gsc[:, 16:24], in_=g, func=AF.Exp,
                bias=self.gsc[:, 8:9]), wait=[t_n, prev_tw8], sig=True)
            prev_te = t_e
            t_m8 = p.do("dve", lambda E, g=gl4[:, tt, :], w=ew4[:, tt, :]:
                E.tensor_tensor(w, g, self.gsc[:, 1:2].to_broadcast([128, NE]),
                                ALU.is_ge),
                wait=[t_e], sig=True)
            t_w8 = p.do("dve", lambda E, w=ew4[:, tt, :]: E.tensor_tensor(
                w, w, self.gsc[:, 16:24], ALU.mult), wait=[t_m8], sig=True)
            prev_tw8 = t_w8
            if "dbg_g2" in self.debug_outputs:
                p.dma("dq_s", ds["dbg_g2"][tt, 0], self.gsc[:, 16:24], wait=[t_e])
                p.dma("dq_s", ds["dbg_g2"][tt, 1], ew4[:, tt, :], wait=[t_m8])
                p.dma("dq_s", ds["dbg_g2"][tt, 2], ew4[:, tt, :], wait=[t_w8])
            t_su = p.do("dve", lambda E, w=ew4[:, tt, :]: E.tensor_reduce(
                self.gsc[:, 10:11], w, AXX, ALU.add), wait=[t_w8], sig=True)
            t_r = p.do("dve", lambda E: E.reciprocal(
                out=self.gsc[:, 11:12], in_=self.gsc[:, 10:11]), wait=[t_su],
                sig=True)
            prev_gate = p.do("dve", lambda E, w=ew4[:, tt, :]:
                E.tensor_tensor(w, w, self.gsc[:, 11:12].to_broadcast([128, NE]),
                                ALU.mult),
                wait=[t_r], sig=True)
            if "dbg_g2" in self.debug_outputs:
                p.dma("dq_s", ds["dbg_g2"][tt, 3], ew4[:, tt, :], wait=[prev_gate])
        self.tok_gate = prev_gate
        self.ewT = self.fF(16384, 2048, p=NE)
        self.b2sb = self.fF(18432, D, p=NE)
        ew_toks = []
        for tt in range(NTT):
            bi, pb = self.bank()
            ttok = self.transpose(pb[0:NE, 0:128], ew4[:, tt, :], False,
                                  wait=[self.tok_gate, self._ln_tok])
            et = p.do("dve", lambda E, o=self.ewT[:, tt * TT:(tt + 1) * TT],
                      i=pb[0:NE, 0:128]: E.tensor_copy(out=o, in_=i),
                      wait=[ttok], sig=True)
            self.bank_done(bi, et)
            ew_toks.append(et)
        self.tok_ewT = ew_toks[-1:]
        if "dbg_gate" in self.debug_outputs:
            p.dma("dq_s", ds["dbg_gate"][0], self.glog[:], wait=[self.tok_gate])
            p.dma("dq_s", ds["dbg_gate"][1], self.ew[:], wait=[self.tok_gate])

    def presplit_moe_weights(self):
        ds = self.ds
        tok = self.wpan_tok[1] if hasattr(self, "wpan_tok") else None
        for e in range(NE):
            for rc in range(NC):
                for chalf in range(2):
                    src = self.din["W1"][e][rc * 128:(rc + 1) * 128,
                                            chalf * 2048:(chalf + 1) * 2048]
                    dst_h = ds["W1hi"][e][rc * 128:(rc + 1) * 128,
                                          chalf * 2048:(chalf + 1) * 2048]
                    dst_l = ds["W1lo"][e][rc * 128:(rc + 1) * 128,
                                          chalf * 2048:(chalf + 1) * 2048]
                    tok = self._presplit_panel(src, dst_h, dst_l, tok)
        for e in range(NE):
            for rc in range(DFF // 128):
                src = self.din["W2"][e][rc * 128:(rc + 1) * 128, :]
                dst_h = ds["W2hi"][e][rc * 128:(rc + 1) * 128, :]
                dst_l = ds["W2lo"][e][rc * 128:(rc + 1) * 128, :]
                tok = self._presplit_panel(src, dst_h, dst_l, tok)
        self.tok_presplit = tok
        self._wpf_tok = tok

    def _presplit_panel(self, src, dst_h, dst_l, tok):
        p = self.p
        w = src.shape[-1]
        pf = self.fF(16384, w)
        ph = self.fB(36864, w)
        pl = self.fB(38912, w)
        w0 = self.wpan_tok if hasattr(self, "wpan_tok") else [None, None]
        t_f = p.dma("dq_g", pf, src, wait=[tok, w0[0], w0[1]])
        t_h = p.dma("dq_g", ph, src)
        t_l = p.do("gp", lambda E: E.tensor_tensor(pl, pf, ph, ALU.subtract),
                   wait=[t_f, t_h], sig=True)
        p.dma("dq_g", dst_h, ph, wait=[t_l])
        t2 = p.dma("dq_g", dst_l, pl, wait=[t_l])
        return t2

    def p5_moe(self):
        p = self.p
        ds = self.ds
        acc = self.fF(0, NTT * D)
        acc4 = acc.rearrange("p (t d) -> p t d", t=NTT)
        tb2 = p.dma("dq_s", self.b2sb, self.din["b2"],
                    wait=[getattr(self, "_s_tok", None)])
        init_toks = []
        for tt in range(NTT):
            for db in range(2):
                bi, pb = self.bank()
                mtok = self.mm3(pb[:], [],
                                extra_f32=[(self.ewT[:, tt * TT:(tt + 1) * TT],
                                            self.b2sb[:, db * 512:(db + 1) * 512])],
                                wait=[tb2] + self.tok_ewT)
                et = p.do("dve", lambda E, o=acc4[:, tt, db * 512:(db + 1) * 512],
                          i=pb[:]: E.tensor_copy(out=o, in_=i),
                          wait=[mtok, self._ln_tok, getattr(self, "_s_tok", None)],
                          sig=True)
                self.bank_done(bi, et)
                init_toks.append(et)
        acc_tok = init_toks[-1]

        W1 = [
            (self.fB(0, NC * 512).rearrange("p (c w) -> p c w", c=NC),
             self.fB(4096, NC * 512).rearrange("p (c w) -> p c w", c=NC)),
            (self.arB2[:, 0:NC * 512].rearrange("p (c w) -> p c w", c=NC),
             self.arB2[:, 4096:4096 + NC * 512].rearrange("p (c w) -> p c w", c=NC)),
        ]
        W2 = [
            (self.fB(8192, 4 * 1024).rearrange("p (c w) -> p c w", c=4),
             self.fB(12288, 4 * 1024).rearrange("p (c w) -> p c w", c=4)),
            (self.fB(16384, 4 * 1024).rearrange("p (c w) -> p c w", c=4),
             self.fB(20480, 4 * 1024).rearrange("p (c w) -> p c w", c=4)),
        ]
        hid_h = [self.fB(24576, 4 * 512).rearrange("p (c w) -> p c w", c=4),
                 self.fB(28672, 4 * 512).rearrange("p (c w) -> p c w", c=4)]
        hid_l = [self.fB(26624, 4 * 512).rearrange("p (c w) -> p c w", c=4),
                 self.fB(30720, 4 * 512).rearrange("p (c w) -> p c w", c=4)]
        h2h = self.fB(32768, NC * 512).rearrange("p (c w) -> p c w", c=NC)
        h2l = self.fB(36864, NC * 512).rearrange("p (c w) -> p c w", c=NC)
        b1s = self.bias["b1"].rearrange("p (e c) -> p e c", e=NE)

        w1_tok = [self._ah_state["tok_qk"], self._ah_state["tok_qk"]]
        w2_tok = [w1_tok[0], w1_tok[0]]
        h2_tok = self.tok_presplit
        hid_tok = [None, None]
        for e in range(NE):
            for pair in range(4):
                wt2 = []
                for half in range(2):
                    f8 = 2 * pair + half
                    t_a = p.dma("dq_s", W2[half][0],
                                ds["W2hi"][e].rearrange("(c p) d -> p c d", p=128)
                                [:, f8 * 4:(f8 + 1) * 4, :],
                                wait=[w2_tok[half], self.tok_presplit])
                    t_b = p.dma("dq_s", W2[half][1],
                                ds["W2lo"][e].rearrange("(c p) d -> p c d", p=128)
                                [:, f8 * 4:(f8 + 1) * 4, :])
                    wt2 += [t_a, t_b]
                wt1 = [[], []]
                for half in range(2):
                    f8 = 2 * pair + half
                    t_w1h = p.dma("dq_s", W1[half][0],
                                  ds["W1hi"][e].rearrange("(c p) f -> p c f", p=128)
                                  [:, :, f8 * 512:(f8 + 1) * 512],
                                  wait=[w1_tok[half], self.tok_presplit])
                    t_w1l = p.dma("dq_s", W1[half][1],
                                  ds["W1lo"][e].rearrange("(c p) f -> p c f", p=128)
                                  [:, :, f8 * 512:(f8 + 1) * 512])
                    wt1[half] = [t_w1h, t_w1l]
                for tb in range(NTB):
                    t_h2a = p.dma("dq_s", h2h,
                                  ds["h2hi"].rearrange("(c p) s -> p c s", p=128)
                                  [:, :, tb * TB:(tb + 1) * TB], wait=[h2_tok])
                    t_h2b = p.dma("dq_s", h2l,
                                  ds["h2lo"].rearrange("(c p) s -> p c s", p=128)
                                  [:, :, tb * TB:(tb + 1) * TB])
                    ht = [t_h2a, t_h2b]
                    for half in range(2):
                        f8 = 2 * pair + half
                        for ft in range(4):
                            bi, pb = self.bank()
                            pairs = []
                            for dc in range(NC):
                                pairs.append((
                                    W1[half][0][:, dc, ft * 128:(ft + 1) * 128],
                                    W1[half][1][:, dc, ft * 128:(ft + 1) * 128],
                                    h2h[:, dc, :], h2l[:, dc, :]))
                            mtok = self.mm3(pb[:], pairs, wait=wt1[half] + ht)
                            h2_tok = mtok
                            w1_tok[half] = mtok
                            bap = b1s[:, e, f8 * 4 + ft: f8 * 4 + ft + 1]
                            fsl = self.prjf[:, (ft % 2) * TB:(ft % 2 + 1) * TB]
                            et = self.evac(pb[:], fsl, func=AF.Relu, bias=bap,
                                           wait=[mtok, hid_tok[half]] + self.tok_const)
                            self.bank_done(bi, et)
                            stok = self.split(fsl, hid_h[half][:, ft, :],
                                              hid_l[half][:, ft, :], wait=[et])
                            hid_tok[half] = stok
                    for ttj in range(4):
                        for db in range(2):
                            bi, pb = self.bank()
                            pairs = []
                            for half in range(2):
                                for fc in range(4):
                                    pairs.append((
                                        hid_h[half][:, fc, ttj * 128:(ttj + 1) * 128],
                                        hid_l[half][:, fc, ttj * 128:(ttj + 1) * 128],
                                        W2[half][0][:, fc, db * 512:(db + 1) * 512],
                                        W2[half][1][:, fc, db * 512:(db + 1) * 512]))
                            mt2 = self.mm3(pb[:], pairs,
                                           wait=[hid_tok[0], hid_tok[1]] + wt2)
                            w2_tok = [mt2, mt2]
                            hid_tok = [mt2, mt2]
                            ewc = self.ew.rearrange("p (t e) -> p t e", t=NTT) \
                                [:, tb * 4 + ttj, e:e + 1]
                            at = p.do("dve", lambda E,
                                      o=acc4[:, tb * 4 + ttj, db * 512:(db + 1) * 512],
                                      i=pb[:], s=ewc:
                                      E.scalar_tensor_tensor(
                                          o, i, s, o, ALU.mult, ALU.add),
                                      wait=[mt2, acc_tok, self.tok_gate], sig=True)
                            self.bank_done(bi, at)
                            acc_tok = at
        self.tok_moe = acc_tok
        if "dbg_acc" in self.debug_outputs:
            p.dma("dq_s", ds["dbg_acc"].rearrange("(t p) d -> p t d", p=128),
                  acc4, wait=[acc_tok])

    def p6_final(self):
        p = self.p
        ds = self.ds
        acc4 = self.fF(0, NTT * D).rearrange("p (t d) -> p t d", t=NTT)
        y = self.fF(16384, D)
        sqv = self.fF(17408, D)
        rel = self.fF(18432, D)
        st_tok = None
        out_toks = []
        for tt in range(NTT):
            rt = p.dma("dq_s", rel, ds["s_tok"][tt * TT:(tt + 1) * TT, :],
                       wait=[st_tok, self.tok_moe])
            t_y = p.do("dve", lambda E, o=y, a=acc4[:, tt, :], b=rel:
                       E.tensor_add(o, a, b), wait=[rt, self.tok_moe], sig=True)
            t_m = p.do("dve", lambda E: E.tensor_reduce(
                self.gsc[:, 0:1], y, AXX, ALU.add), wait=[t_y], sig=True)
            t_nm = p.do("dve", lambda E: E.tensor_scalar_mul(
                self.gsc[:, 1:2], self.gsc[:, 0:1], -1.0 / 1024.0), wait=[t_m],
                sig=True)
            t_sq = p.do("act", lambda E, o=sqv, i=y:
                        E.activation(out=o, in_=i, func=AF.Square),
                        wait=[t_y, st_tok], sig=True)
            t_s2 = p.do("dve", lambda E: E.tensor_reduce(
                self.gsc[:, 2:3], sqv, AXX, ALU.add), wait=[t_sq], sig=True)
            t_v1 = p.do("dve", lambda E: E.tensor_tensor(
                self.gsc[:, 3:4], self.gsc[:, 1:2], self.gsc[:, 1:2], ALU.mult),
                wait=[t_nm], sig=True)
            t_v2 = p.do("dve", lambda E: E.scalar_tensor_tensor(
                self.gsc[:, 4:5], self.gsc[:, 2:3], 1.0 / 1024.0, self.gsc[:, 3:4],
                ALU.mult, ALU.subtract), wait=[t_s2, t_v1], sig=True)
            t_v3 = p.do("dve", lambda E: E.tensor_scalar_add(
                self.gsc[:, 4:5], self.gsc[:, 4:5], EPS), wait=[t_v2], sig=True)
            t_sqr = p.do("act", lambda E: E.activation(
                out=self.gsc[:, 5:6], in_=self.gsc[:, 4:5], func=AF.Sqrt),
                wait=[t_v3], sig=True)
            t_r = p.do("dve", lambda E: E.reciprocal(
                out=self.gsc[:, 6:7], in_=self.gsc[:, 5:6]), wait=[t_sqr], sig=True)
            t_o = p.do("dve", lambda E, o=rel, a=y:
                       E.tensor_scalar(o, a, self.gsc[:, 1:2], self.gsc[:, 6:7],
                                       ALU.add, ALU.mult),
                       wait=[t_r], sig=True)
            st_tok = p.dma("dq_s", self.dout[tt * TT:(tt + 1) * TT, :], rel,
                           wait=[t_o])
            out_toks.append(st_tok)
        p._wait("sync", out_toks)

    def build(self):
        self.setup()
        self.p0_input()
        if self.stop_after == "p0":
            self.p.finalize()
            return self.nc
        self.p1a()
        if self.stop_after == "p1a":
            self.p.finalize()
            return self.nc
        self.presplit_moe_weights()
        self.attn_heads(True)
        if self.stop_after == "p1b":
            self.p.finalize()
            return self.nc
        self.proj_residual_ln("attn1")
        if self.stop_after == "p1c":
            self.p.finalize()
            return self.nc
        self.p2a()
        if self.stop_after == "p2a":
            self.p.finalize()
            return self.nc
        self.attn_heads(False)
        if self.stop_after == "p2b":
            self.p.finalize()
            return self.nc
        self.load_wo()
        self.proj_residual_ln("attn2")
        if self.stop_after == "p2c":
            self.p.finalize()
            return self.nc
        self.p3_gating()
        if self.stop_after == "p3":
            self.p.finalize()
            return self.nc
        self.p5_moe()
        if self.stop_after == "p5":
            self.p.finalize()
            return self.nc
        self.p6_final()
        import sys
        print(f"[kernel-build] sbuf_bytes_remaining={self.nc.sbuf_bytes_remaining}",
              file=sys.stderr)
        self.p.finalize()
        return self.nc


_CACHED = {}


def _get_nc(stop_after=None):
    key = stop_after
    if key not in _CACHED:
        _CACHED[key] = Builder(stop_after=stop_after).build()
    return _CACHED[key]


def kernel(**inputs):
    nc = _get_nc()
    x = np.ascontiguousarray(np.asarray(inputs["L0_attn_x"], dtype=np.float32))
    wmap = {
        "Wq": "L0_attn_Wq", "bq": "L0_attn_bq", "Wk": "L0_attn_Wk",
        "bk": "L0_attn_bk", "Wv": "L0_attn_Wv", "bv": "L0_attn_bv",
        "Wo": "L0_attn_Wo", "bo": "L0_attn_bo",
        "ln1_g": "L0_ffn_ln1_g", "ln1_b": "L0_ffn_ln1_b",
        "w_qkv": "L0_ffn_w_qkv", "b_qkv": "L0_ffn_b_qkv",
        "w_o": "L0_ffn_w_o", "b_o": "L0_ffn_b_o",
        "ln2_g": "L0_ffn_ln2_g", "ln2_b": "L0_ffn_ln2_b",
        "gate_w": "L0_ffn_gate_w", "gate_b": "L0_ffn_gate_b",
        "W1": "L0_ffn_W1", "b1": "L0_ffn_b1",
        "W2": "L0_ffn_W2", "b2": "L0_ffn_b2",
    }
    base = {}
    for k, srcn in wmap.items():
        a = np.ascontiguousarray(np.asarray(inputs[srcn], dtype=np.float32))
        if a.ndim == 1:
            a = a.reshape(1, -1)
        base[k] = a
    in_maps = []
    for b in range(8):
        m = {"x": np.ascontiguousarray(x[b])}
        m.update(base)
        in_maps.append(m)
    res = run_bass_kernel_spmd(nc, in_maps, core_ids=list(range(8)))
    return np.stack([res.results[b]["out"] for b in range(8)], axis=0)
